# version 03
# baseline (speedup 1.0000x reference)
import math
import os
import numpy as np
import ml_dtypes
import concourse.bass as bass
import concourse.mybir as mybir
from concourse.bass_core_rust import IndirectOffsetOnAxis
from concourse.bass_utils import run_bass_kernel_spmd

F32 = mybir.dt.float32
BF16 = mybir.dt.bfloat16
AF = mybir.ActivationFunctionType
ALU = mybir.AluOpType
AX = mybir.AxisListType

D = 1024
S = 4096
B = 4
NH = 16
HS = 64
FR = 256
NHC = 8
QC = 512
NQC = S // QC
NKT = S // 128
TOK2 = 2048
INNER = 4096

C_QKV = float((3.0 * D * HS * NH) ** (-0.25))
C_SQ2F = float(math.sqrt(2.0) / 16.0)
C_H = float(HS ** (-0.5))
C_OUT = float((D * D) ** (-0.25))
MAGIC = float(1.5 * 2 ** 23)
LN_EPS = 1e-5

_OMEGA = (10000.0 ** (-(np.arange(FR) // 2 * 2) / FR)).astype(np.float64)[::2]


def _g_of_lane(eps):
    l = np.arange(128)
    return l if eps == 1 else (l + 64) % 128


def _host_tables():
    t = {}
    k = np.arange(S, dtype=np.float64)
    for eps in (0, 1):
        g = _g_of_lane(eps)
        ang = k[None, :] * _OMEGA[g][:, None]
        sin, cos = np.sin(ang), np.cos(ang)
        in_k = ((np.arange(128) // 64) == eps)[:, None]
        t[f"sinT{eps}"] = sin.astype(ml_dtypes.bfloat16)
        t[f"cosT{eps}"] = cos.astype(ml_dtypes.bfloat16)
        t[f"khatB{eps}"] = np.where(in_k, cos, sin).astype(ml_dtypes.bfloat16)
        t[f"ceA{eps}"] = np.where(~in_k, cos, 0.0).astype(ml_dtypes.bfloat16)
    l = np.arange(128)
    ang2 = k[None, :] * _OMEGA[(l % 64) + 64][:, None]
    t["khatC2"] = np.sin(ang2).astype(ml_dtypes.bfloat16)
    m = np.zeros((4, 128, QC), np.float32)
    kk = np.arange(128)[:, None]
    qq = np.arange(QC)[None, :]
    for j in range(4):
        m[j] = (128 * j + kk <= qq)
    t["masks"] = np.ascontiguousarray(m.transpose(1, 0, 2)).astype(ml_dtypes.bfloat16)
    return t


def _stage_inputs(inp):
    x = np.ascontiguousarray(np.asarray(inp["L0_attn_input"], dtype=np.float32))
    qkv = np.asarray(inp["L0_attn_qkv"], dtype=np.float32)
    qb = np.asarray(inp["L0_attn_q_bias"], dtype=np.float32)
    pos = np.asarray(inp["L0_attn_positional"], dtype=np.float32)
    wout = np.asarray(inp["L0_attn_out_kernel"], dtype=np.float32)
    ob = np.asarray(inp["L0_attn_out_bias"], dtype=np.float32)

    Wqkvb = np.asarray(inp["L1_attn_W_qkvb"], dtype=np.float32)
    eperm = np.zeros(3088, np.int64)
    for n in range(16):
        j = np.arange(193)
        src_rows = n * 193 + j
        dst = np.where(j < 64, n * 64 + j,
              np.where(j < 128, 1024 + n * 64 + (j - 64),
              np.where(j < 192, 2048 + n * 64 + (j - 128), 3072 + n)))
        eperm[dst] = src_rows
    wqkvbT = np.ascontiguousarray(Wqkvb[eperm, :].T)
    woT = np.ascontiguousarray(np.asarray(inp["L1_attn_W_o"], dtype=np.float32).T)
    lng = np.asarray(inp["L1_attn_ln_g"], dtype=np.float32)
    lnb = np.asarray(inp["L1_attn_ln_b"], dtype=np.float32)
    lngb = np.stack([lng.reshape(8, 128).T, lnb.reshape(8, 128).T], axis=2)
    wgT0 = np.ascontiguousarray(np.asarray(inp["L0_ffn_w_gate"], dtype=np.float32).T)
    wuT0 = np.ascontiguousarray(np.asarray(inp["L0_ffn_w_up"], dtype=np.float32).T)
    wdT0 = np.ascontiguousarray(np.asarray(inp["L0_ffn_w_down"], dtype=np.float32).T)
    wgT1 = np.ascontiguousarray(np.asarray(inp["L1_ffn_w_gate"], dtype=np.float32).T)
    wuT1 = np.ascontiguousarray(np.asarray(inp["L1_ffn_w_up"], dtype=np.float32).T)
    wdT1 = np.ascontiguousarray(np.asarray(inp["L1_ffn_w_down"], dtype=np.float32).T)
    maps = []
    for c in range(8):
        b, hf = c // 2, c % 2
        m = {}
        m["x_b_fm"] = np.ascontiguousarray(x[b].T)
        u = c
        m["x_res_fm"] = np.ascontiguousarray(
            x[:, 512 * u:512 * (u + 1), :].transpose(2, 0, 1).reshape(D, TOK2))
        cols = []
        for p in range(4):
            cols += [qkv[:, 1, hf * 8 + 2 * p, :], qkv[:, 1, hf * 8 + 2 * p + 1, :]]
        for p in range(4):
            cols += [qkv[:, 0, hf * 8 + 2 * p, :], qkv[:, 0, hf * 8 + 2 * p + 1, :]]
        m["w_kq"] = np.ascontiguousarray(np.concatenate(cols, axis=1))
        m["w_v"] = np.ascontiguousarray(qkv[:, 2, hf * 8:hf * 8 + 8, :].reshape(D, 512))
        qbc = np.zeros((128, 4), np.float32)
        for p in range(4):
            qbc[0:64, p] = qb[hf * 8 + 2 * p]
            qbc[64:128, p] = qb[hf * 8 + 2 * p + 1]
        m["q_bias_col"] = qbc
        pa = np.zeros((128, NHC, 2, 128), np.float32)
        for h in range(NHC):
            eps = h & 1
            g = _g_of_lane(eps)
            n = hf * 8 + h
            for e in range(2):
                f = 2 * g + e
                pa[64 * eps:64 * eps + 64, h, e, :] = pos[f, n, :].T
        m["pos_arr"] = pa
        perm = np.zeros(D, np.int64)
        for ch in range(8):
            hf2, fc = ch // 4, ch % 4
            p = np.arange(128)
            n = hf2 * 8 + fc * 2 + p // 64
            perm[ch * 128 + p] = n * 64 + (p % 64)
        m["w_out"] = np.ascontiguousarray(wout[perm, :])
        m["out_bias_col"] = np.ascontiguousarray(ob.reshape(8, 128).T)
        m["wgT_0"] = wgT0
        m["wqkvbT"] = wqkvbT
        m["woT"] = woT
        m["lngb"] = np.ascontiguousarray(lngb)
        m["wuT_0"] = wuT0
        m["wdT_0"] = wdT0
        m["wgT_1"] = wgT1
        m["wuT_1"] = wuT1
        m["wdT_1"] = wdT1
        idx = np.zeros((128, 8, 4), np.int32)
        p = np.arange(128)
        for ch in range(8):
            hf2, fc = ch // 4, ch % 4
            for bb in range(4):
                r = 2 * bb + hf2
                idx[:, ch, bb] = (r * NQC * 65 * NHC + u * (65 * NHC)
                                  + (1 + p % 64) * NHC + (fc * 2 + p // 64))
        m["imp_idx"] = idx
        maps.append(m)
    return maps


class Ctr:
    def __init__(self, stack, nc, name):
        self.sem = stack.enter_context(nc.semaphore(name))
        self.n = 0

    def inc(self, instr, k=1):
        instr.then_inc(self.sem, k)
        self.n += k
        return instr

    def wait(self, eng, n=None):
        eng.wait_ge(self.sem, self.n if n is None else n)


def build_nc():
    nc = bass.Bass("TRN2", target_bir_lowering=False, detect_race_conditions=False)
    T = _host_tables()

    x_b_fm = nc.dram_tensor("x_b_fm", [D, S], F32, kind="ExternalInput").ap()
    x_res_fm = nc.dram_tensor("x_res_fm", [D, TOK2], F32, kind="ExternalInput").ap()
    w_kq = nc.dram_tensor("w_kq", [D, D], F32, kind="ExternalInput").ap()
    w_v = nc.dram_tensor("w_v", [D, 512], F32, kind="ExternalInput").ap()
    q_bias_col = nc.dram_tensor("q_bias_col", [128, 4], F32, kind="ExternalInput").ap()
    pos_arr = nc.dram_tensor("pos_arr", [128, NHC, 2, 128], F32, kind="ExternalInput").ap()
    w_out = nc.dram_tensor("w_out", [D, D], F32, kind="ExternalInput").ap()
    out_bias_col = nc.dram_tensor("out_bias_col", [128, 8], F32, kind="ExternalInput").ap()
    imp_idx = nc.dram_tensor("imp_idx", [128, 8, 4], mybir.dt.int32, kind="ExternalInput").ap()

    out_fin = nc.dram_tensor("out_fin", [D, TOK2], F32, kind="ExternalOutput").ap()
    _K2 = "ExternalOutput" if os.environ.get("KDBG2") == "1" else "Internal"
    s_x1 = nc.dram_tensor("s_x1", [D, TOK2], F32, kind=_K2).ap()
    wgT_0 = nc.dram_tensor("wgT_0", [D, INNER], F32, kind="ExternalInput").ap()
    wuT_0 = nc.dram_tensor("wuT_0", [D, INNER], F32, kind="ExternalInput").ap()
    wdT_0 = nc.dram_tensor("wdT_0", [INNER, D], F32, kind="ExternalInput").ap()
    wgT_1 = nc.dram_tensor("wgT_1", [D, INNER], F32, kind="ExternalInput").ap()
    wuT_1 = nc.dram_tensor("wuT_1", [D, INNER], F32, kind="ExternalInput").ap()
    wdT_1 = nc.dram_tensor("wdT_1", [INNER, D], F32, kind="ExternalInput").ap()
    wqd_hbm = nc.dram_tensor("wqd_hbm", [INNER, D], BF16, kind="Internal").ap()
    wqu_hbm_0 = nc.dram_tensor("wqu_hbm_0", [32, 8, 128, 128], BF16, kind="Internal").ap()
    s_x2 = nc.dram_tensor("s_x2", [D, TOK2], F32, kind=_K2).ap()
    s_x3 = nc.dram_tensor("s_x3", [D, TOK2], F32, kind=_K2).ap()
    s_h = nc.dram_tensor("s_h", [D, TOK2], F32, kind="Internal").ap()
    s_y = nc.dram_tensor("s_y", [D, TOK2], F32, kind="Internal").ap()
    wqkvbT_d = nc.dram_tensor("wqkvbT", [D, 3088], F32, kind="ExternalInput").ap()
    woT_d = nc.dram_tensor("woT", [D, D], F32, kind="ExternalInput").ap()
    lngb_d = nc.dram_tensor("lngb", [128, 8, 2], F32, kind="ExternalInput").ap()
    c_iden = nc.inline_tensor(np.eye(128, dtype=np.float32), name="c_iden").ap()
    DBG = os.environ.get("KDBG") == "1"
    if DBG:
        dbg_lnx = nc.dram_tensor("dbg_lnx", [128, 8, 512], BF16, kind="ExternalOutput").ap()
        dbg_qp = nc.dram_tensor("dbg_qp", [128, 4, S], BF16, kind="ExternalOutput").ap()
        dbg_khA = nc.dram_tensor("dbg_khA", [128, NHC, S], BF16, kind="ExternalOutput").ap()
        dbg_vh = nc.dram_tensor("dbg_vh", [128, NHC, NKT, 65], BF16, kind="ExternalOutput").ap()
        dbg_qhat = nc.dram_tensor("dbg_qhat", [128, NHC, 3, QC], BF16, kind="ExternalOutput").ap()
        dbg_oacc = nc.dram_tensor("dbg_oacc", [65, 2, QC], F32, kind="ExternalOutput").ap()
        dbg_oall = nc.dram_tensor("dbg_oall", [128, 8, TOK2], BF16, kind="ExternalOutput").ap()
        dbg_at = nc.dram_tensor("dbg_at", [128, 4, 2, QC], BF16, kind="ExternalOutput").ap()
        dbg_xq = nc.dram_tensor("dbg_xq", [128, 8, 512], BF16, kind="ExternalOutput").ap()
        dbg_gu = nc.dram_tensor("dbg_gu", [128, 32, 512], BF16, kind="ExternalOutput").ap()
        dbg_rows = nc.dram_tensor("dbg_rows", [1, 4, 512], F32, kind="ExternalOutput").ap()
        dbg_wqg = nc.dram_tensor("dbg_wqg", [128, 8, INNER], BF16, kind="ExternalOutput").ap()
        dbg_wut = nc.dram_tensor("dbg_wut", [128, 8, 128], BF16, kind="ExternalOutput").ap()
        dbg_sw = nc.dram_tensor("dbg_sw", [128, 6], F32, kind="ExternalOutput").ap()

    cin = nc.dram_tensor("cin", [NQC, 65, NHC, QC], BF16, kind="Internal").ap()
    cout = nc.dram_tensor("cout", [8, NQC, 65, NHC, QC], BF16, kind="Internal").ap()

    c_sin = [nc.inline_tensor(np.ascontiguousarray(T[f"sinT{e}"]), name=f"c_sin{e}").ap() for e in (0, 1)]
    c_cos = [nc.inline_tensor(np.ascontiguousarray(T[f"cosT{e}"]), name=f"c_cos{e}").ap() for e in (0, 1)]
    c_khB = [nc.inline_tensor(np.ascontiguousarray(T[f"khatB{e}"]), name=f"c_khB{e}").ap() for e in (0, 1)]
    c_khC2 = nc.inline_tensor(np.ascontiguousarray(T["khatC2"]), name="c_khC2").ap()
    c_ceA = [nc.inline_tensor(np.ascontiguousarray(T[f"ceA{e}"]), name=f"c_ceA{e}").ap() for e in (0, 1)]
    c_masks = nc.inline_tensor(np.ascontiguousarray(T["masks"]), name="c_masks").ap()
    c_ones_row = nc.inline_tensor(np.ones((1, 128), np.float32), name="c_ones_row").ap()
    c_ones_col = nc.inline_tensor(np.ones((128, 1), np.float32), name="c_ones_col").ap()

    from contextlib import ExitStack
    stk = ExitStack()
    with stk:
        def sb(name, shape, dt, st=stk):
            return st.enter_context(nc.sbuf_tensor(name, shape, dt))

        ones_row_s = sb("ones_row_s", [1, 128], F32)
        ones_col_s = sb("ones_col_s", [128, 1], F32)
        qbias_s = sb("qbias_s", [128, 4], F32)
        obias_s = sb("obias_s", [128, 8], F32)
        impi_s = sb("impi_s", [128, 8, 4], mybir.dt.int32)

        c_ld = Ctr(stk, nc, "c_ld")
        c_xt0 = Ctr(stk, nc, "c_xt0")
        c_xt1 = Ctr(stk, nc, "c_xt1")
        c_sc = Ctr(stk, nc, "c_sc")
        c_x0 = Ctr(stk, nc, "c_x0")
        c_gld = Ctr(stk, nc, "c_gld")
        c_mm = Ctr(stk, nc, "c_mm")
        c_ve = Ctr(stk, nc, "c_ve")
        c_se = Ctr(stk, nc, "c_se")
        c_s3 = Ctr(stk, nc, "c_s3")
        c_exp = Ctr(stk, nc, "c_exp")
        c_msk = Ctr(stk, nc, "c_msk")
        c_av = Ctr(stk, nc, "c_av")
        c_oev = Ctr(stk, nc, "c_oev")
        c_rot = Ctr(stk, nc, "c_rot")
        c_tmm = Ctr(stk, nc, "c_tmm")
        c_exd = Ctr(stk, nc, "c_exd")
        c_cc = Ctr(stk, nc, "c_cc")
        c_imp = Ctr(stk, nc, "c_imp")
        c_st = Ctr(stk, nc, "c_st")
        c_ws0 = Ctr(stk, nc, "c_ws0")
        c_ws1 = Ctr(stk, nc, "c_ws1")
        c_wu0 = Ctr(stk, nc, "c_wu0")
        c_wu1 = Ctr(stk, nc, "c_wu1")
        c_wd0 = Ctr(stk, nc, "c_wd0")
        c_wd1 = Ctr(stk, nc, "c_wd1")
        c_xs0 = Ctr(stk, nc, "c_xs0")
        c_xs1 = Ctr(stk, nc, "c_xs1")
        c_x1r = Ctr(stk, nc, "c_x1r")
        c_fd = Ctr(stk, nc, "c_fd")
        c_hs0 = Ctr(stk, nc, "c_hs0")
        c_hs1 = Ctr(stk, nc, "c_hs1")
        c_sq = Ctr(stk, nc, "c_sq")

        with nc.Block() as blk:
            atn = ExitStack()
            with atn:
                def sba(name, shape, dt):
                    return atn.enter_context(nc.sbuf_tensor(name, shape, dt))

                khatA = sba("khatA", [128, NHC, S], BF16)
                khB = sba("khB", [128, 2, S], BF16)
                khC = sba("khC", [128, S], BF16)
                vhat = sba("vhat", [128, NHC, NKT, 65], BF16)
                qp_all = sba("qp_all", [128, 4, S], BF16)
                posb = sba("posb", [128, NHC, 2, 128], BF16)
                masks_s = sba("masks_s", [128, 4, QC], BF16)

                @blk.sync
                def _(s):
                    c_ld.inc(s.dma_start(out=masks_s[:], in_=c_masks), 16)
                    c_ld.inc(s.dma_start(out=ones_row_s[:], in_=c_ones_row), 16)
                    c_ld.inc(s.dma_start(out=ones_col_s[:], in_=c_ones_col), 16)
                    c_ld.inc(s.dma_start(out=qbias_s[:], in_=q_bias_col), 16)
                    c_ld.inc(s.dma_start(out=obias_s[:], in_=out_bias_col), 16)
                    c_ld.inc(s.dma_start(out=impi_s[:], in_=imp_idx), 16)
                    for e in (0, 1):
                        c_ld.inc(s.dma_start(out=khB[:, e, :], in_=c_khB[e]), 16)
                    c_ld.inc(s.dma_start(out=khC[:], in_=c_khC2), 16)
                    for h in range(NHC):
                        e = h & 1
                        lo = 64 - 64 * e
                        c_ld.inc(s.dma_start(out=khatA[lo:lo + 64, h, :],
                                             in_=c_ceA[e][lo:lo + 64, :]), 16)
                n_const = c_ld.n

                @blk.gpsimd
                def _(g):
                    c_gld.inc(g.dma_start(out=posb[:], in_=pos_arr), 16)

                @blk.vector
                def _(v):
                    c_ve.inc(v.memset(vhat[:, :, :, 0:1], 1.0))
                    v.wait_ge(c_ld.sem, n_const)
                    c_ve.inc(v.tensor_scalar_mul(qbias_s[:], qbias_s[:], C_H))

                pha = ExitStack()
                with pha:
                    def sbp(name, shape, dt):
                        return pha.enter_context(nc.sbuf_tensor(name, shape, dt))

                    xt = sbp("xt", [128, 2, 512], F32)
                    xsq = sbp("xsq", [128, 512], F32)
                    lnx = sbp("lnx", [128, 8, 512], BF16)
                    rowa = sbp("rowa", [1, 512], F32)
                    rowb = sbp("rowb", [1, 512], F32)
                    rowc = sbp("rowc", [1, 512], F32)
                    wkq_s = sbp("wkq_s", [128, 8, D], BF16)
                    wv_s = sbp("wv_s", [128, 8, 512], BF16)

                    @blk.gpsimd
                    def _(g):
                        c_gld.inc(g.dma_start(out=wkq_s[:], in_=w_kq.rearrange("(c p) m -> p c m", p=128)), 16)
                        c_gld.inc(g.dma_start(out=wv_s[:], in_=w_v.rearrange("(c p) m -> p c m", p=128)), 16)

                    @blk.vector
                    def _(v):
                        c_gld.wait(v)
                        c_ve.inc(v.tensor_scalar_mul(posb[:], posb[:], C_SQ2F))

                    with (
                        nc.psum_tensor("pa_sum", [1, 2, 512], F32) as pa_sum,
                        nc.psum_tensor("pa_bc", [128, 2, 512], F32) as pa_bc,
                        nc.psum_tensor("pa_kq", [128, 512], F32) as pa_kq,
                        nc.psum_tensor("pa_v", [128, 512], F32) as pa_v,
                    ):
                        xv = x_b_fm.rearrange("(c p) t -> p c t", p=128)
                        for tg in range(8):
                            tsl = slice(tg * 512, (tg + 1) * 512)
                            for ch in range(8):
                                cx = c_xt0 if ch % 2 == 0 else c_xt1

                                @blk.sync
                                def _(s, ch=ch, tsl=tsl, cx=cx):
                                    c_mm.wait(s)
                                    c_ve.wait(s)
                                    cx.inc(s.dma_start(out=xt[:, ch % 2, :], in_=xv[:, ch, tsl]), 16)

                                @blk.vector
                                def _(v, ch=ch, cx=cx):
                                    cx.wait(v)
                                    if ch >= 1:
                                        c_mm.wait(v)
                                    c_ve.inc(v.tensor_mul(xsq[:], xt[:, ch % 2, :], xt[:, ch % 2, :]))

                                @blk.tensor
                                def _(t, ch=ch):
                                    c_ve.wait(t)
                                    t.matmul(pa_sum[:, 0, :], ones_col_s[:], xt[:, ch % 2, :],
                                             start=(ch == 0), stop=(ch == 7))
                                    c_mm.inc(t.matmul(pa_sum[:, 1, :], ones_col_s[:], xsq[:],
                                                      start=(ch == 0), stop=(ch == 7)))

                            @blk.vector
                            def _(v):
                                c_mm.wait(v)
                                c_ve.inc(v.tensor_scalar_mul(rowa[:], pa_sum[0:1, 0, :], 1.0 / D))
                                c_ve.inc(v.tensor_mul(rowc[:], rowa[:], rowa[:]))
                                c_ve.inc(v.scalar_tensor_tensor(rowb[:], pa_sum[0:1, 1, :], 1.0 / D,
                                                                rowc[:], ALU.mult, ALU.subtract))
                                c_ve.inc(v.tensor_scalar_add(rowb[:], rowb[:], LN_EPS))

                            @blk.scalar
                            def _(sc):
                                c_ve.wait(sc)
                                c_se.inc(sc.activation(rowb[:], rowb[:], AF.Sqrt))

                            @blk.vector
                            def _(v):
                                c_se.wait(v)
                                v.wait_ge(c_ve.sem, c_ve.n)
                                c_ve.inc(v.reciprocal(rowb[:], rowb[:]))
                                c_ve.inc(v.tensor_scalar_mul(rowb[:], rowb[:], C_QKV))
                                c_ve.inc(v.tensor_mul(rowc[:], rowa[:], rowb[:]))
                                c_ve.inc(v.tensor_scalar_mul(rowc[:], rowc[:], -1.0))

                            @blk.tensor
                            def _(t):
                                c_ve.wait(t)
                                t.matmul(pa_bc[:, 0, :], ones_row_s[:], rowb[:], start=True, stop=True)
                                c_mm.inc(t.matmul(pa_bc[:, 1, :], ones_row_s[:], rowc[:], start=True, stop=True))

                            for ch in range(8):
                                cx = c_xt0 if ch % 2 == 0 else c_xt1

                                @blk.sync
                                def _(s, ch=ch, tsl=tsl, cx=cx):
                                    c_ve.wait(s)
                                    cx.inc(s.dma_start(out=xt[:, ch % 2, :], in_=xv[:, ch, tsl]), 16)

                                @blk.vector
                                def _(v, ch=ch, cx=cx):
                                    cx.wait(v)
                                    if ch == 0:
                                        c_mm.wait(v)
                                    c_ve.inc(v.tensor_mul(xt[:, ch % 2, :], xt[:, ch % 2, :], pa_bc[:, 0, :]))
                                    c_ve.inc(v.tensor_add(lnx[:, ch, :], xt[:, ch % 2, :], pa_bc[:, 1, :]))
                            lnx_done = c_ve.n
                            if DBG and tg == 1:
                                @blk.sync
                                def _(s):
                                    s.wait_ge(c_ve.sem, lnx_done)
                                    c_st.inc(s.dma_start(out=dbg_lnx, in_=lnx[:]), 16)

                            for mck in range(8):
                                @blk.tensor
                                def _(t, mck=mck, lnx_done=lnx_done):
                                    t.wait_ge(c_ve.sem, lnx_done)
                                    c_ve.wait(t)
                                    c_se.wait(t)
                                    if mck == 0:
                                        c_gld.wait(t)
                                    for dch in range(8):
                                        last = t.matmul(pa_kq[:], wkq_s[:, dch, mck * 128:(mck + 1) * 128],
                                                        lnx[:, dch, :], start=(dch == 0), stop=(dch == 7))
                                    c_mm.inc(last)

                                if mck < 4:
                                    @blk.vector
                                    def _(v, mck=mck, tsl=tsl):
                                        c_mm.wait(v)
                                        c_ve.inc(v.tensor_copy(khatA[0:64, 2 * mck, tsl], pa_kq[0:64, :]))
                                        c_ve.inc(v.tensor_copy(khatA[64:128, 2 * mck + 1, tsl], pa_kq[64:128, :]))
                                else:
                                    @blk.scalar
                                    def _(sc, mck=mck, tsl=tsl):
                                        c_mm.wait(sc)
                                        c_se.inc(sc.activation(qp_all[:, mck - 4, tsl], pa_kq[:], AF.Identity,
                                                               bias=qbias_s[:, mck - 4:mck - 3], scale=C_H))

                            for tt in range(4):
                                @blk.tensor
                                def _(t, tt=tt):
                                    c_ve.wait(t)
                                    c_se.wait(t)
                                    for dch in range(8):
                                        last = t.matmul(pa_v[:], lnx[:, dch, tt * 128:(tt + 1) * 128],
                                                        wv_s[:, dch, :], start=(dch == 0), stop=(dch == 7))
                                    c_mm.inc(last)

                                @blk.vector
                                def _(v, tg=tg, tt=tt):
                                    c_mm.wait(v)
                                    kt = tg * 4 + tt
                                    for h in range(NHC):
                                        c_ve.inc(v.tensor_copy(vhat[:, h, kt, 1:65],
                                                               pa_v[:, h * 64:(h + 1) * 64]))

                phb = ExitStack()
                with phb:
                    def sbb(name, shape, dt):
                        return phb.enter_context(nc.sbuf_tensor(name, shape, dt))

                    qhat = sbb("qhat", [128, NHC, 3, QC], BF16)
                    oacc = sbb("oacc", [65, 2, QC], F32)
                    at_s = sbb("at_s", [128, 2, 2, QC], BF16)
                    exp_s = sbb("exp_s", [65, 2, QC], BF16)
                    uws = sbb("uws", [128, QC], F32)
                    uw2 = sbb("uw2", [128, QC], F32)
                    den = sbb("den", [1, QC], F32)
                    sc_qc = sbb("sc_qc", [128, 2, 2, QC], BF16)

                    @blk.vector
                    def _(v):
                        c_mm.wait(v)
                        c_se.wait(v)
                        c_ve.inc(v.memset(qhat[:], 0.0))

                    if DBG:
                        @blk.sync
                        def _(s):
                            c_mm.wait(s)
                            c_ve.wait(s)
                            c_se.wait(s)
                            c_st.inc(s.dma_start(out=dbg_qp, in_=qp_all[:]), 16)
                            c_st.inc(s.dma_start(out=dbg_khA, in_=khatA[:]), 16)
                            c_st.inc(s.dma_start(out=dbg_vh, in_=vhat[:]), 16)

                    with (
                        nc.psum_tensor("pb_T", [128, 2, QC], F32) as pb_T,
                        nc.psum_tensor("pb_S", [128, 2, 2, QC], F32) as pb_S,
                        nc.psum_tensor("pb_O", [65, 2, QC], F32) as pb_O,
                    ):
                        it = 0
                        nmask = 0
                        for qc in range(NQC):
                            qsl = slice(qc * QC, (qc + 1) * QC)
                            s3_before = c_s3.n

                            @blk.sync
                            def _(s, qsl=qsl, first=(qc == 0)):
                                if first:
                                    c_mm.wait(s)
                                    c_ve.wait(s)
                                    c_se.wait(s)
                                c_rot.wait(s)
                                for e in (0, 1):
                                    c_sc.inc(s.dma_start(out=sc_qc[:, e, 0, :], in_=c_sin[e][:, qsl]), 16)
                                    c_sc.inc(s.dma_start(out=sc_qc[:, e, 1, :], in_=c_cos[e][:, qsl]), 16)

                            for h in range(NHC):
                                e = h & 1

                                @blk.tensor
                                def _(t, qc=qc, h=h, e=e, qsl=qsl):
                                    if h == 0:
                                        c_ve.wait(t)
                                    if qc == 0 and h == 0:
                                        c_se.wait(t)
                                    c_rot.wait(t)
                                    t.matmul(pb_T[:, 0, :], posb[64 * e:64 * e + 64, h, 0, :],
                                             qp_all[64 * e:64 * e + 64, h // 2, qsl],
                                             start=True, stop=True)
                                    c_tmm.inc(t.matmul(pb_T[:, 1, :], posb[64 * e:64 * e + 64, h, 1, :],
                                                       qp_all[64 * e:64 * e + 64, h // 2, qsl],
                                                       start=True, stop=True))

                                @blk.vector
                                def _(v, h=h, e=e, s3_before=s3_before):
                                    c_tmm.wait(v)
                                    c_sc.wait(v)
                                    if h == 0:
                                        v.wait_ge(c_s3.sem, s3_before)
                                    ko = 64 * e
                                    no = 64 - ko
                                    c_ve.inc(v.tensor_mul(uws[:], pb_T[:, 0, :], sc_qc[:, e, 0, :]))
                                    c_ve.inc(v.tensor_mul(uw2[:], pb_T[:, 1, :], sc_qc[:, e, 1, :]))
                                    c_ve.inc(v.tensor_add(uws[:], uws[:], uw2[:]))
                                    c_ve.inc(v.tensor_copy(qhat[ko:ko + 64, h, 1, :], uws[ko:ko + 64, :]))
                                    c_ve.inc(v.tensor_copy(qhat[no:no + 64, h, 0, :], uws[no:no + 64, :]))
                                    c_ve.inc(v.tensor_mul(uws[:], pb_T[:, 1, :], sc_qc[:, e, 0, :]))
                                    c_ve.inc(v.tensor_mul(uw2[:], pb_T[:, 0, :], sc_qc[:, e, 1, :]))
                                    c_ve.inc(v.tensor_sub(uws[:], uws[:], uw2[:]))
                                    c_ve.inc(v.tensor_copy(qhat[ko:ko + 64, h, 2, :], uws[ko:ko + 64, :]))
                                    c_rot.inc(v.tensor_copy(qhat[no:no + 64, h, 1, :], uws[no:no + 64, :]))

                            @blk.vector
                            def _(v, qsl=qsl):
                                c_ve.inc(v.tensor_copy(qhat[0:64, 0:NHC:2, 0, :], qp_all[0:64, :, qsl]))
                                c_rot.inc(v.tensor_copy(qhat[64:128, 1:NHC:2, 0, :], qp_all[64:128, :, qsl]))
                            rot_done = c_rot.n
                            if DBG and qc == 0:
                                @blk.sync
                                def _(s, rot_done=rot_done):
                                    s.wait_ge(c_rot.sem, rot_done)
                                    c_st.inc(s.dma_start(out=dbg_qhat, in_=qhat[:]), 16)

                            nkt = 4 * (qc + 1)
                            ktgs = [range(g0, min(g0 + 8, nkt)) for g0 in range(0, nkt, 8)]
                            for pr in range(4):
                                for gi, ktg in enumerate(ktgs):
                                    first = True
                                    for kt in ktg:
                                        buf = it % 2
                                        diag = (kt >= 4 * qc)

                                        @blk.tensor
                                        def _(t, pr=pr, kt=kt, buf=buf, it=it, rot_done=rot_done):
                                            t.wait_ge(c_rot.sem, rot_done)
                                            if it >= 2:
                                                t.wait_ge(c_exp.sem, it - 1)
                                            ksl = slice(kt * 128, (kt + 1) * 128)
                                            for hp in range(2):
                                                h = 2 * pr + hp
                                                e = h & 1
                                                t.matmul(pb_S[:, buf, hp, :], khatA[:, h, ksl],
                                                         qhat[:, h, 0, :], start=True, stop=False)
                                                t.matmul(pb_S[:, buf, hp, :], khB[:, e, ksl],
                                                         qhat[:, h, 1, :], start=False, stop=False)
                                                last = t.matmul(pb_S[:, buf, hp, :],
                                                                khC[64 * e:64 * e + 64, ksl],
                                                                qhat[64 * e:64 * e + 64, h, 2, :],
                                                                start=False, stop=True)
                                            c_s3.inc(last)

                                        @blk.scalar
                                        def _(sc, buf=buf, it=it):
                                            sc.wait_ge(c_s3.sem, it + 1)
                                            if it >= 2:
                                                sc.wait_ge(c_av.sem, it - 1)
                                            c_exp.inc(sc.activation(
                                                at_s[:, buf, :, :].rearrange("p a q -> p (a q)"),
                                                pb_S[:, buf, :, :].rearrange("p a q -> p (a q)"),
                                                AF.Exp))

                                        if diag:
                                            j = kt - 4 * qc
                                            nmask += 1

                                            @blk.vector
                                            def _(v, buf=buf, it=it, j=j):
                                                v.wait_ge(c_exp.sem, it + 1)
                                                c_msk.inc(v.tensor_mul(
                                                    at_s[:, buf, :, :], at_s[:, buf, :, :],
                                                    masks_s[:, j:j + 1, :].broadcast_to([128, 2, QC])))

                                        if DBG and qc == 0 and pr == 0:
                                            @blk.sync
                                            def _(s, kt=kt, buf=buf, nmask=nmask):
                                                s.wait_ge(c_msk.sem, nmask)
                                                c_st.inc(s.dma_start(out=dbg_at[:, kt, :, :],
                                                                     in_=at_s[:, buf, :, :]), 16)

                                        @blk.tensor
                                        def _(t, pr=pr, kt=kt, buf=buf, it=it, diag=diag,
                                              nmask=nmask, first=first, last_kt=(kt == ktg[-1])):
                                            if diag:
                                                t.wait_ge(c_msk.sem, nmask)
                                            else:
                                                t.wait_ge(c_exp.sem, it + 1)
                                            if first:
                                                t.wait_ge(c_oev.sem, c_oev.n)
                                            for hp in range(2):
                                                h = 2 * pr + hp
                                                last = t.matmul(pb_O[:, hp, :], vhat[:, h, kt, :],
                                                                at_s[:, buf, hp, :],
                                                                start=first, stop=last_kt)
                                            c_av.inc(last)
                                        first = False
                                        it += 1

                                    @blk.vector
                                    def _(v, it=it, first_group=(gi == 0)):
                                        v.wait_ge(c_av.sem, it)
                                        if first_group:
                                            c_oev.inc(v.tensor_copy(oacc[:], pb_O[:]))
                                        else:
                                            c_oev.inc(v.tensor_add(oacc[:], oacc[:], pb_O[:]))

                                if DBG and qc == 0 and pr == 0:
                                    @blk.sync
                                    def _(s):
                                        c_oev.wait(s)
                                        c_st.inc(s.dma_start(out=dbg_oacc, in_=oacc[:]), 16)

                                for hp in range(2):
                                    @blk.vector
                                    def _(v, hp=hp):
                                        c_oev.wait(v)
                                        c_mm.wait(v)
                                        c_exd.wait(v)
                                        v.wait_ge(c_ve.sem, c_ve.n)
                                        c_ve.inc(v.reciprocal(den[:], oacc[0:1, hp, :]))

                                    @blk.tensor
                                    def _(t, hp=hp):
                                        c_ve.wait(t)
                                        c_mm.inc(t.matmul(pb_T[:, hp, :], ones_row_s[:], den[:],
                                                          start=True, stop=True))

                                    @blk.vector
                                    def _(v, hp=hp):
                                        c_mm.wait(v)
                                        c_ve.inc(v.tensor_mul(exp_s[:, hp, :], oacc[:, hp, :],
                                                              pb_T[0:65, hp, :]))

                                @blk.sync
                                def _(s, qc=qc, pr=pr):
                                    c_ve.wait(s)
                                    c_exd.inc(s.dma_start(out=cin[qc, :, 2 * pr:2 * pr + 2, :],
                                                          in_=exp_s[:]), 16)

                        @blk.gpsimd
                        def _(g):
                            g.wait_ge(c_exd.sem, c_exd.n)
                            c_cc.inc(g.collective_compute(
                                "AllGather", ALU.bypass,
                                replica_groups=[[0, 1, 2, 3, 4, 5, 6, 7]],
                                ins=[cin], outs=[cout]))

            phd = ExitStack()
            with phd:
                def sbd(name, shape, dt):
                    return phd.enter_context(nc.sbuf_tensor(name, shape, dt))

                wout_s = sbd("wout_s", [128, 8, D], BF16)
                oall = sbd("oall", [128, 8, TOK2], BF16)
                x0t = sbd("x0t", [128, 4, 512], F32)
                x1t = sbd("x1t", [128, 4, 512], F32)

                @blk.gpsimd
                def _(g):
                    g.wait_ge(c_cc.sem, 1)
                    c_gld.inc(g.dma_start(out=wout_s[:], in_=w_out.rearrange("(c p) m -> p c m", p=128)), 16)
                    src_rows = cout.rearrange("r u f h q -> (r u f h) q")
                    for ch in range(8):
                        for b in range(4):
                            c_imp.inc(g.indirect_dma_start(
                                oall[:, ch, b * 512:(b + 1) * 512], None,
                                src_rows,
                                IndirectOffsetOnAxis(impi_s[:, ch, b:b + 1], 0)), 16)
                n_imp = c_imp.n
                if DBG:
                    @blk.sync
                    def _(s, n_imp=n_imp):
                        s.wait_ge(c_imp.sem, n_imp)
                        c_st.inc(s.dma_start(out=dbg_oall, in_=oall[:]), 16)

                with nc.psum_tensor("pd", [128, 4, 512], F32) as pd:
                    for mck in range(8):
                        @blk.sync
                        def _(s, mck=mck, first=(mck == 0)):
                            if first:
                                c_s3.wait(s)
                                c_av.wait(s)
                                c_exp.wait(s)
                                c_exd.wait(s)
                            c_ve.wait(s)
                            c_x0.inc(s.dma_start(
                                out=x0t[:],
                                in_=x_res_fm[mck * 128:(mck + 1) * 128, :]
                                    .rearrange("p (a t) -> p a t", t=512)), 16)

                        @blk.tensor
                        def _(t, mck=mck, n_imp=n_imp):
                            if mck == 0:
                                t.wait_ge(c_imp.sem, n_imp)
                                c_gld.wait(t)
                            else:
                                c_se.wait(t)
                            for tt in range(4):
                                for och in range(8):
                                    last = t.matmul(pd[:, tt, :],
                                                    wout_s[:, och, mck * 128:(mck + 1) * 128],
                                                    oall[:, och, tt * 512:(tt + 1) * 512],
                                                    start=(och == 0), stop=(och == 7))
                                c_mm.inc(last)

                        @blk.scalar
                        def _(sc, mck=mck):
                            c_mm.wait(sc)
                            c_se.inc(sc.activation(x1t[:].rearrange("p a t -> p (a t)"),
                                                   pd[:].rearrange("p a t -> p (a t)"), AF.Identity,
                                                   bias=obias_s[:, mck:mck + 1], scale=C_OUT))

                        @blk.vector
                        def _(v, mck=mck):
                            c_se.wait(v)
                            c_x0.wait(v)
                            c_ve.inc(v.tensor_add(x1t[:], x1t[:], x0t[:]))

                        @blk.sync
                        def _(s, mck=mck):
                            c_ve.wait(s)
                            c_st.inc(s.dma_start(
                                out=s_x1[mck * 128:(mck + 1) * 128, :]
                                    .rearrange("p (a t) -> p a t", t=512),
                                in_=x1t[:]), 16)


            def emit_ffn(s_in, s_out, wgT_d, wuT_d, wdT_d, wqu_scr, wqd_scr, tag):
                phe = ExitStack()
                with phe:
                    def sbe(name, shape, dt):
                        return phe.enter_context(nc.sbuf_tensor(name + tag, shape, dt))

                    wqg = sbe("wqg", [128, 8, INNER], BF16)
                    swcol = sbe("swcol", [128, 3], F32)
                    swinv = sbe("swinv", [128, 3], F32)
                    wq_done = []
                    with (
                        nc.sbuf_tensor("wtile" + tag, [128, 2, INNER], F32) as wtile,
                        nc.sbuf_tensor("wacc" + tag, [128, 1], F32) as wacc,
                        nc.sbuf_tensor("wrow" + tag, [128, 1], F32) as wrow,
                        nc.sbuf_tensor("wq_st" + tag, [128, INNER], BF16) as wq_st,
                        nc.psum_tensor("pw" + tag, [128, 512], F32) as pw,
                    ):
                        for mi, (wsrc, nch, fd) in enumerate(
                                [(wgT_d, 8, INNER), (wuT_d, 8, INNER), (wdT_d, 32, D)]):
                            wv2 = wsrc.rearrange("(c p) m -> p c m", p=128)
                            for i in range(nch):
                                cw = c_ws0 if i % 2 == 0 else c_ws1

                                @blk.sync
                                def _(s, i=i, cw=cw, fd=fd):
                                    c_ve.wait(s)
                                    cw.inc(s.dma_start(out=wtile[:, i % 2, 0:fd], in_=wv2[:, i, :]), 16)

                                @blk.vector
                                def _(v, i=i, cw=cw, fd=fd):
                                    cw.wait(v)
                                    c_ve.inc(v.reduce_sum(wrow[:], wtile[:, i % 2, 0:fd],
                                                          axis=AX.X, apply_absolute_value=True))
                                    v.wait_ge(c_ve.sem, c_ve.n)
                                    if i == 0:
                                        c_ve.inc(v.tensor_copy(wacc[:], wrow[:]))
                                    else:
                                        c_ve.inc(v.tensor_add(wacc[:], wacc[:], wrow[:]))

                            @blk.tensor
                            def _(t):
                                c_ve.wait(t)
                                c_mm.inc(t.matmul(pw[0:1, 0:1], ones_col_s[:], wacc[:],
                                                  start=True, stop=True))

                            @blk.vector
                            def _(v, mi=mi, nch=nch, fd=fd):
                                c_mm.wait(v)
                                c_ve.inc(v.tensor_scalar(wrow[0:1, :], pw[0:1, 0:1],
                                                         1.0 / (nch * fd * 128), 1e-5,
                                                         ALU.mult, ALU.max))
                                v.wait_ge(c_ve.sem, c_ve.n)
                                c_ve.inc(v.reciprocal(wacc[0:1, :], wrow[0:1, :]))

                            @blk.tensor
                            def _(t, mi=mi):
                                c_ve.wait(t)
                                t.matmul(pw[:, 1:2], ones_row_s[:], wacc[0:1, :], start=True, stop=True)
                                c_mm.inc(t.matmul(pw[:, 2:3], ones_row_s[:], wrow[0:1, :], start=True, stop=True))

                            @blk.vector
                            def _(v, mi=mi):
                                c_mm.wait(v)
                                c_ve.inc(v.tensor_copy(swcol[:, mi:mi + 1], pw[:, 1:2]))
                                c_ve.inc(v.tensor_copy(swinv[:, mi:mi + 1], pw[:, 2:3]))
                            sw_ready = c_ve.n

                            for i in range(nch):
                                cw = c_ws0 if i % 2 == 0 else c_ws1

                                @blk.sync
                                def _(s, i=i, cw=cw, fd=fd):
                                    c_ve.wait(s)
                                    c_sq.wait(s)
                                    cw.inc(s.dma_start(out=wtile[:, i % 2, 0:fd], in_=wv2[:, i, :]), 16)

                                @blk.vector
                                def _(v, mi=mi, i=i, cw=cw, fd=fd, sw_ready=sw_ready):
                                    cw.wait(v)
                                    v.wait_ge(c_ve.sem, sw_ready)
                                    c_sq.wait(v)
                                    if i == 0:
                                        v.wait_ge(c_ve.sem, c_ve.n)
                                    dst = wqg[:, i, :] if mi == 0 else wq_st[:, 0:fd]
                                    c_ve.inc(v.tensor_scalar(wtile[:, i % 2, 0:fd], wtile[:, i % 2, 0:fd],
                                                             swcol[:, mi:mi + 1], MAGIC, ALU.mult, ALU.add))
                                    c_ve.inc(v.tensor_scalar(wtile[:, i % 2, 0:fd], wtile[:, i % 2, 0:fd],
                                                             -MAGIC, 1.0, ALU.add, ALU.min))
                                    c_ve.inc(v.tensor_scalar(dst, wtile[:, i % 2, 0:fd],
                                                             -1.0, None, ALU.max))

                                if mi == 1:
                                    @blk.sync
                                    def _(s, i=i):
                                        c_ve.wait(s)
                                        c_sq.inc(s.dma_start(
                                            out=wqu_scr[:, i].rearrange("mc p c -> p mc c"),
                                            in_=wq_st[:].rearrange("p (mc c) -> p mc c", c=128)), 16)
                                elif mi == 2:
                                    @blk.sync
                                    def _(s, i=i):
                                        c_ve.wait(s)
                                        c_sq.inc(s.dma_start(
                                            out=wqd_scr.rearrange("(c p) m -> p c m", p=128)[:, i, :],
                                            in_=wq_st[:, 0:D]), 16)

                    with (
                        nc.sbuf_tensor("xln" + tag, [128, 8, 512], F32) as xln,
                        nc.sbuf_tensor("xqf" + tag, [128, 8, 512], BF16) as xqf,
                        nc.sbuf_tensor("guf" + tag, [128, 32, 512], BF16) as guf,
                        nc.sbuf_tensor("xst" + tag, [128, 2, 512], F32) as xst,
                        nc.sbuf_tensor("xsqb" + tag, [128, 512], F32) as xsqb,
                        nc.sbuf_tensor("gmax" + tag, [128, 512], F32) as gmax,
                        nc.sbuf_tensor("gmin" + tag, [128, 512], F32) as gmin,
                        nc.sbuf_tensor("foldw" + tag, [32, 12, 512], F32) as foldw,
                        nc.sbuf_tensor("rw1" + tag, [1, 512], F32) as rw1,
                        nc.sbuf_tensor("rw2" + tag, [1, 512], F32) as rw2,
                        nc.sbuf_tensor("rw3" + tag, [1, 512], F32) as rw3,
                        nc.sbuf_tensor("bct" + tag, [128, 4, 512], F32) as bct,
                        nc.sbuf_tensor("wut" + tag, [128, 2, 8, 128], BF16) as wut,
                        nc.sbuf_tensor("wdt" + tag, [128, 2, D], BF16) as wdt,
                        nc.sbuf_tensor("tmpa" + tag, [128, 512], F32) as tmpa,
                        nc.sbuf_tensor("tmpb" + tag, [128, 512], F32) as tmpb,
                        nc.sbuf_tensor("x1tl" + tag, [128, 512], F32) as x1tl,
                        nc.sbuf_tensor("x2tl" + tag, [128, 512], F32) as x2tl,
                        nc.psum_tensor("pe" + tag, [128, 8, 512], F32) as pe,
                    ):
                        xv2 = s_in.rearrange("(c p) t -> p c t", p=128)
                        ov2 = s_out.rearrange("(c p) t -> p c t", p=128)

                        def emit_fold(gate_fn=None):
                            @blk.gpsimd
                            def _(g):
                                c_ve.wait(g)
                                c_fd.inc(g.dma_start(out=foldw[0:32, 0:4, :], in_=gmax[:]), 16)
                            @blk.vector
                            def _(v):
                                c_fd.wait(v)
                                c_ve.inc(v.tensor_tensor(foldw[0:32, 0:2, :], foldw[0:32, 0:2, :],
                                                         foldw[0:32, 2:4, :], ALU.max))
                                c_ve.inc(v.tensor_tensor(foldw[0:32, 0:1, :], foldw[0:32, 0:1, :],
                                                         foldw[0:32, 1:2, :], ALU.max))
                            @blk.gpsimd
                            def _(g):
                                c_ve.wait(g)
                                c_fd.inc(g.dma_start(out=foldw[0:4, 4:12, :], in_=foldw[0:32, 0, :]), 16)
                            @blk.vector
                            def _(v):
                                c_fd.wait(v)
                                c_ve.inc(v.tensor_tensor(foldw[0:4, 4:8, :], foldw[0:4, 4:8, :],
                                                         foldw[0:4, 8:12, :], ALU.max))
                                c_ve.inc(v.tensor_tensor(foldw[0:4, 4:6, :], foldw[0:4, 4:6, :],
                                                         foldw[0:4, 6:8, :], ALU.max))
                                c_ve.inc(v.tensor_tensor(foldw[0:4, 4:5, :], foldw[0:4, 4:5, :],
                                                         foldw[0:4, 5:6, :], ALU.max))
                            @blk.gpsimd
                            def _(g):
                                c_ve.wait(g)
                                c_fd.inc(g.dma_start(out=foldw[0:1, 0:4, :], in_=foldw[0:4, 4, :]), 16)
                            @blk.vector
                            def _(v):
                                c_fd.wait(v)
                                c_ve.inc(v.tensor_tensor(foldw[0:1, 0:2, :], foldw[0:1, 0:2, :],
                                                         foldw[0:1, 2:4, :], ALU.max))
                                c_ve.inc(v.tensor_tensor(rw1[:], foldw[0:1, 0, :],
                                                         foldw[0:1, 1, :], ALU.max))

                        def emit_minmax_fold():
                            @blk.vector
                            def _(v):
                                c_ve.inc(v.tensor_scalar_mul(gmin[:], gmin[:], -1.0))
                                c_ve.inc(v.tensor_max(gmax[:], gmax[:], gmin[:]))
                            emit_fold()

                        def _do_quarter(qt):
                            tq = slice(qt * 512, (qt + 1) * 512)
                            for ch in range(8):
                                cx = c_xs0 if ch % 2 == 0 else c_xs1

                                @blk.sync
                                def _(s, ch=ch, cx=cx, tq=tq):
                                    c_mm.wait(s)
                                    c_ve.wait(s)
                                    cx.inc(s.dma_start(out=xst[:, ch % 2, :], in_=xv2[:, ch, tq]), 16)

                                @blk.vector
                                def _(v, ch=ch, cx=cx):
                                    cx.wait(v)
                                    if ch >= 1:
                                        c_mm.wait(v)
                                    c_ve.inc(v.tensor_mul(xsqb[:], xst[:, ch % 2, :], xst[:, ch % 2, :]))

                                @blk.tensor
                                def _(t, ch=ch):
                                    c_ve.wait(t)
                                    t.matmul(pe[0:1, 0, :], ones_col_s[:], xst[:, ch % 2, :],
                                             start=(ch == 0), stop=(ch == 7))
                                    c_mm.inc(t.matmul(pe[0:1, 1, :], ones_col_s[:], xsqb[:],
                                                      start=(ch == 0), stop=(ch == 7)))

                            @blk.vector
                            def _(v):
                                c_mm.wait(v)
                                c_ve.inc(v.tensor_scalar_mul(rw1[:], pe[0:1, 0, :], 1.0 / D))
                                c_ve.inc(v.tensor_mul(rw2[:], rw1[:], rw1[:]))
                                c_ve.inc(v.scalar_tensor_tensor(rw2[:], pe[0:1, 1, :], 1.0 / D,
                                                                rw2[:], ALU.mult, ALU.subtract))
                                c_ve.inc(v.tensor_scalar_add(rw2[:], rw2[:], LN_EPS))

                            @blk.scalar
                            def _(sc):
                                c_ve.wait(sc)
                                c_se.inc(sc.activation(rw2[:], rw2[:], AF.Sqrt))

                            @blk.vector
                            def _(v):
                                c_se.wait(v)
                                v.wait_ge(c_ve.sem, c_ve.n)
                                c_ve.inc(v.reciprocal(rw2[:], rw2[:]))
                                c_ve.inc(v.tensor_mul(rw3[:], rw1[:], rw2[:]))
                                c_ve.inc(v.tensor_scalar_mul(rw3[:], rw3[:], -1.0))

                            @blk.tensor
                            def _(t):
                                c_ve.wait(t)
                                t.matmul(pe[:, 2, :], ones_row_s[:], rw2[:], start=True, stop=True)
                                c_mm.inc(t.matmul(pe[:, 3, :], ones_row_s[:], rw3[:], start=True, stop=True))

                            for ch in range(8):
                                cx = c_xs0 if ch % 2 == 0 else c_xs1

                                @blk.sync
                                def _(s, ch=ch, cx=cx, tq=tq):
                                    c_ve.wait(s)
                                    cx.inc(s.dma_start(out=xst[:, ch % 2, :], in_=xv2[:, ch, tq]), 16)

                                @blk.vector
                                def _(v, ch=ch, cx=cx):
                                    cx.wait(v)
                                    if ch == 0:
                                        c_mm.wait(v)
                                    c_ve.inc(v.tensor_mul(xst[:, ch % 2, :], xst[:, ch % 2, :], pe[:, 2, :]))
                                    c_ve.inc(v.tensor_add(xln[:, ch, :], xst[:, ch % 2, :], pe[:, 3, :]))
                                    if ch == 0:
                                        c_ve.inc(v.tensor_copy(gmax[:], xln[:, ch, :]))
                                        c_ve.inc(v.tensor_copy(gmin[:], xln[:, ch, :]))
                                    else:
                                        c_ve.inc(v.tensor_max(gmax[:], gmax[:], xln[:, ch, :]))
                                        c_ve.inc(v.tensor_tensor(gmin[:], gmin[:], xln[:, ch, :], ALU.min))

                            emit_minmax_fold()

                            @blk.vector
                            def _(v):
                                c_ve.inc(v.tensor_scalar_max(rw1[:], rw1[:], 1e-5))
                                v.wait_ge(c_ve.sem, c_ve.n)
                                c_ve.inc(v.reciprocal(rw2[:], rw1[:]))
                                v.wait_ge(c_ve.sem, c_ve.n)
                                c_ve.inc(v.tensor_scalar_mul(rw2[:], rw2[:], 127.0))
                                c_ve.inc(v.tensor_scalar_mul(rw1[:], rw1[:], 1.0 / 127.0))

                            @blk.tensor
                            def _(t):
                                c_ve.wait(t)
                                t.matmul(pe[:, 2, :], ones_row_s[:], rw2[:], start=True, stop=True)
                                c_mm.inc(t.matmul(pe[:, 3, :], ones_row_s[:], rw1[:], start=True, stop=True))

                            @blk.vector
                            def _(v):
                                c_mm.wait(v)
                                c_ve.inc(v.tensor_copy(bct[:, 0, :], pe[:, 2, :]))
                                c_ve.inc(v.tensor_copy(bct[:, 1, :], pe[:, 3, :]))
                                for ch in range(8):
                                    c_ve.inc(v.tensor_mul(tmpa[:], xln[:, ch, :], bct[:, 0, :]))
                                    c_ve.inc(v.tensor_scalar(tmpa[:], tmpa[:], MAGIC, -MAGIC, ALU.add, ALU.add))
                                    c_ve.inc(v.tensor_copy(xqf[:, ch, :], tmpa[:]))
                            xq_ready = c_ve.n
                            if DBG and tag == "_f0" and qt == 0:
                                @blk.sync
                                def _(s):
                                    s.wait_ge(c_ve.sem, xq_ready)
                                    c_st.inc(s.dma_start(out=dbg_sw[:, 0:3], in_=swcol[:]), 16)
                                    c_st.inc(s.dma_start(out=dbg_sw[:, 3:6], in_=swinv[:]), 16)
                                    c_st.inc(s.dma_start(out=dbg_wqg, in_=wqg[:]), 16)
                                    c_st.inc(s.dma_start(out=dbg_xq, in_=xqf[:]), 16)
                                    c_st.inc(s.dma_start(out=dbg_rows[:, 0:1, :], in_=rw1[:]), 16)
                                    c_st.inc(s.dma_start(out=dbg_rows[:, 1:2, :], in_=rw2[:]), 16)

                            for m in range(32):
                                cwu = c_wu0 if m % 2 == 0 else c_wu1

                                @blk.sync
                                def _(s, m=m, cwu=cwu):
                                    if m < 2:
                                        c_sq.wait(s)
                                    c_mm.wait(s)
                                    cwu.inc(s.dma_start(out=wut[:, m % 2, :, :], in_=wqu_scr[m].rearrange("dc p c -> p dc c")), 16)

                                if DBG and tag == "_f0" and qt == 0 and m == 5:
                                    @blk.sync
                                    def _(s, cwu=cwu):
                                        cwu.wait(s)
                                        c_st.inc(s.dma_start(out=dbg_wut, in_=wut[:, 1, :, :]), 16)

                                @blk.tensor
                                def _(t, m=m, cwu=cwu, xq_ready=xq_ready):
                                    t.wait_ge(c_ve.sem, xq_ready)
                                    cwu.wait(t)
                                    if m > 0:
                                        c_ve.wait(t)
                                    for dc in range(8):
                                        t.matmul(pe[:, 5, :], wqg[:, dc, m * 128:(m + 1) * 128],
                                                 xqf[:, dc, :], start=(dc == 0), stop=(dc == 7))
                                    for dc in range(8):
                                        last = t.matmul(pe[:, 6, :], wut[:, m % 2, dc, :],
                                                        xqf[:, dc, :], start=(dc == 0), stop=(dc == 7))
                                    c_mm.inc(last)

                                @blk.vector
                                def _(v, m=m):
                                    c_mm.wait(v)
                                    c_ve.inc(v.tensor_mul(tmpa[:], pe[:, 5, :], bct[:, 1, :]))

                                @blk.scalar
                                def _(sc, m=m):
                                    c_ve.wait(sc)
                                    c_se.inc(sc.activation(tmpb[:], tmpa[:], AF.Silu,
                                                           scale=swinv[:, 0:1]))

                                @blk.vector
                                def _(v, m=m):
                                    c_se.wait(v)
                                    c_ve.inc(v.tensor_mul(tmpa[:], pe[:, 6, :], bct[:, 1, :]))
                                    c_ve.inc(v.scalar_tensor_tensor(tmpa[:], tmpa[:], swinv[:, 1:2],
                                                                    tmpb[:], ALU.mult, ALU.mult))
                                    c_ve.inc(v.tensor_copy(guf[:, m, :], tmpa[:]))
                                    if m == 0:
                                        c_ve.inc(v.tensor_copy(gmax[:], tmpa[:]))
                                        c_ve.inc(v.tensor_copy(gmin[:], tmpa[:]))
                                    else:
                                        c_ve.inc(v.tensor_max(gmax[:], gmax[:], tmpa[:]))
                                        c_ve.inc(v.tensor_tensor(gmin[:], gmin[:], tmpa[:], ALU.min))

                            emit_minmax_fold()

                            @blk.vector
                            def _(v):
                                c_ve.inc(v.tensor_scalar_max(rw1[:], rw1[:], 1e-5))
                                v.wait_ge(c_ve.sem, c_ve.n)
                                c_ve.inc(v.reciprocal(rw2[:], rw1[:]))
                                v.wait_ge(c_ve.sem, c_ve.n)
                                c_ve.inc(v.tensor_scalar_mul(rw2[:], rw2[:], 127.0))
                                c_ve.inc(v.tensor_scalar_mul(rw1[:], rw1[:], 1.0 / 127.0))

                            @blk.tensor
                            def _(t):
                                c_ve.wait(t)
                                t.matmul(pe[:, 2, :], ones_row_s[:], rw2[:], start=True, stop=True)
                                c_mm.inc(t.matmul(pe[:, 3, :], ones_row_s[:], rw1[:], start=True, stop=True))

                            @blk.vector
                            def _(v):
                                c_mm.wait(v)
                                c_ve.inc(v.tensor_copy(bct[:, 2, :], pe[:, 2, :]))
                                c_ve.inc(v.tensor_copy(bct[:, 3, :], pe[:, 3, :]))
                                for m in range(32):
                                    c_ve.inc(v.tensor_mul(tmpa[:], guf[:, m, :], bct[:, 2, :]))
                                    c_ve.inc(v.tensor_scalar(tmpa[:], tmpa[:], MAGIC, -MAGIC, ALU.add, ALU.add))
                                    c_ve.inc(v.tensor_copy(guf[:, m, :], tmpa[:]))
                            gu_ready = c_ve.n
                            if DBG and tag == "_f0" and qt == 0:
                                @blk.sync
                                def _(s):
                                    s.wait_ge(c_ve.sem, gu_ready)
                                    c_st.inc(s.dma_start(out=dbg_gu, in_=guf[:]), 16)

                            for ic in range(32):
                                cwd = c_wd0 if ic % 2 == 0 else c_wd1

                                @blk.sync
                                def _(s, ic=ic, cwd=cwd):
                                    if ic < 2:
                                        c_sq.wait(s)
                                        c_mm.wait(s)
                                    else:
                                        c_mm.wait(s)
                                    cwd.inc(s.dma_start(
                                        out=wdt[:, ic % 2, :],
                                        in_=wqd_scr.rearrange("(c p) m -> p c m", p=128)[:, ic, :]), 16)

                                @blk.tensor
                                def _(t, ic=ic, cwd=cwd, gu_ready=gu_ready):
                                    t.wait_ge(c_ve.sem, gu_ready)
                                    cwd.wait(t)
                                    if ic == 0:
                                        c_se.wait(t)
                                        c_ve.wait(t)
                                    for dm in range(8):
                                        last = t.matmul(pe[:, dm, :], wdt[:, ic % 2, dm * 128:(dm + 1) * 128],
                                                        guf[:, ic, :], start=(ic == 0), stop=(ic == 31))
                                    c_mm.inc(last)

                            for dm in range(8):
                                @blk.sync
                                def _(s, dm=dm, tq=tq):
                                    c_ve.wait(s)
                                    c_x1r.inc(s.dma_start(out=x1tl[:], in_=xv2[:, dm, tq]), 16)

                                @blk.scalar
                                def _(sc, dm=dm):
                                    c_mm.wait(sc)
                                    if dm > 0:
                                        c_ve.wait(sc)
                                    c_se.inc(sc.activation(tmpb[:], pe[:, dm, :], AF.Identity,
                                                           scale=swinv[:, 2:3]))

                                @blk.vector
                                def _(v, dm=dm):
                                    c_se.wait(v)
                                    c_x1r.wait(v)
                                    c_ve.inc(v.tensor_mul(tmpb[:], tmpb[:], bct[:, 3, :]))
                                    c_ve.inc(v.tensor_add(x2tl[:], tmpb[:], x1tl[:]))

                                @blk.sync
                                def _(s, dm=dm, tq=tq):
                                    c_ve.wait(s)
                                    c_st.inc(s.dma_start(out=ov2[:, dm, tq], in_=x2tl[:]), 16)

                        for qt in range(4):
                            _do_quarter(qt)

            emit_ffn(s_x1, s_x2, wgT_0, wuT_0, wdT_0, wqu_hbm_0, wqd_hbm, "_f0")


            phf = ExitStack()
            with phf:
                def sbf(name, shape, dt):
                    return phf.enter_context(nc.sbuf_tensor(name, shape, dt))

                hstat = sbf("hstat", [128, 2, 8, 128], BF16)
                qk4 = sbf("qk4", [128, 4, 3088], F32)
                loT = sbf("loT", [128, 8, TOK2], BF16)
                wqvT = sbf("wqvT", [128, 8, 3088], BF16)
                woT_s = sbf("woT_s", [128, 8, D], BF16)
                iden_s = sbf("iden_s", [128, 128], F32)
                sca = sbf("sca", [128, 1024], F32)
                scb = sbf("scb", [128, 1024], F32)
                u_s = sbf("u_s", [128, 4, 1024], F32)
                o_s = sbf("o_s", [128, 4, 1024], F32)
                gseg = sbf("gseg", [128, 16, 20], F32)
                xs2 = sbf("xs2", [128, 2, 512], F32)
                xq2 = sbf("xq2", [128, 512], F32)
                rf1 = sbf("rf1", [1, 512], F32)
                rf2 = sbf("rf2", [1, 512], F32)
                lngb_s = sbf("lngb_s", [128, 8, 2], F32)
                htile = sbf("htile", [128, 512], F32)
                ytile = sbf("ytile", [128, 512], F32)

                @blk.gpsimd
                def _(g):
                    c_st.wait(g)
                    c_gld.inc(g.dma_start(out=wqvT[:], in_=wqkvbT_d.rearrange("(c p) m -> p c m", p=128)), 16)
                    c_gld.inc(g.dma_start(out=woT_s[:], in_=woT_d.rearrange("(c p) m -> p c m", p=128)), 16)
                    c_gld.inc(g.dma_start(out=iden_s[:], in_=c_iden), 16)

                @blk.sync
                def _(s):
                    c_st.wait(s)
                    c_ld.inc(s.dma_start(out=lngb_s[:], in_=lngb_d), 16)

                @blk.vector
                def _(v):
                    c_gld.wait(v)
                    c_ve.inc(v.tensor_scalar_mul(woT_s[:], woT_s[:], float(HS ** -0.5)))

                with nc.psum_tensor("pf", [128, 8, 512], F32) as pf:
                    xv3 = s_x2.rearrange("(c p) t -> p c t", p=128)
                    hv = s_h.rearrange("(c p) t -> p c t", p=128)
                    for qt in range(4):
                        tq = slice(qt * 512, (qt + 1) * 512)
                        for ch in range(8):
                            cx = c_xs0 if ch % 2 == 0 else c_xs1

                            @blk.sync
                            def _(s, ch=ch, cx=cx, tq=tq):
                                c_mm.wait(s)
                                c_ve.wait(s)
                                cx.inc(s.dma_start(out=xs2[:, ch % 2, :], in_=xv3[:, ch, tq]), 16)

                            @blk.vector
                            def _(v, ch=ch, cx=cx):
                                cx.wait(v)
                                if ch >= 1:
                                    c_mm.wait(v)
                                c_ve.inc(v.tensor_mul(xq2[:], xs2[:, ch % 2, :], xs2[:, ch % 2, :]))

                            @blk.tensor
                            def _(t, ch=ch):
                                c_ve.wait(t)
                                t.matmul(pf[0:1, 0, :], ones_col_s[:], xs2[:, ch % 2, :],
                                         start=(ch == 0), stop=(ch == 7))
                                c_mm.inc(t.matmul(pf[0:1, 1, :], ones_col_s[:], xq2[:],
                                                  start=(ch == 0), stop=(ch == 7)))

                        @blk.vector
                        def _(v):
                            c_mm.wait(v)
                            c_ve.inc(v.tensor_scalar_mul(rf1[:], pf[0:1, 0, :], 1.0 / D))
                            c_ve.inc(v.tensor_mul(rf2[:], rf1[:], rf1[:]))
                            c_ve.inc(v.scalar_tensor_tensor(rf2[:], pf[0:1, 1, :], 1.0 / D,
                                                            rf2[:], ALU.mult, ALU.subtract))
                            c_ve.inc(v.tensor_scalar_add(rf2[:], rf2[:], LN_EPS))

                        @blk.scalar
                        def _(sc):
                            c_ve.wait(sc)
                            c_se.inc(sc.activation(rf2[:], rf2[:], AF.Sqrt))

                        @blk.vector
                        def _(v):
                            c_se.wait(v)
                            v.wait_ge(c_ve.sem, c_ve.n)
                            c_ve.inc(v.reciprocal(rf2[:], rf2[:]))
                            c_ve.inc(v.tensor_mul(rf1[:], rf1[:], rf2[:]))
                            c_ve.inc(v.tensor_scalar_mul(rf1[:], rf1[:], -1.0))

                        @blk.tensor
                        def _(t):
                            c_ve.wait(t)
                            t.matmul(pf[:, 2, :], ones_row_s[:], rf2[:], start=True, stop=True)
                            c_mm.inc(t.matmul(pf[:, 3, :], ones_row_s[:], rf1[:], start=True, stop=True))

                        for ch in range(8):
                            cx = c_xs0 if ch % 2 == 0 else c_xs1

                            @blk.sync
                            def _(s, ch=ch, cx=cx, tq=tq):
                                c_ve.wait(s)
                                cx.inc(s.dma_start(out=xs2[:, ch % 2, :], in_=xv3[:, ch, tq]), 16)

                            @blk.vector
                            def _(v, ch=ch, cx=cx, qt=qt, tq=tq):
                                cx.wait(v)
                                if ch == 0:
                                    c_mm.wait(v)
                                c_ve.inc(v.tensor_mul(xs2[:, ch % 2, :], xs2[:, ch % 2, :], pf[:, 2, :]))
                                c_ve.inc(v.tensor_add(htile[:], xs2[:, ch % 2, :], pf[:, 3, :]))

                            @blk.sync
                            def _(s, ch=ch, tq=tq):
                                c_ve.wait(s)
                                c_st.inc(s.dma_start(out=hv[:, ch, tq], in_=htile[:]), 16)

                with (
                    nc.psum_tensor("pq", [128, 7, 512], F32) as pq,
                    nc.psum_tensor("pt2", [128, 512], F32) as pt2,
                ):
                    hv4 = s_h.rearrange("(c p) t -> p c t", p=128)
                    for sub in range(4):
                        for b in range(B):
                            tt = b * 4 + sub
                            chs = c_hs0 if tt % 2 == 0 else c_hs1

                            @blk.gpsimd
                            def _(g, tt=tt, chs=chs):
                                c_mm.wait(g)
                                if tt < 2:
                                    c_st.wait(g)
                                chs.inc(g.dma_start(out=hstat[:, tt % 2, :, :],
                                                    in_=hv4[:, :, tt * 128:(tt + 1) * 128]), 16)

                            @blk.tensor
                            def _(t, tt=tt, b=b, sub=sub, chs=chs):
                                if sub == 0 and b == 0:
                                    c_gld.wait(t)
                                    c_ve.wait(t)
                                chs.wait(t)
                                c_ve.wait(t)
                                for nch in range(7):
                                    nsl = slice(nch * 512, min((nch + 1) * 512, 3088))
                                    w = nsl.stop - nsl.start
                                    for dch in range(8):
                                        last = t.matmul(pq[:, nch, 0:w], hstat[:, tt % 2, dch, :],
                                                        wqvT[:, dch, nsl], start=(dch == 0), stop=(dch == 7))
                                c_mm.inc(last)

                            @blk.vector
                            def _(v, b=b):
                                c_mm.wait(v)
                                c_ve.inc(v.tensor_copy(qk4[:, b, 0:3072],
                                                       pq[:, 0:6, :].rearrange("p a t -> p (a t)")))
                                c_ve.inc(v.tensor_copy(qk4[:, b, 3072:3088], pq[:, 6, 0:16]))

                        QS, KS, VS, BS = 0, 1024, 2048, 3072
                        seg = lambda b, base: qk4[:, b, base:base + 1024]
                        seg3 = lambda b, base: qk4[:, b, base:base + 1024].rearrange("p (h d) -> p h d", d=64)
                        bseg = lambda b: qk4[:, b, BS:BS + 16]

                        @blk.scalar
                        def _(sc):
                            c_ve.wait(sc)
                            for b in range(B):
                                c_se.inc(sc.activation(bseg(b), bseg(b), AF.Sigmoid))

                        for b in range(B):
                            for base in (QS, KS):
                                @blk.vector
                                def _(v, b=b, base=base):
                                    c_se.wait(v)
                                    c_ve.inc(v.tensor_scalar_min(sca[:], seg(b, base), 0.0))

                                @blk.scalar
                                def _(sc):
                                    c_ve.wait(sc)
                                    c_se.inc(sc.activation(sca[:], sca[:], AF.Exp))

                                @blk.vector
                                def _(v, b=b, base=base):
                                    c_se.wait(v)
                                    c_ve.inc(v.tensor_scalar_max(scb[:], seg(b, base), 0.0))
                                    c_ve.inc(v.tensor_add(sca[:], sca[:], scb[:]))
                                    c_ve.inc(v.reduce_sum(gseg[:, :, 0:1].rearrange("p h o -> p (h o)"),
                                                          sca[:].rearrange("p (h d) -> p h d", d=64), axis=AX.X))
                                    v.wait_ge(c_ve.sem, c_ve.n)
                                    c_ve.inc(v.reciprocal(gseg[:, :, 0], gseg[:, :, 0]))
                                    v.wait_ge(c_ve.sem, c_ve.n)
                                    c_ve.inc(v.tensor_tensor(
                                        seg(b, base).rearrange("p (h d) -> p h d", d=64),
                                        sca[:].rearrange("p (h d) -> p h d", d=64),
                                        gseg[:, :, 0:1].broadcast_to([128, 16, 64]), ALU.mult))

                        @blk.vector
                        def _(v):
                            gi = 0
                            for i in range(B):
                                for j in range(i + 1, B):
                                    c_ve.inc(v.tensor_mul(sca[:], seg(i, KS), seg(j, KS)))
                                    c_ve.inc(v.reduce_sum(gseg[:, :, 1 + gi:2 + gi].rearrange("p h o -> p (h o)"),
                                                          sca[:].rearrange("p (h d) -> p h d", d=64), axis=AX.X))
                                    gi += 1
                            ai = 0
                            for ss in range(B):
                                for j in range(ss + 1):
                                    c_ve.inc(v.tensor_mul(sca[:], seg(j, KS), seg(ss, QS)))
                                    c_ve.inc(v.reduce_sum(gseg[:, :, 8 + ai:9 + ai].rearrange("p h o -> p (h o)"),
                                                          sca[:].rearrange("p (h d) -> p h d", d=64), axis=AX.X))
                                    ai += 1
                        @blk.vector
                        def _(v):
                            bc = lambda col: gseg[:, :, col:col + 1].broadcast_to([128, 16, 64])
                            G = {}
                            gi = 1
                            for i in range(B):
                                for j in range(i + 1, B):
                                    G[(i, j)] = gi
                                    gi += 1
                            AQ = {}
                            ai = 8
                            for ss in range(B):
                                for j in range(ss + 1):
                                    AQ[(ss, j)] = ai
                                    ai += 1
                            bcb = lambda b: qk4[:, b, BS:BS + 16].rearrange("p (h o) -> p h o", o=1).broadcast_to([128, 16, 64])
                            u3 = lambda i: u_s[:, i, :].rearrange("p (h d) -> p h d", d=64)
                            o3 = lambda i: o_s[:, i, :].rearrange("p (h d) -> p h d", d=64)
                            v3 = lambda b: seg3(b, VS)
                            s3a = sca[:].rearrange("p (h d) -> p h d", d=64)
                            v.wait_ge(c_ve.sem, c_ve.n)
                            c_ve.inc(v.tensor_tensor(u3(0), v3(0), bcb(0), ALU.mult))
                            for ss in range(1, B):
                                c_ve.inc(v.tensor_tensor(s3a, u3(0), bc(G[(0, ss)]), ALU.mult))
                                for i in range(1, ss):
                                    c_ve.inc(v.tensor_tensor(scb[:].rearrange("p (h d) -> p h d", d=64),
                                                             u3(i), bc(G[(i, ss)]), ALU.mult))
                                    c_ve.inc(v.tensor_add(sca[:], sca[:], scb[:]))
                                c_ve.inc(v.tensor_sub(sca[:], seg(ss, VS), sca[:]))
                                c_ve.inc(v.tensor_tensor(u3(ss), s3a, bcb(ss), ALU.mult))
                            for ss in range(B):
                                c_ve.inc(v.tensor_tensor(o3(ss), u3(0), bc(AQ[(ss, 0)]), ALU.mult))
                                for j in range(1, ss + 1):
                                    c_ve.inc(v.tensor_tensor(scb[:].rearrange("p (h d) -> p h d", d=64),
                                                             u3(j), bc(AQ[(ss, j)]), ALU.mult))
                                    c_ve.inc(v.tensor_add(o_s[:, ss, :], o_s[:, ss, :], scb[:]))

                        for b in range(B):
                            for hp2 in range(8):
                                @blk.tensor
                                def _(t, b=b, hp2=hp2, sub=sub):
                                    c_ve.wait(t)
                                    if hp2 == 0 and b == 0:
                                        pass
                                    c_se.wait(t)
                                    c_mm.inc(t.transpose(pt2[:, 0:128], o_s[:, b, hp2 * 128:(hp2 + 1) * 128],
                                                         iden_s[:]))

                                @blk.vector
                                def _(v, b=b, hp2=hp2, sub=sub):
                                    c_mm.wait(v)
                                    tok = slice(b * 512 + sub * 128, b * 512 + sub * 128 + 128)
                                    c_ve.inc(v.tensor_copy(loT[:, hp2, tok], pt2[:, 0:128]))

                with nc.psum_tensor("pw2", [128, 4, 512], F32) as pw2:
                    yv = s_y.rearrange("(c p) t -> p c t", p=128)
                    for m in range(8):
                        @blk.tensor
                        def _(t, m=m):
                            c_ve.wait(t)
                            for tt2 in range(4):
                                for ec in range(8):
                                    last = t.matmul(pw2[:, tt2, :], woT_s[:, ec, m * 128:(m + 1) * 128],
                                                    loT[:, ec, tt2 * 512:(tt2 + 1) * 512],
                                                    start=(ec == 0), stop=(ec == 7))
                                c_mm.inc(last)

                        for tt2 in range(4):
                            @blk.sync
                            def _(s, m=m, tt2=tt2):
                                c_ve.wait(s)
                                c_x1r.inc(s.dma_start(out=htile[:], in_=hv[:, m, tt2 * 512:(tt2 + 1) * 512]), 16)

                            @blk.vector
                            def _(v, m=m, tt2=tt2):
                                c_x1r.wait(v)
                                c_mm.wait(v)
                                c_ve.inc(v.tensor_add(ytile[:], pw2[:, tt2, :], htile[:]))

                            @blk.sync
                            def _(s, m=m, tt2=tt2):
                                c_ve.wait(s)
                                c_st.inc(s.dma_start(out=yv[:, m, tt2 * 512:(tt2 + 1) * 512], in_=ytile[:]), 16)

                with nc.psum_tensor("pf2", [128, 4, 512], F32) as pf2:
                    yv = s_y.rearrange("(c p) t -> p c t", p=128)
                    xv3 = s_x2.rearrange("(c p) t -> p c t", p=128)
                    ov3 = s_x3.rearrange("(c p) t -> p c t", p=128)
                    for qt in range(4):
                        tq = slice(qt * 512, (qt + 1) * 512)
                        for ch in range(8):
                            cx = c_xs0 if ch % 2 == 0 else c_xs1

                            @blk.sync
                            def _(s, ch=ch, cx=cx, tq=tq):
                                c_mm.wait(s)
                                c_ve.wait(s)
                                c_st.wait(s)
                                cx.inc(s.dma_start(out=xs2[:, ch % 2, :], in_=yv[:, ch, tq]), 16)

                            @blk.vector
                            def _(v, ch=ch, cx=cx):
                                cx.wait(v)
                                if ch >= 1:
                                    c_mm.wait(v)
                                c_ve.inc(v.tensor_mul(xq2[:], xs2[:, ch % 2, :], xs2[:, ch % 2, :]))

                            @blk.tensor
                            def _(t, ch=ch):
                                c_ve.wait(t)
                                t.matmul(pf2[0:1, 0, :], ones_col_s[:], xs2[:, ch % 2, :],
                                         start=(ch == 0), stop=(ch == 7))
                                c_mm.inc(t.matmul(pf2[0:1, 1, :], ones_col_s[:], xq2[:],
                                                  start=(ch == 0), stop=(ch == 7)))

                        @blk.vector
                        def _(v):
                            c_mm.wait(v)
                            c_ve.inc(v.tensor_scalar_mul(rf1[:], pf2[0:1, 0, :], 1.0 / D))
                            c_ve.inc(v.tensor_mul(rf2[:], rf1[:], rf1[:]))
                            c_ve.inc(v.scalar_tensor_tensor(rf2[:], pf2[0:1, 1, :], 1.0 / D,
                                                            rf2[:], ALU.mult, ALU.subtract))
                            c_ve.inc(v.tensor_scalar_add(rf2[:], rf2[:], LN_EPS))

                        @blk.scalar
                        def _(sc):
                            c_ve.wait(sc)
                            c_se.inc(sc.activation(rf2[:], rf2[:], AF.Sqrt))

                        @blk.vector
                        def _(v):
                            c_se.wait(v)
                            v.wait_ge(c_ve.sem, c_ve.n)
                            c_ve.inc(v.reciprocal(rf2[:], rf2[:]))
                            c_ve.inc(v.tensor_mul(rf1[:], rf1[:], rf2[:]))
                            c_ve.inc(v.tensor_scalar_mul(rf1[:], rf1[:], -1.0))

                        @blk.tensor
                        def _(t):
                            c_ve.wait(t)
                            t.matmul(pf2[:, 2, :], ones_row_s[:], rf2[:], start=True, stop=True)
                            c_mm.inc(t.matmul(pf2[:, 3, :], ones_row_s[:], rf1[:], start=True, stop=True))

                        for ch in range(8):
                            cx = c_xs0 if ch % 2 == 0 else c_xs1

                            @blk.sync
                            def _(s, ch=ch, cx=cx, tq=tq):
                                c_ve.wait(s)
                                cx.inc(s.dma_start(out=xs2[:, ch % 2, :], in_=yv[:, ch, tq]), 16)
                                c_x1r.inc(s.dma_start(out=htile[:], in_=xv3[:, ch, tq]), 16)

                            @blk.vector
                            def _(v, ch=ch, cx=cx):
                                cx.wait(v)
                                c_x1r.wait(v)
                                if ch == 0:
                                    c_mm.wait(v)
                                c_ve.inc(v.tensor_mul(xs2[:, ch % 2, :], xs2[:, ch % 2, :], pf2[:, 2, :]))
                                c_ve.inc(v.tensor_add(ytile[:], xs2[:, ch % 2, :], pf2[:, 3, :]))
                                c_ve.inc(v.tensor_scalar_add(htile[:], htile[:], lngb_s[:, ch, 1:2]))
                                c_ve.inc(v.scalar_tensor_tensor(ytile[:], ytile[:], lngb_s[:, ch, 0:1],
                                                                htile[:], ALU.mult, ALU.add))

                            @blk.sync
                            def _(s, ch=ch, tq=tq):
                                c_ve.wait(s)
                                c_st.inc(s.dma_start(out=ov3[:, ch, tq], in_=ytile[:]), 16)

            emit_ffn(s_x3, out_fin, wgT_1, wuT_1, wdT_1, wqu_hbm_0, wqd_hbm, "_f1")

            @blk.sync
            def _(s):
                s.wait_ge(c_st.sem, c_st.n)

    return nc


_NC_CACHE = {}


def _get_nc():
    if "nc" not in _NC_CACHE:
        _NC_CACHE["nc"] = build_nc()
    return _NC_CACHE["nc"]


def kernel(**inputs):
    maps = _stage_inputs(inputs)
    nc = _get_nc()
    res = run_bass_kernel_spmd(nc, maps, core_ids=list(range(8)))
    out = np.zeros((B, S, D), np.float32)
    for c in range(8):
        u = c
        y = res.results[c]["out_fin"]
        out[:, 512 * u:512 * (u + 1), :] = y.reshape(D, B, 512).transpose(1, 2, 0)
    return out


# version 04
# speedup vs baseline: 1.1178x; 1.1178x over previous
import math
import os
import numpy as np
import ml_dtypes
import concourse.bass as bass
import concourse.mybir as mybir
from concourse.bass_core_rust import IndirectOffsetOnAxis
from concourse.bass_utils import run_bass_kernel_spmd

F32 = mybir.dt.float32
BF16 = mybir.dt.bfloat16
AF = mybir.ActivationFunctionType
ALU = mybir.AluOpType
AX = mybir.AxisListType

D = 1024
S = 4096
B = 4
NH = 16
HS = 64
FR = 256
NHC = 8
QC = 512
NQC = S // QC
NKT = S // 128
TOK2 = 2048
INNER = 4096

C_QKV = float((3.0 * D * HS * NH) ** (-0.25))
C_SQ2F = float(math.sqrt(2.0) / 16.0)
C_H = float(HS ** (-0.5))
C_OUT = float((D * D) ** (-0.25))
MAGIC = float(1.5 * 2 ** 23)
LN_EPS = 1e-5

_OMEGA = (10000.0 ** (-(np.arange(FR) // 2 * 2) / FR)).astype(np.float64)[::2]


def _g_of_lane(eps):
    l = np.arange(128)
    return l if eps == 1 else (l + 64) % 128


def _host_tables():
    t = {}
    k = np.arange(S, dtype=np.float64)
    for eps in (0, 1):
        g = _g_of_lane(eps)
        ang = k[None, :] * _OMEGA[g][:, None]
        sin, cos = np.sin(ang), np.cos(ang)
        in_k = ((np.arange(128) // 64) == eps)[:, None]
        t[f"sinT{eps}"] = sin.astype(ml_dtypes.bfloat16)
        t[f"cosT{eps}"] = cos.astype(ml_dtypes.bfloat16)
        t[f"khatB{eps}"] = np.where(in_k, cos, sin).astype(ml_dtypes.bfloat16)
        t[f"ceA{eps}"] = np.where(~in_k, cos, 0.0).astype(ml_dtypes.bfloat16)
    l = np.arange(128)
    ang2 = k[None, :] * _OMEGA[(l % 64) + 64][:, None]
    t["khatC2"] = np.sin(ang2).astype(ml_dtypes.bfloat16)
    m = np.zeros((4, 128, QC), np.float32)
    kk = np.arange(128)[:, None]
    qq = np.arange(QC)[None, :]
    for j in range(4):
        m[j] = (128 * j + kk <= qq)
    t["masks"] = np.ascontiguousarray(m.transpose(1, 0, 2)).astype(ml_dtypes.bfloat16)
    return t


def _stage_inputs(inp):
    x = np.ascontiguousarray(np.asarray(inp["L0_attn_input"], dtype=np.float32))
    qkv = np.asarray(inp["L0_attn_qkv"], dtype=np.float32)
    qb = np.asarray(inp["L0_attn_q_bias"], dtype=np.float32)
    pos = np.asarray(inp["L0_attn_positional"], dtype=np.float32)
    wout = np.asarray(inp["L0_attn_out_kernel"], dtype=np.float32)
    ob = np.asarray(inp["L0_attn_out_bias"], dtype=np.float32)

    Wqkvb = np.asarray(inp["L1_attn_W_qkvb"], dtype=np.float32)
    eperm = np.zeros(3088, np.int64)
    for n in range(16):
        j = np.arange(193)
        src_rows = n * 193 + j
        dst = np.where(j < 64, n * 64 + j,
              np.where(j < 128, 1024 + n * 64 + (j - 64),
              np.where(j < 192, 2048 + n * 64 + (j - 128), 3072 + n)))
        eperm[dst] = src_rows
    wqkvbT = np.ascontiguousarray(Wqkvb[eperm, :].T)
    woT = np.ascontiguousarray(np.asarray(inp["L1_attn_W_o"], dtype=np.float32).T)
    lng = np.asarray(inp["L1_attn_ln_g"], dtype=np.float32)
    lnb = np.asarray(inp["L1_attn_ln_b"], dtype=np.float32)
    lngb = np.stack([lng.reshape(8, 128).T, lnb.reshape(8, 128).T], axis=2)
    wgT0 = np.ascontiguousarray(np.asarray(inp["L0_ffn_w_gate"], dtype=np.float32).T)
    wuT0 = np.ascontiguousarray(np.asarray(inp["L0_ffn_w_up"], dtype=np.float32).T)
    wdT0 = np.ascontiguousarray(np.asarray(inp["L0_ffn_w_down"], dtype=np.float32).T)
    wgT1 = np.ascontiguousarray(np.asarray(inp["L1_ffn_w_gate"], dtype=np.float32).T)
    wuT1 = np.ascontiguousarray(np.asarray(inp["L1_ffn_w_up"], dtype=np.float32).T)
    wdT1 = np.ascontiguousarray(np.asarray(inp["L1_ffn_w_down"], dtype=np.float32).T)
    maps = []
    for c in range(8):
        b, hf = c // 2, c % 2
        m = {}
        m["x_b_fm"] = np.ascontiguousarray(x[b].T)
        u = c
        m["x_res_fm"] = np.ascontiguousarray(
            x[:, 512 * u:512 * (u + 1), :].transpose(2, 0, 1).reshape(D, TOK2))
        cols = []
        for p in range(4):
            cols += [qkv[:, 1, hf * 8 + 2 * p, :], qkv[:, 1, hf * 8 + 2 * p + 1, :]]
        for p in range(4):
            cols += [qkv[:, 0, hf * 8 + 2 * p, :], qkv[:, 0, hf * 8 + 2 * p + 1, :]]
        m["w_kq"] = np.ascontiguousarray(np.concatenate(cols, axis=1))
        m["w_v"] = np.ascontiguousarray(qkv[:, 2, hf * 8:hf * 8 + 8, :].reshape(D, 512))
        qbc = np.zeros((128, 4), np.float32)
        for p in range(4):
            qbc[0:64, p] = qb[hf * 8 + 2 * p]
            qbc[64:128, p] = qb[hf * 8 + 2 * p + 1]
        m["q_bias_col"] = qbc
        pa = np.zeros((128, NHC, 2, 128), np.float32)
        for h in range(NHC):
            eps = h & 1
            g = _g_of_lane(eps)
            n = hf * 8 + h
            for e in range(2):
                f = 2 * g + e
                pa[64 * eps:64 * eps + 64, h, e, :] = pos[f, n, :].T
        m["pos_arr"] = pa
        perm = np.zeros(D, np.int64)
        for ch in range(8):
            hf2, fc = ch // 4, ch % 4
            p = np.arange(128)
            n = hf2 * 8 + fc * 2 + p // 64
            perm[ch * 128 + p] = n * 64 + (p % 64)
        m["w_out"] = np.ascontiguousarray(wout[perm, :])
        m["out_bias_col"] = np.ascontiguousarray(ob.reshape(8, 128).T)
        m["wgT_0"] = wgT0
        m["wqkvbT"] = wqkvbT
        m["woT"] = woT
        m["lngb"] = np.ascontiguousarray(lngb)
        m["wuT_0"] = wuT0
        m["wdT_0"] = wdT0
        m["wgT_1"] = wgT1
        m["wuT_1"] = wuT1
        m["wdT_1"] = wdT1
        idx = np.zeros((128, 8, 4), np.int32)
        p = np.arange(128)
        for ch in range(8):
            hf2, fc = ch // 4, ch % 4
            for bb in range(4):
                r = 2 * bb + hf2
                idx[:, ch, bb] = (r * NQC * 65 * NHC + u * (65 * NHC)
                                  + (1 + p % 64) * NHC + (fc * 2 + p // 64))
        m["imp_idx"] = idx
        maps.append(m)
    return maps


class Ctr:
    def __init__(self, stack, nc, name):
        self.sem = stack.enter_context(nc.semaphore(name))
        self.n = 0

    def inc(self, instr, k=1):
        instr.then_inc(self.sem, k)
        self.n += k
        return instr

    def wait(self, eng, n=None):
        eng.wait_ge(self.sem, self.n if n is None else n)


def build_nc():
    nc = bass.Bass("TRN2", target_bir_lowering=False, detect_race_conditions=False)
    T = _host_tables()

    x_b_fm = nc.dram_tensor("x_b_fm", [D, S], F32, kind="ExternalInput").ap()
    x_res_fm = nc.dram_tensor("x_res_fm", [D, TOK2], F32, kind="ExternalInput").ap()
    w_kq = nc.dram_tensor("w_kq", [D, D], F32, kind="ExternalInput").ap()
    w_v = nc.dram_tensor("w_v", [D, 512], F32, kind="ExternalInput").ap()
    q_bias_col = nc.dram_tensor("q_bias_col", [128, 4], F32, kind="ExternalInput").ap()
    pos_arr = nc.dram_tensor("pos_arr", [128, NHC, 2, 128], F32, kind="ExternalInput").ap()
    w_out = nc.dram_tensor("w_out", [D, D], F32, kind="ExternalInput").ap()
    out_bias_col = nc.dram_tensor("out_bias_col", [128, 8], F32, kind="ExternalInput").ap()
    imp_idx = nc.dram_tensor("imp_idx", [128, 8, 4], mybir.dt.int32, kind="ExternalInput").ap()

    out_fin = nc.dram_tensor("out_fin", [D, TOK2], F32, kind="ExternalOutput").ap()
    _K2 = "ExternalOutput" if os.environ.get("KDBG2") == "1" else "Internal"
    s_x1 = nc.dram_tensor("s_x1", [D, TOK2], F32, kind=_K2).ap()
    wgT_0 = nc.dram_tensor("wgT_0", [D, INNER], F32, kind="ExternalInput").ap()
    wuT_0 = nc.dram_tensor("wuT_0", [D, INNER], F32, kind="ExternalInput").ap()
    wdT_0 = nc.dram_tensor("wdT_0", [INNER, D], F32, kind="ExternalInput").ap()
    wgT_1 = nc.dram_tensor("wgT_1", [D, INNER], F32, kind="ExternalInput").ap()
    wuT_1 = nc.dram_tensor("wuT_1", [D, INNER], F32, kind="ExternalInput").ap()
    wdT_1 = nc.dram_tensor("wdT_1", [INNER, D], F32, kind="ExternalInput").ap()
    wqd_hbm = nc.dram_tensor("wqd_hbm", [INNER, D], BF16, kind="Internal").ap()
    wqu_hbm_0 = nc.dram_tensor("wqu_hbm_0", [32, 8, 128, 128], BF16, kind="Internal").ap()
    s_x2 = nc.dram_tensor("s_x2", [D, TOK2], F32, kind=_K2).ap()
    s_x3 = nc.dram_tensor("s_x3", [D, TOK2], F32, kind=_K2).ap()
    s_h = nc.dram_tensor("s_h", [D, TOK2], F32, kind="Internal").ap()
    s_y = nc.dram_tensor("s_y", [D, TOK2], F32, kind="Internal").ap()
    wqkvbT_d = nc.dram_tensor("wqkvbT", [D, 3088], F32, kind="ExternalInput").ap()
    woT_d = nc.dram_tensor("woT", [D, D], F32, kind="ExternalInput").ap()
    lngb_d = nc.dram_tensor("lngb", [128, 8, 2], F32, kind="ExternalInput").ap()
    c_iden = nc.inline_tensor(np.eye(128, dtype=np.float32), name="c_iden").ap()
    DBG = os.environ.get("KDBG") == "1"
    if DBG:
        dbg_lnx = nc.dram_tensor("dbg_lnx", [128, 8, 512], BF16, kind="ExternalOutput").ap()
        dbg_qp = nc.dram_tensor("dbg_qp", [128, 4, S], BF16, kind="ExternalOutput").ap()
        dbg_khA = nc.dram_tensor("dbg_khA", [128, NHC, S], BF16, kind="ExternalOutput").ap()
        dbg_vh = nc.dram_tensor("dbg_vh", [128, NHC, NKT, 65], BF16, kind="ExternalOutput").ap()
        dbg_qhat = nc.dram_tensor("dbg_qhat", [128, NHC, 3, QC], BF16, kind="ExternalOutput").ap()
        dbg_oacc = nc.dram_tensor("dbg_oacc", [65, 2, QC], F32, kind="ExternalOutput").ap()
        dbg_oall = nc.dram_tensor("dbg_oall", [128, 8, TOK2], BF16, kind="ExternalOutput").ap()
        dbg_at = nc.dram_tensor("dbg_at", [128, 4, 2, QC], BF16, kind="ExternalOutput").ap()
        dbg_xq = nc.dram_tensor("dbg_xq", [128, 8, 512], BF16, kind="ExternalOutput").ap()
        dbg_gu = nc.dram_tensor("dbg_gu", [128, 32, 512], BF16, kind="ExternalOutput").ap()
        dbg_rows = nc.dram_tensor("dbg_rows", [1, 4, 512], F32, kind="ExternalOutput").ap()
        dbg_wqg = nc.dram_tensor("dbg_wqg", [128, 8, INNER], BF16, kind="ExternalOutput").ap()
        dbg_wut = nc.dram_tensor("dbg_wut", [128, 8, 128], BF16, kind="ExternalOutput").ap()
        dbg_sw = nc.dram_tensor("dbg_sw", [128, 6], F32, kind="ExternalOutput").ap()

    cin = nc.dram_tensor("cin", [NQC, 65, NHC, QC], BF16, kind="Internal").ap()
    cout = nc.dram_tensor("cout", [8, NQC, 65, NHC, QC], BF16, kind="Internal").ap()

    c_sin = [nc.inline_tensor(np.ascontiguousarray(T[f"sinT{e}"]), name=f"c_sin{e}").ap() for e in (0, 1)]
    c_cos = [nc.inline_tensor(np.ascontiguousarray(T[f"cosT{e}"]), name=f"c_cos{e}").ap() for e in (0, 1)]
    c_khB = [nc.inline_tensor(np.ascontiguousarray(T[f"khatB{e}"]), name=f"c_khB{e}").ap() for e in (0, 1)]
    c_khC2 = nc.inline_tensor(np.ascontiguousarray(T["khatC2"]), name="c_khC2").ap()
    c_ceA = [nc.inline_tensor(np.ascontiguousarray(T[f"ceA{e}"]), name=f"c_ceA{e}").ap() for e in (0, 1)]
    c_masks = nc.inline_tensor(np.ascontiguousarray(T["masks"]), name="c_masks").ap()
    c_ones_row = nc.inline_tensor(np.ones((1, 128), np.float32), name="c_ones_row").ap()
    c_ones_col = nc.inline_tensor(np.ones((128, 1), np.float32), name="c_ones_col").ap()

    from contextlib import ExitStack
    stk = ExitStack()
    with stk:
        def sb(name, shape, dt, st=stk):
            return st.enter_context(nc.sbuf_tensor(name, shape, dt))

        ones_row_s = sb("ones_row_s", [1, 128], F32)
        ones_col_s = sb("ones_col_s", [128, 1], F32)
        qbias_s = sb("qbias_s", [128, 4], F32)
        obias_s = sb("obias_s", [128, 8], F32)
        impi_s = sb("impi_s", [128, 8, 4], mybir.dt.int32)

        c_ld = Ctr(stk, nc, "c_ld")
        c_xt0 = Ctr(stk, nc, "c_xt0")
        c_xt1 = Ctr(stk, nc, "c_xt1")
        c_sc = Ctr(stk, nc, "c_sc")
        c_x0 = Ctr(stk, nc, "c_x0")
        c_gld = Ctr(stk, nc, "c_gld")
        c_mm = Ctr(stk, nc, "c_mm")
        c_ve = Ctr(stk, nc, "c_ve")
        c_se = Ctr(stk, nc, "c_se")
        c_s3 = Ctr(stk, nc, "c_s3")
        c_exp = Ctr(stk, nc, "c_exp")
        c_msk = Ctr(stk, nc, "c_msk")
        c_av = Ctr(stk, nc, "c_av")
        c_oev = Ctr(stk, nc, "c_oev")
        c_rot = Ctr(stk, nc, "c_rot")
        c_tmm = Ctr(stk, nc, "c_tmm")
        c_exd = Ctr(stk, nc, "c_exd")
        c_cc = Ctr(stk, nc, "c_cc")
        c_imp = Ctr(stk, nc, "c_imp")
        c_st = Ctr(stk, nc, "c_st")
        c_ws0 = Ctr(stk, nc, "c_ws0")
        c_ws1 = Ctr(stk, nc, "c_ws1")
        c_wu0 = Ctr(stk, nc, "c_wu0")
        c_wu1 = Ctr(stk, nc, "c_wu1")
        c_wd0 = Ctr(stk, nc, "c_wd0")
        c_wd1 = Ctr(stk, nc, "c_wd1")
        c_xs0 = Ctr(stk, nc, "c_xs0")
        c_xs1 = Ctr(stk, nc, "c_xs1")
        c_x1r = Ctr(stk, nc, "c_x1r")
        c_fd = Ctr(stk, nc, "c_fd")
        c_hs0 = Ctr(stk, nc, "c_hs0")
        c_hs1 = Ctr(stk, nc, "c_hs1")
        c_sq = Ctr(stk, nc, "c_sq")

        with nc.Block() as blk:
            atn = ExitStack()
            with atn:
                def sba(name, shape, dt):
                    return atn.enter_context(nc.sbuf_tensor(name, shape, dt))

                khatA = sba("khatA", [128, NHC, S], BF16)
                khB = sba("khB", [128, 2, S], BF16)
                khC = sba("khC", [128, S], BF16)
                vhat = sba("vhat", [128, NHC, NKT, 65], BF16)
                qp_all = sba("qp_all", [128, 4, S], BF16)
                posb = sba("posb", [128, NHC, 2, 128], BF16)
                masks_s = sba("masks_s", [128, 4, QC], BF16)

                @blk.sync
                def _(s):
                    c_ld.inc(s.dma_start(out=masks_s[:], in_=c_masks), 16)
                    c_ld.inc(s.dma_start(out=ones_row_s[:], in_=c_ones_row), 16)
                    c_ld.inc(s.dma_start(out=ones_col_s[:], in_=c_ones_col), 16)
                    c_ld.inc(s.dma_start(out=qbias_s[:], in_=q_bias_col), 16)
                    c_ld.inc(s.dma_start(out=obias_s[:], in_=out_bias_col), 16)
                    c_ld.inc(s.dma_start(out=impi_s[:], in_=imp_idx), 16)
                    for e in (0, 1):
                        c_ld.inc(s.dma_start(out=khB[:, e, :], in_=c_khB[e]), 16)
                    c_ld.inc(s.dma_start(out=khC[:], in_=c_khC2), 16)
                    for h in range(NHC):
                        e = h & 1
                        lo = 64 - 64 * e
                        c_ld.inc(s.dma_start(out=khatA[lo:lo + 64, h, :],
                                             in_=c_ceA[e][lo:lo + 64, :]), 16)
                n_const = c_ld.n

                @blk.gpsimd
                def _(g):
                    c_gld.inc(g.dma_start(out=posb[:], in_=pos_arr), 16)

                @blk.vector
                def _(v):
                    c_ve.inc(v.memset(vhat[:, :, :, 0:1], 1.0))
                    v.wait_ge(c_ld.sem, n_const)
                    c_ve.inc(v.tensor_scalar_mul(qbias_s[:], qbias_s[:], C_H))

                pha = ExitStack()
                with pha:
                    def sbp(name, shape, dt):
                        return pha.enter_context(nc.sbuf_tensor(name, shape, dt))

                    xt = sbp("xt", [128, 2, 512], F32)
                    xsq = sbp("xsq", [128, 512], F32)
                    lnx = sbp("lnx", [128, 8, 512], BF16)
                    rowa = sbp("rowa", [1, 512], F32)
                    rowb = sbp("rowb", [1, 512], F32)
                    rowc = sbp("rowc", [1, 512], F32)
                    wkq_s = sbp("wkq_s", [128, 8, D], BF16)
                    wv_s = sbp("wv_s", [128, 8, 512], BF16)

                    @blk.gpsimd
                    def _(g):
                        c_gld.inc(g.dma_start(out=wkq_s[:], in_=w_kq.rearrange("(c p) m -> p c m", p=128)), 16)
                        c_gld.inc(g.dma_start(out=wv_s[:], in_=w_v.rearrange("(c p) m -> p c m", p=128)), 16)

                    @blk.vector
                    def _(v):
                        c_gld.wait(v)
                        c_ve.inc(v.tensor_scalar_mul(posb[:], posb[:], C_SQ2F))

                    with (
                        nc.psum_tensor("pa_sum", [1, 2, 512], F32) as pa_sum,
                        nc.psum_tensor("pa_bc", [128, 2, 512], F32) as pa_bc,
                        nc.psum_tensor("pa_kq", [128, 512], F32) as pa_kq,
                        nc.psum_tensor("pa_v", [128, 512], F32) as pa_v,
                    ):
                        xv = x_b_fm.rearrange("(c p) t -> p c t", p=128)
                        for tg in range(8):
                            tsl = slice(tg * 512, (tg + 1) * 512)
                            for ch in range(8):
                                cx = c_xt0 if ch % 2 == 0 else c_xt1

                                @blk.sync
                                def _(s, ch=ch, tsl=tsl, cx=cx):
                                    c_mm.wait(s)
                                    c_ve.wait(s)
                                    cx.inc(s.dma_start(out=xt[:, ch % 2, :], in_=xv[:, ch, tsl]), 16)

                                @blk.vector
                                def _(v, ch=ch, cx=cx):
                                    cx.wait(v)
                                    if ch >= 1:
                                        c_mm.wait(v)
                                    c_ve.inc(v.tensor_mul(xsq[:], xt[:, ch % 2, :], xt[:, ch % 2, :]))

                                @blk.tensor
                                def _(t, ch=ch):
                                    c_ve.wait(t)
                                    t.matmul(pa_sum[:, 0, :], ones_col_s[:], xt[:, ch % 2, :],
                                             start=(ch == 0), stop=(ch == 7))
                                    c_mm.inc(t.matmul(pa_sum[:, 1, :], ones_col_s[:], xsq[:],
                                                      start=(ch == 0), stop=(ch == 7)))

                            @blk.vector
                            def _(v):
                                c_mm.wait(v)
                                c_ve.inc(v.tensor_scalar_mul(rowa[:], pa_sum[0:1, 0, :], 1.0 / D))
                                c_ve.inc(v.tensor_mul(rowc[:], rowa[:], rowa[:]))
                                c_ve.inc(v.scalar_tensor_tensor(rowb[:], pa_sum[0:1, 1, :], 1.0 / D,
                                                                rowc[:], ALU.mult, ALU.subtract))
                                c_ve.inc(v.tensor_scalar_add(rowb[:], rowb[:], LN_EPS))

                            @blk.scalar
                            def _(sc):
                                c_ve.wait(sc)
                                c_se.inc(sc.activation(rowb[:], rowb[:], AF.Sqrt))

                            @blk.vector
                            def _(v):
                                c_se.wait(v)
                                v.wait_ge(c_ve.sem, c_ve.n)
                                c_ve.inc(v.reciprocal(rowb[:], rowb[:]))
                                c_ve.inc(v.tensor_scalar_mul(rowb[:], rowb[:], C_QKV))
                                c_ve.inc(v.tensor_mul(rowc[:], rowa[:], rowb[:]))
                                c_ve.inc(v.tensor_scalar_mul(rowc[:], rowc[:], -1.0))

                            @blk.tensor
                            def _(t):
                                c_ve.wait(t)
                                t.matmul(pa_bc[:, 0, :], ones_row_s[:], rowb[:], start=True, stop=True)
                                c_mm.inc(t.matmul(pa_bc[:, 1, :], ones_row_s[:], rowc[:], start=True, stop=True))

                            for ch in range(8):
                                cx = c_xt0 if ch % 2 == 0 else c_xt1

                                @blk.sync
                                def _(s, ch=ch, tsl=tsl, cx=cx):
                                    c_ve.wait(s)
                                    cx.inc(s.dma_start(out=xt[:, ch % 2, :], in_=xv[:, ch, tsl]), 16)

                                @blk.vector
                                def _(v, ch=ch, cx=cx):
                                    cx.wait(v)
                                    if ch == 0:
                                        c_mm.wait(v)
                                    c_ve.inc(v.tensor_mul(xt[:, ch % 2, :], xt[:, ch % 2, :], pa_bc[:, 0, :]))
                                    c_ve.inc(v.tensor_add(lnx[:, ch, :], xt[:, ch % 2, :], pa_bc[:, 1, :]))
                            lnx_done = c_ve.n
                            if DBG and tg == 1:
                                @blk.sync
                                def _(s):
                                    s.wait_ge(c_ve.sem, lnx_done)
                                    c_st.inc(s.dma_start(out=dbg_lnx, in_=lnx[:]), 16)

                            for mck in range(8):
                                @blk.tensor
                                def _(t, mck=mck, lnx_done=lnx_done):
                                    t.wait_ge(c_ve.sem, lnx_done)
                                    c_ve.wait(t)
                                    c_se.wait(t)
                                    if mck == 0:
                                        c_gld.wait(t)
                                    for dch in range(8):
                                        last = t.matmul(pa_kq[:], wkq_s[:, dch, mck * 128:(mck + 1) * 128],
                                                        lnx[:, dch, :], start=(dch == 0), stop=(dch == 7))
                                    c_mm.inc(last)

                                if mck < 4:
                                    @blk.vector
                                    def _(v, mck=mck, tsl=tsl):
                                        c_mm.wait(v)
                                        c_ve.inc(v.tensor_copy(khatA[0:64, 2 * mck, tsl], pa_kq[0:64, :]))
                                        c_ve.inc(v.tensor_copy(khatA[64:128, 2 * mck + 1, tsl], pa_kq[64:128, :]))
                                else:
                                    @blk.scalar
                                    def _(sc, mck=mck, tsl=tsl):
                                        c_mm.wait(sc)
                                        c_se.inc(sc.activation(qp_all[:, mck - 4, tsl], pa_kq[:], AF.Identity,
                                                               bias=qbias_s[:, mck - 4:mck - 3], scale=C_H))

                            for tt in range(4):
                                @blk.tensor
                                def _(t, tt=tt):
                                    c_ve.wait(t)
                                    c_se.wait(t)
                                    for dch in range(8):
                                        last = t.matmul(pa_v[:], lnx[:, dch, tt * 128:(tt + 1) * 128],
                                                        wv_s[:, dch, :], start=(dch == 0), stop=(dch == 7))
                                    c_mm.inc(last)

                                @blk.vector
                                def _(v, tg=tg, tt=tt):
                                    c_mm.wait(v)
                                    kt = tg * 4 + tt
                                    for h in range(NHC):
                                        c_ve.inc(v.tensor_copy(vhat[:, h, kt, 1:65],
                                                               pa_v[:, h * 64:(h + 1) * 64]))

                phb = ExitStack()
                with phb:
                    def sbb(name, shape, dt):
                        return phb.enter_context(nc.sbuf_tensor(name, shape, dt))

                    qhat = sbb("qhat", [128, NHC, 3, QC], BF16)
                    oacc = sbb("oacc", [65, 2, QC], F32)
                    at_s = sbb("at_s", [128, 2, 2, QC], BF16)
                    exp_s = sbb("exp_s", [65, 2, QC], BF16)
                    uws = sbb("uws", [128, QC], F32)
                    uw2 = sbb("uw2", [128, QC], F32)
                    den = sbb("den", [1, QC], F32)
                    sc_qc = sbb("sc_qc", [128, 2, 2, QC], BF16)

                    @blk.vector
                    def _(v):
                        c_mm.wait(v)
                        c_se.wait(v)
                        c_ve.inc(v.memset(qhat[:], 0.0))

                    if DBG:
                        @blk.sync
                        def _(s):
                            c_mm.wait(s)
                            c_ve.wait(s)
                            c_se.wait(s)
                            c_st.inc(s.dma_start(out=dbg_qp, in_=qp_all[:]), 16)
                            c_st.inc(s.dma_start(out=dbg_khA, in_=khatA[:]), 16)
                            c_st.inc(s.dma_start(out=dbg_vh, in_=vhat[:]), 16)

                    with (
                        nc.psum_tensor("pb_T", [128, 2, QC], F32) as pb_T,
                        nc.psum_tensor("pb_S", [128, 2, 2, QC], F32) as pb_S,
                        nc.psum_tensor("pb_O", [65, 2, QC], F32) as pb_O,
                    ):
                        it = 0
                        nmask = 0
                        for qc in range(NQC):
                            qsl = slice(qc * QC, (qc + 1) * QC)
                            s3_before = c_s3.n

                            @blk.sync
                            def _(s, qsl=qsl, first=(qc == 0)):
                                if first:
                                    c_mm.wait(s)
                                    c_ve.wait(s)
                                    c_se.wait(s)
                                c_rot.wait(s)
                                for e in (0, 1):
                                    c_sc.inc(s.dma_start(out=sc_qc[:, e, 0, :], in_=c_sin[e][:, qsl]), 16)
                                    c_sc.inc(s.dma_start(out=sc_qc[:, e, 1, :], in_=c_cos[e][:, qsl]), 16)

                            for h in range(NHC):
                                e = h & 1

                                @blk.tensor
                                def _(t, qc=qc, h=h, e=e, qsl=qsl):
                                    if h == 0:
                                        c_ve.wait(t)
                                    if qc == 0 and h == 0:
                                        c_se.wait(t)
                                    c_rot.wait(t)
                                    t.matmul(pb_T[:, 0, :], posb[64 * e:64 * e + 64, h, 0, :],
                                             qp_all[64 * e:64 * e + 64, h // 2, qsl],
                                             start=True, stop=True)
                                    c_tmm.inc(t.matmul(pb_T[:, 1, :], posb[64 * e:64 * e + 64, h, 1, :],
                                                       qp_all[64 * e:64 * e + 64, h // 2, qsl],
                                                       start=True, stop=True))

                                @blk.vector
                                def _(v, h=h, e=e, s3_before=s3_before):
                                    c_tmm.wait(v)
                                    c_sc.wait(v)
                                    if h == 0:
                                        v.wait_ge(c_s3.sem, s3_before)
                                    ko = 64 * e
                                    no = 64 - ko
                                    c_ve.inc(v.tensor_mul(uws[:], pb_T[:, 0, :], sc_qc[:, e, 0, :]))
                                    c_ve.inc(v.tensor_mul(uw2[:], pb_T[:, 1, :], sc_qc[:, e, 1, :]))
                                    c_ve.inc(v.tensor_add(uws[:], uws[:], uw2[:]))
                                    c_ve.inc(v.tensor_copy(qhat[ko:ko + 64, h, 1, :], uws[ko:ko + 64, :]))
                                    c_ve.inc(v.tensor_copy(qhat[no:no + 64, h, 0, :], uws[no:no + 64, :]))
                                    c_ve.inc(v.tensor_mul(uws[:], pb_T[:, 1, :], sc_qc[:, e, 0, :]))
                                    c_ve.inc(v.tensor_mul(uw2[:], pb_T[:, 0, :], sc_qc[:, e, 1, :]))
                                    c_ve.inc(v.tensor_sub(uws[:], uws[:], uw2[:]))
                                    c_ve.inc(v.tensor_copy(qhat[ko:ko + 64, h, 2, :], uws[ko:ko + 64, :]))
                                    c_rot.inc(v.tensor_copy(qhat[no:no + 64, h, 1, :], uws[no:no + 64, :]))

                            @blk.vector
                            def _(v, qsl=qsl):
                                c_ve.inc(v.tensor_copy(qhat[0:64, 0:NHC:2, 0, :], qp_all[0:64, :, qsl]))
                                c_rot.inc(v.tensor_copy(qhat[64:128, 1:NHC:2, 0, :], qp_all[64:128, :, qsl]))
                            rot_done = c_rot.n
                            if DBG and qc == 0:
                                @blk.sync
                                def _(s, rot_done=rot_done):
                                    s.wait_ge(c_rot.sem, rot_done)
                                    c_st.inc(s.dma_start(out=dbg_qhat, in_=qhat[:]), 16)

                            nkt = 4 * (qc + 1)
                            ktgs = [range(g0, min(g0 + 8, nkt)) for g0 in range(0, nkt, 8)]
                            for pr in range(4):
                                for gi, ktg in enumerate(ktgs):
                                    first = True
                                    pend_av = [None]

                                    def mk_av(pr, kt, buf, it, diag, nmask, first, last_kt):
                                        def emit_av(t):
                                            if diag:
                                                t.wait_ge(c_msk.sem, nmask)
                                            else:
                                                t.wait_ge(c_exp.sem, it + 1)
                                            if first:
                                                t.wait_ge(c_oev.sem, c_oev.n)
                                            for hp in range(2):
                                                h = 2 * pr + hp
                                                last = t.matmul(pb_O[:, hp, :], vhat[:, h, kt, :],
                                                                at_s[:, buf, hp, :],
                                                                start=first, stop=last_kt)
                                            c_av.inc(last)
                                        return emit_av

                                    for kt in ktg:
                                        buf = it % 2
                                        diag = (kt >= 4 * qc)
                                        if diag:
                                            nmask += 1

                                        @blk.tensor
                                        def _(t, pr=pr, kt=kt, buf=buf, it=it, rot_done=rot_done,
                                              pend=pend_av[0]):
                                            t.wait_ge(c_rot.sem, rot_done)
                                            if it >= 2:
                                                t.wait_ge(c_exp.sem, it - 1)
                                            ksl = slice(kt * 128, (kt + 1) * 128)
                                            for hp in range(2):
                                                h = 2 * pr + hp
                                                e = h & 1
                                                t.matmul(pb_S[:, buf, hp, :], khatA[:, h, ksl],
                                                         qhat[:, h, 0, :], start=True, stop=False)
                                                t.matmul(pb_S[:, buf, hp, :], khB[:, e, ksl],
                                                         qhat[:, h, 1, :], start=False, stop=False)
                                                last = t.matmul(pb_S[:, buf, hp, :],
                                                                khC[64 * e:64 * e + 64, ksl],
                                                                qhat[64 * e:64 * e + 64, h, 2, :],
                                                                start=False, stop=True)
                                            c_s3.inc(last)
                                            if pend is not None:
                                                pend(t)

                                        @blk.scalar
                                        def _(sc, buf=buf, it=it):
                                            sc.wait_ge(c_s3.sem, it + 1)
                                            if it >= 2:
                                                sc.wait_ge(c_av.sem, it - 1)
                                            c_exp.inc(sc.activation(
                                                at_s[:, buf, :, :].rearrange("p a q -> p (a q)"),
                                                pb_S[:, buf, :, :].rearrange("p a q -> p (a q)"),
                                                AF.Exp))

                                        if diag:
                                            j = kt - 4 * qc

                                            @blk.vector
                                            def _(v, buf=buf, it=it, j=j):
                                                v.wait_ge(c_exp.sem, it + 1)
                                                c_msk.inc(v.tensor_mul(
                                                    at_s[:, buf, :, :], at_s[:, buf, :, :],
                                                    masks_s[:, j:j + 1, :].broadcast_to([128, 2, QC])))

                                        if DBG and qc == 0 and pr == 0:
                                            @blk.sync
                                            def _(s, kt=kt, buf=buf, nmask=nmask):
                                                s.wait_ge(c_msk.sem, nmask)
                                                c_st.inc(s.dma_start(out=dbg_at[:, kt, :, :],
                                                                     in_=at_s[:, buf, :, :]), 16)

                                        pend_av[0] = mk_av(pr, kt, buf, it, diag, nmask,
                                                           first, kt == ktg[-1])
                                        first = False
                                        it += 1

                                    @blk.tensor
                                    def _(t, pend=pend_av[0]):
                                        pend(t)

                                    @blk.vector
                                    def _(v, it=it, first_group=(gi == 0)):
                                        v.wait_ge(c_av.sem, it)
                                        if first_group:
                                            c_oev.inc(v.tensor_copy(oacc[:], pb_O[:]))
                                        else:
                                            c_oev.inc(v.tensor_add(oacc[:], oacc[:], pb_O[:]))

                                if DBG and qc == 0 and pr == 0:
                                    @blk.sync
                                    def _(s):
                                        c_oev.wait(s)
                                        c_st.inc(s.dma_start(out=dbg_oacc, in_=oacc[:]), 16)

                                for hp in range(2):
                                    @blk.vector
                                    def _(v, hp=hp):
                                        c_oev.wait(v)
                                        c_mm.wait(v)
                                        c_exd.wait(v)
                                        v.wait_ge(c_ve.sem, c_ve.n)
                                        c_ve.inc(v.reciprocal(den[:], oacc[0:1, hp, :]))

                                    @blk.tensor
                                    def _(t, hp=hp):
                                        c_ve.wait(t)
                                        c_mm.inc(t.matmul(pb_T[:, hp, :], ones_row_s[:], den[:],
                                                          start=True, stop=True))

                                    @blk.vector
                                    def _(v, hp=hp):
                                        c_mm.wait(v)
                                        c_ve.inc(v.tensor_mul(exp_s[:, hp, :], oacc[:, hp, :],
                                                              pb_T[0:65, hp, :]))

                                @blk.sync
                                def _(s, qc=qc, pr=pr):
                                    c_ve.wait(s)
                                    c_exd.inc(s.dma_start(out=cin[qc, :, 2 * pr:2 * pr + 2, :],
                                                          in_=exp_s[:]), 16)

                        @blk.gpsimd
                        def _(g):
                            g.wait_ge(c_exd.sem, c_exd.n)
                            c_cc.inc(g.collective_compute(
                                "AllGather", ALU.bypass,
                                replica_groups=[[0, 1, 2, 3, 4, 5, 6, 7]],
                                ins=[cin], outs=[cout]))

            phd = ExitStack()
            with phd:
                def sbd(name, shape, dt):
                    return phd.enter_context(nc.sbuf_tensor(name, shape, dt))

                wout_s = sbd("wout_s", [128, 8, D], BF16)
                oall = sbd("oall", [128, 8, TOK2], BF16)
                x0t = sbd("x0t", [128, 4, 512], F32)
                x1t = sbd("x1t", [128, 4, 512], F32)

                @blk.gpsimd
                def _(g):
                    g.wait_ge(c_cc.sem, 1)
                    c_gld.inc(g.dma_start(out=wout_s[:], in_=w_out.rearrange("(c p) m -> p c m", p=128)), 16)
                    src_rows = cout.rearrange("r u f h q -> (r u f h) q")
                    for ch in range(8):
                        for b in range(4):
                            c_imp.inc(g.indirect_dma_start(
                                oall[:, ch, b * 512:(b + 1) * 512], None,
                                src_rows,
                                IndirectOffsetOnAxis(impi_s[:, ch, b:b + 1], 0)), 16)
                n_imp = c_imp.n
                if DBG:
                    @blk.sync
                    def _(s, n_imp=n_imp):
                        s.wait_ge(c_imp.sem, n_imp)
                        c_st.inc(s.dma_start(out=dbg_oall, in_=oall[:]), 16)

                with nc.psum_tensor("pd", [128, 4, 512], F32) as pd:
                    for mck in range(8):
                        @blk.sync
                        def _(s, mck=mck, first=(mck == 0)):
                            if first:
                                c_s3.wait(s)
                                c_av.wait(s)
                                c_exp.wait(s)
                                c_exd.wait(s)
                            c_ve.wait(s)
                            c_x0.inc(s.dma_start(
                                out=x0t[:],
                                in_=x_res_fm[mck * 128:(mck + 1) * 128, :]
                                    .rearrange("p (a t) -> p a t", t=512)), 16)

                        @blk.tensor
                        def _(t, mck=mck, n_imp=n_imp):
                            if mck == 0:
                                t.wait_ge(c_imp.sem, n_imp)
                                c_gld.wait(t)
                            else:
                                c_se.wait(t)
                            for tt in range(4):
                                for och in range(8):
                                    last = t.matmul(pd[:, tt, :],
                                                    wout_s[:, och, mck * 128:(mck + 1) * 128],
                                                    oall[:, och, tt * 512:(tt + 1) * 512],
                                                    start=(och == 0), stop=(och == 7))
                                c_mm.inc(last)

                        @blk.scalar
                        def _(sc, mck=mck):
                            c_mm.wait(sc)
                            c_se.inc(sc.activation(x1t[:].rearrange("p a t -> p (a t)"),
                                                   pd[:].rearrange("p a t -> p (a t)"), AF.Identity,
                                                   bias=obias_s[:, mck:mck + 1], scale=C_OUT))

                        @blk.vector
                        def _(v, mck=mck):
                            c_se.wait(v)
                            c_x0.wait(v)
                            c_ve.inc(v.tensor_add(x1t[:], x1t[:], x0t[:]))

                        @blk.sync
                        def _(s, mck=mck):
                            c_ve.wait(s)
                            c_st.inc(s.dma_start(
                                out=s_x1[mck * 128:(mck + 1) * 128, :]
                                    .rearrange("p (a t) -> p a t", t=512),
                                in_=x1t[:]), 16)


            def emit_ffn(s_in, s_out, wgT_d, wuT_d, wdT_d, wqu_scr, wqd_scr, tag):
                phe = ExitStack()
                with phe:
                    def sbe(name, shape, dt):
                        return phe.enter_context(nc.sbuf_tensor(name + tag, shape, dt))

                    wqg = sbe("wqg", [128, 8, INNER], BF16)
                    swcol = sbe("swcol", [128, 3], F32)
                    swinv = sbe("swinv", [128, 3], F32)
                    wq_done = []
                    with (
                        nc.sbuf_tensor("wtile" + tag, [128, 2, INNER], F32) as wtile,
                        nc.sbuf_tensor("wacc" + tag, [128, 1], F32) as wacc,
                        nc.sbuf_tensor("wrow" + tag, [128, 1], F32) as wrow,
                        nc.sbuf_tensor("wq_st" + tag, [128, INNER], BF16) as wq_st,
                        nc.psum_tensor("pw" + tag, [128, 512], F32) as pw,
                    ):
                        for mi, (wsrc, nch, fd) in enumerate(
                                [(wgT_d, 8, INNER), (wuT_d, 8, INNER), (wdT_d, 32, D)]):
                            wv2 = wsrc.rearrange("(c p) m -> p c m", p=128)
                            for i in range(nch):
                                cw = c_ws0 if i % 2 == 0 else c_ws1

                                @blk.sync
                                def _(s, i=i, cw=cw, fd=fd):
                                    c_ve.wait(s)
                                    cw.inc(s.dma_start(out=wtile[:, i % 2, 0:fd], in_=wv2[:, i, :]), 16)

                                @blk.vector
                                def _(v, i=i, cw=cw, fd=fd):
                                    cw.wait(v)
                                    c_ve.inc(v.reduce_sum(wrow[:], wtile[:, i % 2, 0:fd],
                                                          axis=AX.X, apply_absolute_value=True))
                                    v.wait_ge(c_ve.sem, c_ve.n)
                                    if i == 0:
                                        c_ve.inc(v.tensor_copy(wacc[:], wrow[:]))
                                    else:
                                        c_ve.inc(v.tensor_add(wacc[:], wacc[:], wrow[:]))

                            @blk.tensor
                            def _(t):
                                c_ve.wait(t)
                                c_mm.inc(t.matmul(pw[0:1, 0:1], ones_col_s[:], wacc[:],
                                                  start=True, stop=True))

                            @blk.vector
                            def _(v, mi=mi, nch=nch, fd=fd):
                                c_mm.wait(v)
                                c_ve.inc(v.tensor_scalar(wrow[0:1, :], pw[0:1, 0:1],
                                                         1.0 / (nch * fd * 128), 1e-5,
                                                         ALU.mult, ALU.max))
                                v.wait_ge(c_ve.sem, c_ve.n)
                                c_ve.inc(v.reciprocal(wacc[0:1, :], wrow[0:1, :]))

                            @blk.tensor
                            def _(t, mi=mi):
                                c_ve.wait(t)
                                t.matmul(pw[:, 1:2], ones_row_s[:], wacc[0:1, :], start=True, stop=True)
                                c_mm.inc(t.matmul(pw[:, 2:3], ones_row_s[:], wrow[0:1, :], start=True, stop=True))

                            @blk.vector
                            def _(v, mi=mi):
                                c_mm.wait(v)
                                c_ve.inc(v.tensor_copy(swcol[:, mi:mi + 1], pw[:, 1:2]))
                                c_ve.inc(v.tensor_copy(swinv[:, mi:mi + 1], pw[:, 2:3]))
                            sw_ready = c_ve.n

                            for i in range(nch):
                                cw = c_ws0 if i % 2 == 0 else c_ws1

                                @blk.sync
                                def _(s, i=i, cw=cw, fd=fd):
                                    c_ve.wait(s)
                                    c_sq.wait(s)
                                    cw.inc(s.dma_start(out=wtile[:, i % 2, 0:fd], in_=wv2[:, i, :]), 16)

                                @blk.vector
                                def _(v, mi=mi, i=i, cw=cw, fd=fd, sw_ready=sw_ready):
                                    cw.wait(v)
                                    v.wait_ge(c_ve.sem, sw_ready)
                                    c_sq.wait(v)
                                    if i == 0:
                                        v.wait_ge(c_ve.sem, c_ve.n)
                                    dst = wqg[:, i, :] if mi == 0 else wq_st[:, 0:fd]
                                    c_ve.inc(v.tensor_scalar(wtile[:, i % 2, 0:fd], wtile[:, i % 2, 0:fd],
                                                             swcol[:, mi:mi + 1], MAGIC, ALU.mult, ALU.add))
                                    c_ve.inc(v.tensor_scalar(wtile[:, i % 2, 0:fd], wtile[:, i % 2, 0:fd],
                                                             -MAGIC, 1.0, ALU.add, ALU.min))
                                    c_ve.inc(v.tensor_scalar(dst, wtile[:, i % 2, 0:fd],
                                                             -1.0, None, ALU.max))

                                if mi == 1:
                                    @blk.sync
                                    def _(s, i=i):
                                        c_ve.wait(s)
                                        c_sq.inc(s.dma_start(
                                            out=wqu_scr[:, i].rearrange("mc p c -> p mc c"),
                                            in_=wq_st[:].rearrange("p (mc c) -> p mc c", c=128)), 16)
                                elif mi == 2:
                                    @blk.sync
                                    def _(s, i=i):
                                        c_ve.wait(s)
                                        c_sq.inc(s.dma_start(
                                            out=wqd_scr.rearrange("(c p) m -> p c m", p=128)[:, i, :],
                                            in_=wq_st[:, 0:D]), 16)

                    with (
                        nc.sbuf_tensor("xln" + tag, [128, 8, 512], F32) as xln,
                        nc.sbuf_tensor("xqf" + tag, [128, 8, 512], BF16) as xqf,
                        nc.sbuf_tensor("guf" + tag, [128, 32, 512], BF16) as guf,
                        nc.sbuf_tensor("xst" + tag, [128, 2, 512], F32) as xst,
                        nc.sbuf_tensor("xsqb" + tag, [128, 512], F32) as xsqb,
                        nc.sbuf_tensor("gmax" + tag, [128, 512], F32) as gmax,
                        nc.sbuf_tensor("gmin" + tag, [128, 512], F32) as gmin,
                        nc.sbuf_tensor("foldw" + tag, [32, 12, 512], F32) as foldw,
                        nc.sbuf_tensor("rw1" + tag, [1, 512], F32) as rw1,
                        nc.sbuf_tensor("rw2" + tag, [1, 512], F32) as rw2,
                        nc.sbuf_tensor("rw3" + tag, [1, 512], F32) as rw3,
                        nc.sbuf_tensor("bct" + tag, [128, 4, 512], F32) as bct,
                        nc.sbuf_tensor("wut" + tag, [128, 2, 8, 128], BF16) as wut,
                        nc.sbuf_tensor("wdt" + tag, [128, 2, D], BF16) as wdt,
                        nc.sbuf_tensor("tmpa" + tag, [128, 512], F32) as tmpa,
                        nc.sbuf_tensor("tmpb" + tag, [128, 512], F32) as tmpb,
                        nc.sbuf_tensor("x1tl" + tag, [128, 512], F32) as x1tl,
                        nc.sbuf_tensor("x2tl" + tag, [128, 512], F32) as x2tl,
                        nc.psum_tensor("pe" + tag, [128, 8, 512], F32) as pe,
                    ):
                        xv2 = s_in.rearrange("(c p) t -> p c t", p=128)
                        ov2 = s_out.rearrange("(c p) t -> p c t", p=128)

                        def emit_fold(gate_fn=None):
                            @blk.gpsimd
                            def _(g):
                                c_ve.wait(g)
                                c_fd.inc(g.dma_start(out=foldw[0:32, 0:4, :], in_=gmax[:]), 16)
                            @blk.vector
                            def _(v):
                                c_fd.wait(v)
                                c_ve.inc(v.tensor_tensor(foldw[0:32, 0:2, :], foldw[0:32, 0:2, :],
                                                         foldw[0:32, 2:4, :], ALU.max))
                                c_ve.inc(v.tensor_tensor(foldw[0:32, 0:1, :], foldw[0:32, 0:1, :],
                                                         foldw[0:32, 1:2, :], ALU.max))
                            @blk.gpsimd
                            def _(g):
                                c_ve.wait(g)
                                c_fd.inc(g.dma_start(out=foldw[0:4, 4:12, :], in_=foldw[0:32, 0, :]), 16)
                            @blk.vector
                            def _(v):
                                c_fd.wait(v)
                                c_ve.inc(v.tensor_tensor(foldw[0:4, 4:8, :], foldw[0:4, 4:8, :],
                                                         foldw[0:4, 8:12, :], ALU.max))
                                c_ve.inc(v.tensor_tensor(foldw[0:4, 4:6, :], foldw[0:4, 4:6, :],
                                                         foldw[0:4, 6:8, :], ALU.max))
                                c_ve.inc(v.tensor_tensor(foldw[0:4, 4:5, :], foldw[0:4, 4:5, :],
                                                         foldw[0:4, 5:6, :], ALU.max))
                            @blk.gpsimd
                            def _(g):
                                c_ve.wait(g)
                                c_fd.inc(g.dma_start(out=foldw[0:1, 0:4, :], in_=foldw[0:4, 4, :]), 16)
                            @blk.vector
                            def _(v):
                                c_fd.wait(v)
                                c_ve.inc(v.tensor_tensor(foldw[0:1, 0:2, :], foldw[0:1, 0:2, :],
                                                         foldw[0:1, 2:4, :], ALU.max))
                                c_ve.inc(v.tensor_tensor(rw1[:], foldw[0:1, 0, :],
                                                         foldw[0:1, 1, :], ALU.max))

                        def emit_minmax_fold():
                            @blk.vector
                            def _(v):
                                c_ve.inc(v.tensor_scalar_mul(gmin[:], gmin[:], -1.0))
                                c_ve.inc(v.tensor_max(gmax[:], gmax[:], gmin[:]))
                            emit_fold()

                        def _do_quarter(qt):
                            tq = slice(qt * 512, (qt + 1) * 512)
                            for ch in range(8):
                                cx = c_xs0 if ch % 2 == 0 else c_xs1

                                @blk.sync
                                def _(s, ch=ch, cx=cx, tq=tq):
                                    c_mm.wait(s)
                                    c_ve.wait(s)
                                    cx.inc(s.dma_start(out=xst[:, ch % 2, :], in_=xv2[:, ch, tq]), 16)

                                @blk.vector
                                def _(v, ch=ch, cx=cx):
                                    cx.wait(v)
                                    if ch >= 1:
                                        c_mm.wait(v)
                                    c_ve.inc(v.tensor_mul(xsqb[:], xst[:, ch % 2, :], xst[:, ch % 2, :]))

                                @blk.tensor
                                def _(t, ch=ch):
                                    c_ve.wait(t)
                                    t.matmul(pe[0:1, 0, :], ones_col_s[:], xst[:, ch % 2, :],
                                             start=(ch == 0), stop=(ch == 7))
                                    c_mm.inc(t.matmul(pe[0:1, 1, :], ones_col_s[:], xsqb[:],
                                                      start=(ch == 0), stop=(ch == 7)))

                            @blk.vector
                            def _(v):
                                c_mm.wait(v)
                                c_ve.inc(v.tensor_scalar_mul(rw1[:], pe[0:1, 0, :], 1.0 / D))
                                c_ve.inc(v.tensor_mul(rw2[:], rw1[:], rw1[:]))
                                c_ve.inc(v.scalar_tensor_tensor(rw2[:], pe[0:1, 1, :], 1.0 / D,
                                                                rw2[:], ALU.mult, ALU.subtract))
                                c_ve.inc(v.tensor_scalar_add(rw2[:], rw2[:], LN_EPS))

                            @blk.scalar
                            def _(sc):
                                c_ve.wait(sc)
                                c_se.inc(sc.activation(rw2[:], rw2[:], AF.Sqrt))

                            @blk.vector
                            def _(v):
                                c_se.wait(v)
                                v.wait_ge(c_ve.sem, c_ve.n)
                                c_ve.inc(v.reciprocal(rw2[:], rw2[:]))
                                c_ve.inc(v.tensor_mul(rw3[:], rw1[:], rw2[:]))
                                c_ve.inc(v.tensor_scalar_mul(rw3[:], rw3[:], -1.0))

                            @blk.tensor
                            def _(t):
                                c_ve.wait(t)
                                t.matmul(pe[:, 2, :], ones_row_s[:], rw2[:], start=True, stop=True)
                                c_mm.inc(t.matmul(pe[:, 3, :], ones_row_s[:], rw3[:], start=True, stop=True))

                            for ch in range(8):
                                cx = c_xs0 if ch % 2 == 0 else c_xs1

                                @blk.sync
                                def _(s, ch=ch, cx=cx, tq=tq):
                                    c_ve.wait(s)
                                    cx.inc(s.dma_start(out=xst[:, ch % 2, :], in_=xv2[:, ch, tq]), 16)

                                @blk.vector
                                def _(v, ch=ch, cx=cx):
                                    cx.wait(v)
                                    if ch == 0:
                                        c_mm.wait(v)
                                    c_ve.inc(v.tensor_mul(xst[:, ch % 2, :], xst[:, ch % 2, :], pe[:, 2, :]))
                                    c_ve.inc(v.tensor_add(xln[:, ch, :], xst[:, ch % 2, :], pe[:, 3, :]))
                                    if ch == 0:
                                        c_ve.inc(v.tensor_copy(gmax[:], xln[:, ch, :]))
                                        c_ve.inc(v.tensor_copy(gmin[:], xln[:, ch, :]))
                                    else:
                                        c_ve.inc(v.tensor_max(gmax[:], gmax[:], xln[:, ch, :]))
                                        c_ve.inc(v.tensor_tensor(gmin[:], gmin[:], xln[:, ch, :], ALU.min))

                            emit_minmax_fold()

                            @blk.vector
                            def _(v):
                                c_ve.inc(v.tensor_scalar_max(rw1[:], rw1[:], 1e-5))
                                v.wait_ge(c_ve.sem, c_ve.n)
                                c_ve.inc(v.reciprocal(rw2[:], rw1[:]))
                                v.wait_ge(c_ve.sem, c_ve.n)
                                c_ve.inc(v.tensor_scalar_mul(rw2[:], rw2[:], 127.0))
                                c_ve.inc(v.tensor_scalar_mul(rw1[:], rw1[:], 1.0 / 127.0))

                            @blk.tensor
                            def _(t):
                                c_ve.wait(t)
                                t.matmul(pe[:, 2, :], ones_row_s[:], rw2[:], start=True, stop=True)
                                c_mm.inc(t.matmul(pe[:, 3, :], ones_row_s[:], rw1[:], start=True, stop=True))

                            @blk.vector
                            def _(v):
                                c_mm.wait(v)
                                c_ve.inc(v.tensor_copy(bct[:, 0, :], pe[:, 2, :]))
                                c_ve.inc(v.tensor_copy(bct[:, 1, :], pe[:, 3, :]))
                                for ch in range(8):
                                    c_ve.inc(v.tensor_mul(tmpa[:], xln[:, ch, :], bct[:, 0, :]))
                                    c_ve.inc(v.tensor_scalar(tmpa[:], tmpa[:], MAGIC, -MAGIC, ALU.add, ALU.add))
                                    c_ve.inc(v.tensor_copy(xqf[:, ch, :], tmpa[:]))
                            xq_ready = c_ve.n
                            if DBG and tag == "_f0" and qt == 0:
                                @blk.sync
                                def _(s):
                                    s.wait_ge(c_ve.sem, xq_ready)
                                    c_st.inc(s.dma_start(out=dbg_sw[:, 0:3], in_=swcol[:]), 16)
                                    c_st.inc(s.dma_start(out=dbg_sw[:, 3:6], in_=swinv[:]), 16)
                                    c_st.inc(s.dma_start(out=dbg_wqg, in_=wqg[:]), 16)
                                    c_st.inc(s.dma_start(out=dbg_xq, in_=xqf[:]), 16)
                                    c_st.inc(s.dma_start(out=dbg_rows[:, 0:1, :], in_=rw1[:]), 16)
                                    c_st.inc(s.dma_start(out=dbg_rows[:, 1:2, :], in_=rw2[:]), 16)

                            for m in range(32):
                                cwu = c_wu0 if m % 2 == 0 else c_wu1

                                @blk.sync
                                def _(s, m=m, cwu=cwu):
                                    if m < 2:
                                        c_sq.wait(s)
                                    c_mm.wait(s)
                                    cwu.inc(s.dma_start(out=wut[:, m % 2, :, :], in_=wqu_scr[m].rearrange("dc p c -> p dc c")), 16)

                                if DBG and tag == "_f0" and qt == 0 and m == 5:
                                    @blk.sync
                                    def _(s, cwu=cwu):
                                        cwu.wait(s)
                                        c_st.inc(s.dma_start(out=dbg_wut, in_=wut[:, 1, :, :]), 16)

                                @blk.tensor
                                def _(t, m=m, cwu=cwu, xq_ready=xq_ready):
                                    t.wait_ge(c_ve.sem, xq_ready)
                                    cwu.wait(t)
                                    if m > 0:
                                        c_ve.wait(t)
                                    for dc in range(8):
                                        t.matmul(pe[:, 5, :], wqg[:, dc, m * 128:(m + 1) * 128],
                                                 xqf[:, dc, :], start=(dc == 0), stop=(dc == 7))
                                    for dc in range(8):
                                        last = t.matmul(pe[:, 6, :], wut[:, m % 2, dc, :],
                                                        xqf[:, dc, :], start=(dc == 0), stop=(dc == 7))
                                    c_mm.inc(last)

                                @blk.vector
                                def _(v, m=m):
                                    c_mm.wait(v)
                                    c_ve.inc(v.tensor_mul(tmpa[:], pe[:, 5, :], bct[:, 1, :]))

                                @blk.scalar
                                def _(sc, m=m):
                                    c_ve.wait(sc)
                                    c_se.inc(sc.activation(tmpb[:], tmpa[:], AF.Silu,
                                                           scale=swinv[:, 0:1]))

                                @blk.vector
                                def _(v, m=m):
                                    c_se.wait(v)
                                    c_ve.inc(v.tensor_mul(tmpa[:], pe[:, 6, :], bct[:, 1, :]))
                                    c_ve.inc(v.scalar_tensor_tensor(tmpa[:], tmpa[:], swinv[:, 1:2],
                                                                    tmpb[:], ALU.mult, ALU.mult))
                                    c_ve.inc(v.tensor_copy(guf[:, m, :], tmpa[:]))
                                    if m == 0:
                                        c_ve.inc(v.tensor_copy(gmax[:], tmpa[:]))
                                        c_ve.inc(v.tensor_copy(gmin[:], tmpa[:]))
                                    else:
                                        c_ve.inc(v.tensor_max(gmax[:], gmax[:], tmpa[:]))
                                        c_ve.inc(v.tensor_tensor(gmin[:], gmin[:], tmpa[:], ALU.min))

                            emit_minmax_fold()

                            @blk.vector
                            def _(v):
                                c_ve.inc(v.tensor_scalar_max(rw1[:], rw1[:], 1e-5))
                                v.wait_ge(c_ve.sem, c_ve.n)
                                c_ve.inc(v.reciprocal(rw2[:], rw1[:]))
                                v.wait_ge(c_ve.sem, c_ve.n)
                                c_ve.inc(v.tensor_scalar_mul(rw2[:], rw2[:], 127.0))
                                c_ve.inc(v.tensor_scalar_mul(rw1[:], rw1[:], 1.0 / 127.0))

                            @blk.tensor
                            def _(t):
                                c_ve.wait(t)
                                t.matmul(pe[:, 2, :], ones_row_s[:], rw2[:], start=True, stop=True)
                                c_mm.inc(t.matmul(pe[:, 3, :], ones_row_s[:], rw1[:], start=True, stop=True))

                            @blk.vector
                            def _(v):
                                c_mm.wait(v)
                                c_ve.inc(v.tensor_copy(bct[:, 2, :], pe[:, 2, :]))
                                c_ve.inc(v.tensor_copy(bct[:, 3, :], pe[:, 3, :]))
                                for m in range(32):
                                    c_ve.inc(v.tensor_mul(tmpa[:], guf[:, m, :], bct[:, 2, :]))
                                    c_ve.inc(v.tensor_scalar(tmpa[:], tmpa[:], MAGIC, -MAGIC, ALU.add, ALU.add))
                                    c_ve.inc(v.tensor_copy(guf[:, m, :], tmpa[:]))
                            gu_ready = c_ve.n
                            if DBG and tag == "_f0" and qt == 0:
                                @blk.sync
                                def _(s):
                                    s.wait_ge(c_ve.sem, gu_ready)
                                    c_st.inc(s.dma_start(out=dbg_gu, in_=guf[:]), 16)

                            for ic in range(32):
                                cwd = c_wd0 if ic % 2 == 0 else c_wd1

                                @blk.sync
                                def _(s, ic=ic, cwd=cwd):
                                    if ic < 2:
                                        c_sq.wait(s)
                                        c_mm.wait(s)
                                    else:
                                        c_mm.wait(s)
                                    cwd.inc(s.dma_start(
                                        out=wdt[:, ic % 2, :],
                                        in_=wqd_scr.rearrange("(c p) m -> p c m", p=128)[:, ic, :]), 16)

                                @blk.tensor
                                def _(t, ic=ic, cwd=cwd, gu_ready=gu_ready):
                                    t.wait_ge(c_ve.sem, gu_ready)
                                    cwd.wait(t)
                                    if ic == 0:
                                        c_se.wait(t)
                                        c_ve.wait(t)
                                    for dm in range(8):
                                        last = t.matmul(pe[:, dm, :], wdt[:, ic % 2, dm * 128:(dm + 1) * 128],
                                                        guf[:, ic, :], start=(ic == 0), stop=(ic == 31))
                                    c_mm.inc(last)

                            for dm in range(8):
                                @blk.sync
                                def _(s, dm=dm, tq=tq):
                                    c_ve.wait(s)
                                    c_x1r.inc(s.dma_start(out=x1tl[:], in_=xv2[:, dm, tq]), 16)

                                @blk.scalar
                                def _(sc, dm=dm):
                                    c_mm.wait(sc)
                                    if dm > 0:
                                        c_ve.wait(sc)
                                    c_se.inc(sc.activation(tmpb[:], pe[:, dm, :], AF.Identity,
                                                           scale=swinv[:, 2:3]))

                                @blk.vector
                                def _(v, dm=dm):
                                    c_se.wait(v)
                                    c_x1r.wait(v)
                                    c_ve.inc(v.tensor_mul(tmpb[:], tmpb[:], bct[:, 3, :]))
                                    c_ve.inc(v.tensor_add(x2tl[:], tmpb[:], x1tl[:]))

                                @blk.sync
                                def _(s, dm=dm, tq=tq):
                                    c_ve.wait(s)
                                    c_st.inc(s.dma_start(out=ov2[:, dm, tq], in_=x2tl[:]), 16)

                        for qt in range(4):
                            _do_quarter(qt)

            emit_ffn(s_x1, s_x2, wgT_0, wuT_0, wdT_0, wqu_hbm_0, wqd_hbm, "_f0")


            phf = ExitStack()
            with phf:
                def sbf(name, shape, dt):
                    return phf.enter_context(nc.sbuf_tensor(name, shape, dt))

                hstat = sbf("hstat", [128, 2, 8, 128], BF16)
                qk4 = sbf("qk4", [128, 4, 3088], F32)
                loT = sbf("loT", [128, 8, TOK2], BF16)
                wqvT = sbf("wqvT", [128, 8, 3088], BF16)
                woT_s = sbf("woT_s", [128, 8, D], BF16)
                iden_s = sbf("iden_s", [128, 128], F32)
                sca = sbf("sca", [128, 1024], F32)
                scb = sbf("scb", [128, 1024], F32)
                u_s = sbf("u_s", [128, 4, 1024], F32)
                o_s = sbf("o_s", [128, 4, 1024], F32)
                gseg = sbf("gseg", [128, 16, 20], F32)
                xs2 = sbf("xs2", [128, 2, 512], F32)
                xq2 = sbf("xq2", [128, 512], F32)
                rf1 = sbf("rf1", [1, 512], F32)
                rf2 = sbf("rf2", [1, 512], F32)
                lngb_s = sbf("lngb_s", [128, 8, 2], F32)
                htile = sbf("htile", [128, 512], F32)
                ytile = sbf("ytile", [128, 512], F32)

                @blk.gpsimd
                def _(g):
                    c_st.wait(g)
                    c_gld.inc(g.dma_start(out=wqvT[:], in_=wqkvbT_d.rearrange("(c p) m -> p c m", p=128)), 16)
                    c_gld.inc(g.dma_start(out=woT_s[:], in_=woT_d.rearrange("(c p) m -> p c m", p=128)), 16)
                    c_gld.inc(g.dma_start(out=iden_s[:], in_=c_iden), 16)

                @blk.sync
                def _(s):
                    c_st.wait(s)
                    c_ld.inc(s.dma_start(out=lngb_s[:], in_=lngb_d), 16)

                @blk.vector
                def _(v):
                    c_gld.wait(v)
                    c_ve.inc(v.tensor_scalar_mul(woT_s[:], woT_s[:], float(HS ** -0.5)))

                with nc.psum_tensor("pf", [128, 8, 512], F32) as pf:
                    xv3 = s_x2.rearrange("(c p) t -> p c t", p=128)
                    hv = s_h.rearrange("(c p) t -> p c t", p=128)
                    for qt in range(4):
                        tq = slice(qt * 512, (qt + 1) * 512)
                        for ch in range(8):
                            cx = c_xs0 if ch % 2 == 0 else c_xs1

                            @blk.sync
                            def _(s, ch=ch, cx=cx, tq=tq):
                                c_mm.wait(s)
                                c_ve.wait(s)
                                cx.inc(s.dma_start(out=xs2[:, ch % 2, :], in_=xv3[:, ch, tq]), 16)

                            @blk.vector
                            def _(v, ch=ch, cx=cx):
                                cx.wait(v)
                                if ch >= 1:
                                    c_mm.wait(v)
                                c_ve.inc(v.tensor_mul(xq2[:], xs2[:, ch % 2, :], xs2[:, ch % 2, :]))

                            @blk.tensor
                            def _(t, ch=ch):
                                c_ve.wait(t)
                                t.matmul(pf[0:1, 0, :], ones_col_s[:], xs2[:, ch % 2, :],
                                         start=(ch == 0), stop=(ch == 7))
                                c_mm.inc(t.matmul(pf[0:1, 1, :], ones_col_s[:], xq2[:],
                                                  start=(ch == 0), stop=(ch == 7)))

                        @blk.vector
                        def _(v):
                            c_mm.wait(v)
                            c_ve.inc(v.tensor_scalar_mul(rf1[:], pf[0:1, 0, :], 1.0 / D))
                            c_ve.inc(v.tensor_mul(rf2[:], rf1[:], rf1[:]))
                            c_ve.inc(v.scalar_tensor_tensor(rf2[:], pf[0:1, 1, :], 1.0 / D,
                                                            rf2[:], ALU.mult, ALU.subtract))
                            c_ve.inc(v.tensor_scalar_add(rf2[:], rf2[:], LN_EPS))

                        @blk.scalar
                        def _(sc):
                            c_ve.wait(sc)
                            c_se.inc(sc.activation(rf2[:], rf2[:], AF.Sqrt))

                        @blk.vector
                        def _(v):
                            c_se.wait(v)
                            v.wait_ge(c_ve.sem, c_ve.n)
                            c_ve.inc(v.reciprocal(rf2[:], rf2[:]))
                            c_ve.inc(v.tensor_mul(rf1[:], rf1[:], rf2[:]))
                            c_ve.inc(v.tensor_scalar_mul(rf1[:], rf1[:], -1.0))

                        @blk.tensor
                        def _(t):
                            c_ve.wait(t)
                            t.matmul(pf[:, 2, :], ones_row_s[:], rf2[:], start=True, stop=True)
                            c_mm.inc(t.matmul(pf[:, 3, :], ones_row_s[:], rf1[:], start=True, stop=True))

                        for ch in range(8):
                            cx = c_xs0 if ch % 2 == 0 else c_xs1

                            @blk.sync
                            def _(s, ch=ch, cx=cx, tq=tq):
                                c_ve.wait(s)
                                cx.inc(s.dma_start(out=xs2[:, ch % 2, :], in_=xv3[:, ch, tq]), 16)

                            @blk.vector
                            def _(v, ch=ch, cx=cx, qt=qt, tq=tq):
                                cx.wait(v)
                                if ch == 0:
                                    c_mm.wait(v)
                                c_ve.inc(v.tensor_mul(xs2[:, ch % 2, :], xs2[:, ch % 2, :], pf[:, 2, :]))
                                c_ve.inc(v.tensor_add(htile[:], xs2[:, ch % 2, :], pf[:, 3, :]))

                            @blk.sync
                            def _(s, ch=ch, tq=tq):
                                c_ve.wait(s)
                                c_st.inc(s.dma_start(out=hv[:, ch, tq], in_=htile[:]), 16)

                with (
                    nc.psum_tensor("pq", [128, 7, 512], F32) as pq,
                    nc.psum_tensor("pt2", [128, 512], F32) as pt2,
                ):
                    hv4 = s_h.rearrange("(c p) t -> p c t", p=128)
                    for sub in range(4):
                        for b in range(B):
                            tt = b * 4 + sub
                            chs = c_hs0 if tt % 2 == 0 else c_hs1

                            @blk.gpsimd
                            def _(g, tt=tt, chs=chs):
                                c_mm.wait(g)
                                if tt < 2:
                                    c_st.wait(g)
                                chs.inc(g.dma_start(out=hstat[:, tt % 2, :, :],
                                                    in_=hv4[:, :, tt * 128:(tt + 1) * 128]), 16)

                            @blk.tensor
                            def _(t, tt=tt, b=b, sub=sub, chs=chs):
                                if sub == 0 and b == 0:
                                    c_gld.wait(t)
                                    c_ve.wait(t)
                                chs.wait(t)
                                c_ve.wait(t)
                                for nch in range(7):
                                    nsl = slice(nch * 512, min((nch + 1) * 512, 3088))
                                    w = nsl.stop - nsl.start
                                    for dch in range(8):
                                        last = t.matmul(pq[:, nch, 0:w], hstat[:, tt % 2, dch, :],
                                                        wqvT[:, dch, nsl], start=(dch == 0), stop=(dch == 7))
                                c_mm.inc(last)

                            @blk.vector
                            def _(v, b=b):
                                c_mm.wait(v)
                                c_ve.inc(v.tensor_copy(qk4[:, b, 0:3072],
                                                       pq[:, 0:6, :].rearrange("p a t -> p (a t)")))
                                c_ve.inc(v.tensor_copy(qk4[:, b, 3072:3088], pq[:, 6, 0:16]))

                        QS, KS, VS, BS = 0, 1024, 2048, 3072
                        seg = lambda b, base: qk4[:, b, base:base + 1024]
                        seg3 = lambda b, base: qk4[:, b, base:base + 1024].rearrange("p (h d) -> p h d", d=64)
                        bseg = lambda b: qk4[:, b, BS:BS + 16]

                        @blk.scalar
                        def _(sc):
                            c_ve.wait(sc)
                            for b in range(B):
                                c_se.inc(sc.activation(bseg(b), bseg(b), AF.Sigmoid))

                        for b in range(B):
                            for base in (QS, KS):
                                @blk.vector
                                def _(v, b=b, base=base):
                                    c_se.wait(v)
                                    c_ve.inc(v.tensor_scalar_min(sca[:], seg(b, base), 0.0))

                                @blk.scalar
                                def _(sc):
                                    c_ve.wait(sc)
                                    c_se.inc(sc.activation(sca[:], sca[:], AF.Exp))

                                @blk.vector
                                def _(v, b=b, base=base):
                                    c_se.wait(v)
                                    c_ve.inc(v.tensor_scalar_max(scb[:], seg(b, base), 0.0))
                                    c_ve.inc(v.tensor_add(sca[:], sca[:], scb[:]))
                                    c_ve.inc(v.reduce_sum(gseg[:, :, 0:1].rearrange("p h o -> p (h o)"),
                                                          sca[:].rearrange("p (h d) -> p h d", d=64), axis=AX.X))
                                    v.wait_ge(c_ve.sem, c_ve.n)
                                    c_ve.inc(v.reciprocal(gseg[:, :, 0], gseg[:, :, 0]))
                                    v.wait_ge(c_ve.sem, c_ve.n)
                                    c_ve.inc(v.tensor_tensor(
                                        seg(b, base).rearrange("p (h d) -> p h d", d=64),
                                        sca[:].rearrange("p (h d) -> p h d", d=64),
                                        gseg[:, :, 0:1].broadcast_to([128, 16, 64]), ALU.mult))

                        @blk.vector
                        def _(v):
                            gi = 0
                            for i in range(B):
                                for j in range(i + 1, B):
                                    c_ve.inc(v.tensor_mul(sca[:], seg(i, KS), seg(j, KS)))
                                    c_ve.inc(v.reduce_sum(gseg[:, :, 1 + gi:2 + gi].rearrange("p h o -> p (h o)"),
                                                          sca[:].rearrange("p (h d) -> p h d", d=64), axis=AX.X))
                                    gi += 1
                            ai = 0
                            for ss in range(B):
                                for j in range(ss + 1):
                                    c_ve.inc(v.tensor_mul(sca[:], seg(j, KS), seg(ss, QS)))
                                    c_ve.inc(v.reduce_sum(gseg[:, :, 8 + ai:9 + ai].rearrange("p h o -> p (h o)"),
                                                          sca[:].rearrange("p (h d) -> p h d", d=64), axis=AX.X))
                                    ai += 1
                        @blk.vector
                        def _(v):
                            bc = lambda col: gseg[:, :, col:col + 1].broadcast_to([128, 16, 64])
                            G = {}
                            gi = 1
                            for i in range(B):
                                for j in range(i + 1, B):
                                    G[(i, j)] = gi
                                    gi += 1
                            AQ = {}
                            ai = 8
                            for ss in range(B):
                                for j in range(ss + 1):
                                    AQ[(ss, j)] = ai
                                    ai += 1
                            bcb = lambda b: qk4[:, b, BS:BS + 16].rearrange("p (h o) -> p h o", o=1).broadcast_to([128, 16, 64])
                            u3 = lambda i: u_s[:, i, :].rearrange("p (h d) -> p h d", d=64)
                            o3 = lambda i: o_s[:, i, :].rearrange("p (h d) -> p h d", d=64)
                            v3 = lambda b: seg3(b, VS)
                            s3a = sca[:].rearrange("p (h d) -> p h d", d=64)
                            v.wait_ge(c_ve.sem, c_ve.n)
                            c_ve.inc(v.tensor_tensor(u3(0), v3(0), bcb(0), ALU.mult))
                            for ss in range(1, B):
                                c_ve.inc(v.tensor_tensor(s3a, u3(0), bc(G[(0, ss)]), ALU.mult))
                                for i in range(1, ss):
                                    c_ve.inc(v.tensor_tensor(scb[:].rearrange("p (h d) -> p h d", d=64),
                                                             u3(i), bc(G[(i, ss)]), ALU.mult))
                                    c_ve.inc(v.tensor_add(sca[:], sca[:], scb[:]))
                                c_ve.inc(v.tensor_sub(sca[:], seg(ss, VS), sca[:]))
                                c_ve.inc(v.tensor_tensor(u3(ss), s3a, bcb(ss), ALU.mult))
                            for ss in range(B):
                                c_ve.inc(v.tensor_tensor(o3(ss), u3(0), bc(AQ[(ss, 0)]), ALU.mult))
                                for j in range(1, ss + 1):
                                    c_ve.inc(v.tensor_tensor(scb[:].rearrange("p (h d) -> p h d", d=64),
                                                             u3(j), bc(AQ[(ss, j)]), ALU.mult))
                                    c_ve.inc(v.tensor_add(o_s[:, ss, :], o_s[:, ss, :], scb[:]))

                        for b in range(B):
                            for hp2 in range(8):
                                @blk.tensor
                                def _(t, b=b, hp2=hp2, sub=sub):
                                    c_ve.wait(t)
                                    if hp2 == 0 and b == 0:
                                        pass
                                    c_se.wait(t)
                                    c_mm.inc(t.transpose(pt2[:, 0:128], o_s[:, b, hp2 * 128:(hp2 + 1) * 128],
                                                         iden_s[:]))

                                @blk.vector
                                def _(v, b=b, hp2=hp2, sub=sub):
                                    c_mm.wait(v)
                                    tok = slice(b * 512 + sub * 128, b * 512 + sub * 128 + 128)
                                    c_ve.inc(v.tensor_copy(loT[:, hp2, tok], pt2[:, 0:128]))

                with nc.psum_tensor("pw2", [128, 4, 512], F32) as pw2:
                    yv = s_y.rearrange("(c p) t -> p c t", p=128)
                    for m in range(8):
                        @blk.tensor
                        def _(t, m=m):
                            c_ve.wait(t)
                            for tt2 in range(4):
                                for ec in range(8):
                                    last = t.matmul(pw2[:, tt2, :], woT_s[:, ec, m * 128:(m + 1) * 128],
                                                    loT[:, ec, tt2 * 512:(tt2 + 1) * 512],
                                                    start=(ec == 0), stop=(ec == 7))
                                c_mm.inc(last)

                        for tt2 in range(4):
                            @blk.sync
                            def _(s, m=m, tt2=tt2):
                                c_ve.wait(s)
                                c_x1r.inc(s.dma_start(out=htile[:], in_=hv[:, m, tt2 * 512:(tt2 + 1) * 512]), 16)

                            @blk.vector
                            def _(v, m=m, tt2=tt2):
                                c_x1r.wait(v)
                                c_mm.wait(v)
                                c_ve.inc(v.tensor_add(ytile[:], pw2[:, tt2, :], htile[:]))

                            @blk.sync
                            def _(s, m=m, tt2=tt2):
                                c_ve.wait(s)
                                c_st.inc(s.dma_start(out=yv[:, m, tt2 * 512:(tt2 + 1) * 512], in_=ytile[:]), 16)

                with nc.psum_tensor("pf2", [128, 4, 512], F32) as pf2:
                    yv = s_y.rearrange("(c p) t -> p c t", p=128)
                    xv3 = s_x2.rearrange("(c p) t -> p c t", p=128)
                    ov3 = s_x3.rearrange("(c p) t -> p c t", p=128)
                    for qt in range(4):
                        tq = slice(qt * 512, (qt + 1) * 512)
                        for ch in range(8):
                            cx = c_xs0 if ch % 2 == 0 else c_xs1

                            @blk.sync
                            def _(s, ch=ch, cx=cx, tq=tq):
                                c_mm.wait(s)
                                c_ve.wait(s)
                                c_st.wait(s)
                                cx.inc(s.dma_start(out=xs2[:, ch % 2, :], in_=yv[:, ch, tq]), 16)

                            @blk.vector
                            def _(v, ch=ch, cx=cx):
                                cx.wait(v)
                                if ch >= 1:
                                    c_mm.wait(v)
                                c_ve.inc(v.tensor_mul(xq2[:], xs2[:, ch % 2, :], xs2[:, ch % 2, :]))

                            @blk.tensor
                            def _(t, ch=ch):
                                c_ve.wait(t)
                                t.matmul(pf2[0:1, 0, :], ones_col_s[:], xs2[:, ch % 2, :],
                                         start=(ch == 0), stop=(ch == 7))
                                c_mm.inc(t.matmul(pf2[0:1, 1, :], ones_col_s[:], xq2[:],
                                                  start=(ch == 0), stop=(ch == 7)))

                        @blk.vector
                        def _(v):
                            c_mm.wait(v)
                            c_ve.inc(v.tensor_scalar_mul(rf1[:], pf2[0:1, 0, :], 1.0 / D))
                            c_ve.inc(v.tensor_mul(rf2[:], rf1[:], rf1[:]))
                            c_ve.inc(v.scalar_tensor_tensor(rf2[:], pf2[0:1, 1, :], 1.0 / D,
                                                            rf2[:], ALU.mult, ALU.subtract))
                            c_ve.inc(v.tensor_scalar_add(rf2[:], rf2[:], LN_EPS))

                        @blk.scalar
                        def _(sc):
                            c_ve.wait(sc)
                            c_se.inc(sc.activation(rf2[:], rf2[:], AF.Sqrt))

                        @blk.vector
                        def _(v):
                            c_se.wait(v)
                            v.wait_ge(c_ve.sem, c_ve.n)
                            c_ve.inc(v.reciprocal(rf2[:], rf2[:]))
                            c_ve.inc(v.tensor_mul(rf1[:], rf1[:], rf2[:]))
                            c_ve.inc(v.tensor_scalar_mul(rf1[:], rf1[:], -1.0))

                        @blk.tensor
                        def _(t):
                            c_ve.wait(t)
                            t.matmul(pf2[:, 2, :], ones_row_s[:], rf2[:], start=True, stop=True)
                            c_mm.inc(t.matmul(pf2[:, 3, :], ones_row_s[:], rf1[:], start=True, stop=True))

                        for ch in range(8):
                            cx = c_xs0 if ch % 2 == 0 else c_xs1

                            @blk.sync
                            def _(s, ch=ch, cx=cx, tq=tq):
                                c_ve.wait(s)
                                cx.inc(s.dma_start(out=xs2[:, ch % 2, :], in_=yv[:, ch, tq]), 16)
                                c_x1r.inc(s.dma_start(out=htile[:], in_=xv3[:, ch, tq]), 16)

                            @blk.vector
                            def _(v, ch=ch, cx=cx):
                                cx.wait(v)
                                c_x1r.wait(v)
                                if ch == 0:
                                    c_mm.wait(v)
                                c_ve.inc(v.tensor_mul(xs2[:, ch % 2, :], xs2[:, ch % 2, :], pf2[:, 2, :]))
                                c_ve.inc(v.tensor_add(ytile[:], xs2[:, ch % 2, :], pf2[:, 3, :]))
                                c_ve.inc(v.tensor_scalar_add(htile[:], htile[:], lngb_s[:, ch, 1:2]))
                                c_ve.inc(v.scalar_tensor_tensor(ytile[:], ytile[:], lngb_s[:, ch, 0:1],
                                                                htile[:], ALU.mult, ALU.add))

                            @blk.sync
                            def _(s, ch=ch, tq=tq):
                                c_ve.wait(s)
                                c_st.inc(s.dma_start(out=ov3[:, ch, tq], in_=ytile[:]), 16)

            emit_ffn(s_x3, out_fin, wgT_1, wuT_1, wdT_1, wqu_hbm_0, wqd_hbm, "_f1")

            @blk.sync
            def _(s):
                s.wait_ge(c_st.sem, c_st.n)

    return nc


_NC_CACHE = {}


def _get_nc():
    if "nc" not in _NC_CACHE:
        _NC_CACHE["nc"] = build_nc()
    return _NC_CACHE["nc"]


def kernel(**inputs):
    maps = _stage_inputs(inputs)
    nc = _get_nc()
    res = run_bass_kernel_spmd(nc, maps, core_ids=list(range(8)))
    out = np.zeros((B, S, D), np.float32)
    for c in range(8):
        u = c
        y = res.results[c]["out_fin"]
        out[:, 512 * u:512 * (u + 1), :] = y.reshape(D, B, 512).transpose(1, 2, 0)
    return out


# version 07
# speedup vs baseline: 1.3287x; 1.1886x over previous
import math
import os
import numpy as np
import ml_dtypes
import concourse.bass as bass
import concourse.mybir as mybir
from concourse.bass_core_rust import IndirectOffsetOnAxis
from concourse.bass_utils import run_bass_kernel_spmd

F32 = mybir.dt.float32
BF16 = mybir.dt.bfloat16
AF = mybir.ActivationFunctionType
ALU = mybir.AluOpType
AX = mybir.AxisListType

D = 1024
S = 4096
B = 4
NH = 16
HS = 64
FR = 256
NHC = 8
QC = 512
NQC = S // QC
NKT = S // 128
TOK2 = 2048
INNER = 4096

C_QKV = float((3.0 * D * HS * NH) ** (-0.25))
C_SQ2F = float(math.sqrt(2.0) / 16.0)
C_H = float(HS ** (-0.5))
C_OUT = float((D * D) ** (-0.25))
MAGIC = float(1.5 * 2 ** 23)
LN_EPS = 1e-5

_OMEGA = (10000.0 ** (-(np.arange(FR) // 2 * 2) / FR)).astype(np.float64)[::2]


def _g_of_lane(eps):
    l = np.arange(128)
    return l if eps == 1 else (l + 64) % 128


def _host_tables():
    t = {}
    k = np.arange(S, dtype=np.float64)
    for eps in (0, 1):
        g = _g_of_lane(eps)
        ang = k[None, :] * _OMEGA[g][:, None]
        sin, cos = np.sin(ang), np.cos(ang)
        in_k = ((np.arange(128) // 64) == eps)[:, None]
        t[f"sinT{eps}"] = sin.astype(ml_dtypes.bfloat16)
        t[f"cosT{eps}"] = cos.astype(ml_dtypes.bfloat16)
        t[f"khatB{eps}"] = np.where(in_k, cos, sin).astype(ml_dtypes.bfloat16)
        t[f"ceA{eps}"] = np.where(~in_k, cos, 0.0).astype(ml_dtypes.bfloat16)
    l = np.arange(128)
    ang2 = k[None, :] * _OMEGA[(l % 64) + 64][:, None]
    t["khatC2"] = np.sin(ang2).astype(ml_dtypes.bfloat16)
    m = np.zeros((4, 128, QC), np.float32)
    kk = np.arange(128)[:, None]
    qq = np.arange(QC)[None, :]
    for j in range(4):
        m[j] = (128 * j + kk <= qq)
    t["masks"] = np.ascontiguousarray(m.transpose(1, 0, 2)).astype(ml_dtypes.bfloat16)
    return t


def _stage_inputs(inp):
    x = np.ascontiguousarray(np.asarray(inp["L0_attn_input"], dtype=np.float32))
    qkv = np.asarray(inp["L0_attn_qkv"], dtype=np.float32)
    qb = np.asarray(inp["L0_attn_q_bias"], dtype=np.float32)
    pos = np.asarray(inp["L0_attn_positional"], dtype=np.float32)
    wout = np.asarray(inp["L0_attn_out_kernel"], dtype=np.float32)
    ob = np.asarray(inp["L0_attn_out_bias"], dtype=np.float32)

    Wqkvb = np.asarray(inp["L1_attn_W_qkvb"], dtype=np.float32)
    eperm = np.zeros(3088, np.int64)
    for n in range(16):
        j = np.arange(193)
        src_rows = n * 193 + j
        dst = np.where(j < 64, n * 64 + j,
              np.where(j < 128, 1024 + n * 64 + (j - 64),
              np.where(j < 192, 2048 + n * 64 + (j - 128), 3072 + n)))
        eperm[dst] = src_rows
    wqkvbT = np.ascontiguousarray(Wqkvb[eperm, :].T)
    woT = np.ascontiguousarray(np.asarray(inp["L1_attn_W_o"], dtype=np.float32).T)
    lng = np.asarray(inp["L1_attn_ln_g"], dtype=np.float32)
    lnb = np.asarray(inp["L1_attn_ln_b"], dtype=np.float32)
    lngb = np.stack([lng.reshape(8, 128).T, lnb.reshape(8, 128).T], axis=2)
    wgT0 = np.ascontiguousarray(np.asarray(inp["L0_ffn_w_gate"], dtype=np.float32).T)
    wuT0 = np.ascontiguousarray(np.asarray(inp["L0_ffn_w_up"], dtype=np.float32).T)
    wdT0 = np.ascontiguousarray(np.asarray(inp["L0_ffn_w_down"], dtype=np.float32).T)
    wgT1 = np.ascontiguousarray(np.asarray(inp["L1_ffn_w_gate"], dtype=np.float32).T)
    wuT1 = np.ascontiguousarray(np.asarray(inp["L1_ffn_w_up"], dtype=np.float32).T)
    wdT1 = np.ascontiguousarray(np.asarray(inp["L1_ffn_w_down"], dtype=np.float32).T)
    maps = []
    for c in range(8):
        b, hf = c // 2, c % 2
        m = {}
        m["x_b_fm"] = np.ascontiguousarray(x[b].T)
        u = c
        m["x_res_fm"] = np.ascontiguousarray(
            x[:, 512 * u:512 * (u + 1), :].transpose(2, 0, 1).reshape(D, TOK2))
        cols = []
        for p in range(4):
            cols += [qkv[:, 1, hf * 8 + 2 * p, :], qkv[:, 1, hf * 8 + 2 * p + 1, :]]
        for p in range(4):
            cols += [qkv[:, 0, hf * 8 + 2 * p, :], qkv[:, 0, hf * 8 + 2 * p + 1, :]]
        m["w_kq"] = np.ascontiguousarray(np.concatenate(cols, axis=1))
        m["w_v"] = np.ascontiguousarray(qkv[:, 2, hf * 8:hf * 8 + 8, :].reshape(D, 512))
        qbc = np.zeros((128, 4), np.float32)
        for p in range(4):
            qbc[0:64, p] = qb[hf * 8 + 2 * p]
            qbc[64:128, p] = qb[hf * 8 + 2 * p + 1]
        m["q_bias_col"] = qbc
        pa = np.zeros((128, NHC, 2, 128), np.float32)
        for h in range(NHC):
            eps = h & 1
            g = _g_of_lane(eps)
            n = hf * 8 + h
            for e in range(2):
                f = 2 * g + e
                pa[64 * eps:64 * eps + 64, h, e, :] = pos[f, n, :].T
        m["pos_arr"] = pa
        perm = np.zeros(D, np.int64)
        for ch in range(8):
            hf2, fc = ch // 4, ch % 4
            p = np.arange(128)
            n = hf2 * 8 + fc * 2 + p // 64
            perm[ch * 128 + p] = n * 64 + (p % 64)
        m["w_out"] = np.ascontiguousarray(wout[perm, :])
        m["out_bias_col"] = np.ascontiguousarray(ob.reshape(8, 128).T)
        m["wgT_0"] = wgT0
        m["wqkvbT"] = wqkvbT
        m["woT"] = woT
        m["lngb"] = np.ascontiguousarray(lngb)
        m["wuT_0"] = wuT0
        m["wdT_0"] = wdT0
        m["wgT_1"] = wgT1
        m["wuT_1"] = wuT1
        m["wdT_1"] = wdT1
        idx = np.zeros((128, 8, 4), np.int32)
        p = np.arange(128)
        for ch in range(8):
            hf2, fc = ch // 4, ch % 4
            for bb in range(4):
                r = 2 * bb + hf2
                idx[:, ch, bb] = (r * NQC * 65 * NHC + u * (65 * NHC)
                                  + (1 + p % 64) * NHC + (fc * 2 + p // 64))
        m["imp_idx"] = idx
        maps.append(m)
    return maps


class Ctr:
    def __init__(self, stack, nc, name):
        self.sem = stack.enter_context(nc.semaphore(name))
        self.n = 0

    def inc(self, instr, k=1):
        instr.then_inc(self.sem, k)
        self.n += k
        return instr

    def wait(self, eng, n=None):
        eng.wait_ge(self.sem, self.n if n is None else n)


def build_nc():
    nc = bass.Bass("TRN2", target_bir_lowering=False, detect_race_conditions=False)
    T = _host_tables()

    x_b_fm = nc.dram_tensor("x_b_fm", [D, S], F32, kind="ExternalInput").ap()
    x_res_fm = nc.dram_tensor("x_res_fm", [D, TOK2], F32, kind="ExternalInput").ap()
    w_kq = nc.dram_tensor("w_kq", [D, D], F32, kind="ExternalInput").ap()
    w_v = nc.dram_tensor("w_v", [D, 512], F32, kind="ExternalInput").ap()
    q_bias_col = nc.dram_tensor("q_bias_col", [128, 4], F32, kind="ExternalInput").ap()
    pos_arr = nc.dram_tensor("pos_arr", [128, NHC, 2, 128], F32, kind="ExternalInput").ap()
    w_out = nc.dram_tensor("w_out", [D, D], F32, kind="ExternalInput").ap()
    out_bias_col = nc.dram_tensor("out_bias_col", [128, 8], F32, kind="ExternalInput").ap()
    imp_idx = nc.dram_tensor("imp_idx", [128, 8, 4], mybir.dt.int32, kind="ExternalInput").ap()

    out_fin = nc.dram_tensor("out_fin", [D, TOK2], F32, kind="ExternalOutput").ap()
    _K2 = "ExternalOutput" if os.environ.get("KDBG2") == "1" else "Internal"
    s_x1 = nc.dram_tensor("s_x1", [D, TOK2], F32, kind=_K2).ap()
    wgT_0 = nc.dram_tensor("wgT_0", [D, INNER], F32, kind="ExternalInput").ap()
    wuT_0 = nc.dram_tensor("wuT_0", [D, INNER], F32, kind="ExternalInput").ap()
    wdT_0 = nc.dram_tensor("wdT_0", [INNER, D], F32, kind="ExternalInput").ap()
    wgT_1 = nc.dram_tensor("wgT_1", [D, INNER], F32, kind="ExternalInput").ap()
    wuT_1 = nc.dram_tensor("wuT_1", [D, INNER], F32, kind="ExternalInput").ap()
    wdT_1 = nc.dram_tensor("wdT_1", [INNER, D], F32, kind="ExternalInput").ap()
    wqd_hbm = nc.dram_tensor("wqd_hbm", [INNER, D], BF16, kind="Internal").ap()
    wqu_hbm_0 = nc.dram_tensor("wqu_hbm_0", [32, 128, 8, 128], BF16, kind="Internal").ap()
    s_x2 = nc.dram_tensor("s_x2", [D, TOK2], F32, kind=_K2).ap()
    s_x3 = nc.dram_tensor("s_x3", [D, TOK2], F32, kind=_K2).ap()
    s_h = nc.dram_tensor("s_h", [D, TOK2], F32, kind="Internal").ap()
    s_y = nc.dram_tensor("s_y", [D, TOK2], F32, kind="Internal").ap()
    wqkvbT_d = nc.dram_tensor("wqkvbT", [D, 3088], F32, kind="ExternalInput").ap()
    woT_d = nc.dram_tensor("woT", [D, D], F32, kind="ExternalInput").ap()
    lngb_d = nc.dram_tensor("lngb", [128, 8, 2], F32, kind="ExternalInput").ap()
    c_iden = nc.inline_tensor(np.eye(128, dtype=np.float32), name="c_iden").ap()
    DBG = os.environ.get("KDBG") == "1"
    if DBG:
        dbg_lnx = nc.dram_tensor("dbg_lnx", [128, 8, 512], BF16, kind="ExternalOutput").ap()
        dbg_qp = nc.dram_tensor("dbg_qp", [128, 4, S], BF16, kind="ExternalOutput").ap()
        dbg_khA = nc.dram_tensor("dbg_khA", [128, NHC, S], BF16, kind="ExternalOutput").ap()
        dbg_vh = nc.dram_tensor("dbg_vh", [128, NHC, NKT, 65], BF16, kind="ExternalOutput").ap()
        dbg_qhat = nc.dram_tensor("dbg_qhat", [128, NHC, 3, QC], BF16, kind="ExternalOutput").ap()
        dbg_oacc = nc.dram_tensor("dbg_oacc", [65, 2, QC], F32, kind="ExternalOutput").ap()
        dbg_oall = nc.dram_tensor("dbg_oall", [128, 8, TOK2], BF16, kind="ExternalOutput").ap()
        dbg_at = nc.dram_tensor("dbg_at", [128, 4, 2, QC], BF16, kind="ExternalOutput").ap()
        dbg_xq = nc.dram_tensor("dbg_xq", [128, 8, 512], BF16, kind="ExternalOutput").ap()
        dbg_gu = nc.dram_tensor("dbg_gu", [128, 32, 512], BF16, kind="ExternalOutput").ap()
        dbg_rows = nc.dram_tensor("dbg_rows", [1, 4, 512], F32, kind="ExternalOutput").ap()
        dbg_wqg = nc.dram_tensor("dbg_wqg", [128, 8, INNER], BF16, kind="ExternalOutput").ap()
        dbg_wut = nc.dram_tensor("dbg_wut", [128, 8, 128], BF16, kind="ExternalOutput").ap()
        dbg_sw = nc.dram_tensor("dbg_sw", [128, 6], F32, kind="ExternalOutput").ap()

    cin = nc.dram_tensor("cin", [NQC, 65, NHC, QC], BF16, kind="Internal").ap()
    cout = nc.dram_tensor("cout", [8, NQC, 65, NHC, QC], BF16, kind="Internal").ap()

    c_sin = [nc.inline_tensor(np.ascontiguousarray(T[f"sinT{e}"]), name=f"c_sin{e}").ap() for e in (0, 1)]
    c_cos = [nc.inline_tensor(np.ascontiguousarray(T[f"cosT{e}"]), name=f"c_cos{e}").ap() for e in (0, 1)]
    c_khB = [nc.inline_tensor(np.ascontiguousarray(T[f"khatB{e}"]), name=f"c_khB{e}").ap() for e in (0, 1)]
    c_khC2 = nc.inline_tensor(np.ascontiguousarray(T["khatC2"]), name="c_khC2").ap()
    c_ceA = [nc.inline_tensor(np.ascontiguousarray(T[f"ceA{e}"]), name=f"c_ceA{e}").ap() for e in (0, 1)]
    c_masks = nc.inline_tensor(np.ascontiguousarray(T["masks"]), name="c_masks").ap()
    c_ones_row = nc.inline_tensor(np.ones((1, 128), np.float32), name="c_ones_row").ap()
    c_ones_col = nc.inline_tensor(np.ones((128, 1), np.float32), name="c_ones_col").ap()

    from contextlib import ExitStack
    stk = ExitStack()
    with stk:
        def sb(name, shape, dt, st=stk):
            return st.enter_context(nc.sbuf_tensor(name, shape, dt))

        ones_row_s = sb("ones_row_s", [1, 128], F32)
        ones_col_s = sb("ones_col_s", [128, 1], F32)
        qbias_s = sb("qbias_s", [128, 4], F32)
        obias_s = sb("obias_s", [128, 8], F32)
        impi_s = sb("impi_s", [128, 8, 4], mybir.dt.int32)

        c_ld = Ctr(stk, nc, "c_ld")
        c_xt0 = Ctr(stk, nc, "c_xt0")
        c_xt1 = Ctr(stk, nc, "c_xt1")
        c_sc = Ctr(stk, nc, "c_sc")
        c_x0 = Ctr(stk, nc, "c_x0")
        c_gld = Ctr(stk, nc, "c_gld")
        c_mm = Ctr(stk, nc, "c_mm")
        c_ve = Ctr(stk, nc, "c_ve")
        c_se = Ctr(stk, nc, "c_se")
        c_s3 = Ctr(stk, nc, "c_s3")
        c_exp = Ctr(stk, nc, "c_exp")
        c_msk = Ctr(stk, nc, "c_msk")
        c_av = Ctr(stk, nc, "c_av")
        c_oev = Ctr(stk, nc, "c_oev")
        c_rot = Ctr(stk, nc, "c_rot")
        c_tmm = Ctr(stk, nc, "c_tmm")
        c_exd = Ctr(stk, nc, "c_exd")
        c_cc = Ctr(stk, nc, "c_cc")
        c_imp = Ctr(stk, nc, "c_imp")
        c_st = Ctr(stk, nc, "c_st")
        c_ws0 = Ctr(stk, nc, "c_ws0")
        c_ws1 = Ctr(stk, nc, "c_ws1")
        c_wu0 = Ctr(stk, nc, "c_wu0")
        c_wu1 = Ctr(stk, nc, "c_wu1")
        c_wd0 = Ctr(stk, nc, "c_wd0")
        c_wd1 = Ctr(stk, nc, "c_wd1")
        c_xs0 = Ctr(stk, nc, "c_xs0")
        c_xs1 = Ctr(stk, nc, "c_xs1")
        c_x1r = Ctr(stk, nc, "c_x1r")
        c_fd = Ctr(stk, nc, "c_fd")
        c_hs0 = Ctr(stk, nc, "c_hs0")
        c_hs1 = Ctr(stk, nc, "c_hs1")
        c_sq = Ctr(stk, nc, "c_sq")

        with nc.Block() as blk:
            atn = ExitStack()
            with atn:
                def sba(name, shape, dt):
                    return atn.enter_context(nc.sbuf_tensor(name, shape, dt))

                khatA = sba("khatA", [128, NHC, S], BF16)
                khB = sba("khB", [128, 2, S], BF16)
                khC = sba("khC", [128, S], BF16)
                vhat = sba("vhat", [128, NHC, NKT, 65], BF16)
                qp_all = sba("qp_all", [128, 4, S], BF16)
                posb = sba("posb", [128, NHC, 2, 128], BF16)
                masks_s = sba("masks_s", [128, 4, QC], BF16)

                @blk.sync
                def _(s):
                    c_ld.inc(s.dma_start(out=masks_s[:], in_=c_masks), 16)
                    c_ld.inc(s.dma_start(out=ones_row_s[:], in_=c_ones_row), 16)
                    c_ld.inc(s.dma_start(out=ones_col_s[:], in_=c_ones_col), 16)
                    c_ld.inc(s.dma_start(out=qbias_s[:], in_=q_bias_col), 16)
                    c_ld.inc(s.dma_start(out=obias_s[:], in_=out_bias_col), 16)
                    c_ld.inc(s.dma_start(out=impi_s[:], in_=imp_idx), 16)
                    for e in (0, 1):
                        c_ld.inc(s.dma_start(out=khB[:, e, :], in_=c_khB[e]), 16)
                    c_ld.inc(s.dma_start(out=khC[:], in_=c_khC2), 16)
                    for h in range(NHC):
                        e = h & 1
                        lo = 64 - 64 * e
                        c_ld.inc(s.dma_start(out=khatA[lo:lo + 64, h, :],
                                             in_=c_ceA[e][lo:lo + 64, :]), 16)
                n_const = c_ld.n

                @blk.gpsimd
                def _(g):
                    c_gld.inc(g.dma_start(out=posb[:], in_=pos_arr), 16)

                @blk.vector
                def _(v):
                    c_ve.inc(v.memset(vhat[:, :, :, 0:1], 1.0))
                    v.wait_ge(c_ld.sem, n_const)
                    c_ve.inc(v.tensor_scalar_mul(qbias_s[:], qbias_s[:], C_H))

                pha = ExitStack()
                with pha:
                    def sbp(name, shape, dt):
                        return pha.enter_context(nc.sbuf_tensor(name, shape, dt))

                    xt = sbp("xt", [128, 2, 512], F32)
                    xsq = sbp("xsq", [128, 512], F32)
                    lnx = sbp("lnx", [128, 8, 512], BF16)
                    rowa = sbp("rowa", [1, 512], F32)
                    rowb = sbp("rowb", [1, 512], F32)
                    rowc = sbp("rowc", [1, 512], F32)
                    wkq_s = sbp("wkq_s", [128, 8, D], BF16)
                    wv_s = sbp("wv_s", [128, 8, 512], BF16)

                    @blk.gpsimd
                    def _(g):
                        c_gld.inc(g.dma_start(out=wkq_s[:], in_=w_kq.rearrange("(c p) m -> p c m", p=128)), 16)
                        c_gld.inc(g.dma_start(out=wv_s[:], in_=w_v.rearrange("(c p) m -> p c m", p=128)), 16)

                    @blk.vector
                    def _(v):
                        c_gld.wait(v)
                        c_ve.inc(v.tensor_scalar_mul(posb[:], posb[:], C_SQ2F))

                    with (
                        nc.psum_tensor("pa_sum", [1, 2, 512], F32) as pa_sum,
                        nc.psum_tensor("pa_bc", [128, 2, 512], F32) as pa_bc,
                        nc.psum_tensor("pa_kq", [128, 512], F32) as pa_kq,
                        nc.psum_tensor("pa_v", [128, 512], F32) as pa_v,
                    ):
                        xv = x_b_fm.rearrange("(c p) t -> p c t", p=128)
                        for tg in range(8):
                            tsl = slice(tg * 512, (tg + 1) * 512)
                            for ch in range(8):
                                cx = c_xt0 if ch % 2 == 0 else c_xt1

                                @blk.sync
                                def _(s, ch=ch, tsl=tsl, cx=cx):
                                    c_mm.wait(s)
                                    c_ve.wait(s)
                                    cx.inc(s.dma_start(out=xt[:, ch % 2, :], in_=xv[:, ch, tsl]), 16)

                                @blk.vector
                                def _(v, ch=ch, cx=cx):
                                    cx.wait(v)
                                    if ch >= 1:
                                        c_mm.wait(v)
                                    c_ve.inc(v.tensor_mul(xsq[:], xt[:, ch % 2, :], xt[:, ch % 2, :]))

                                @blk.tensor
                                def _(t, ch=ch):
                                    c_ve.wait(t)
                                    t.matmul(pa_sum[:, 0, :], ones_col_s[:], xt[:, ch % 2, :],
                                             start=(ch == 0), stop=(ch == 7))
                                    c_mm.inc(t.matmul(pa_sum[:, 1, :], ones_col_s[:], xsq[:],
                                                      start=(ch == 0), stop=(ch == 7)))

                            @blk.vector
                            def _(v):
                                c_mm.wait(v)
                                c_ve.inc(v.tensor_scalar_mul(rowa[:], pa_sum[0:1, 0, :], 1.0 / D))
                                c_ve.inc(v.tensor_mul(rowc[:], rowa[:], rowa[:]))
                                c_ve.inc(v.scalar_tensor_tensor(rowb[:], pa_sum[0:1, 1, :], 1.0 / D,
                                                                rowc[:], ALU.mult, ALU.subtract))
                                c_ve.inc(v.tensor_scalar_add(rowb[:], rowb[:], LN_EPS))

                            @blk.scalar
                            def _(sc):
                                c_ve.wait(sc)
                                c_se.inc(sc.activation(rowb[:], rowb[:], AF.Sqrt))

                            @blk.vector
                            def _(v):
                                c_se.wait(v)
                                v.wait_ge(c_ve.sem, c_ve.n)
                                c_ve.inc(v.reciprocal(rowb[:], rowb[:]))
                                c_ve.inc(v.tensor_scalar_mul(rowb[:], rowb[:], C_QKV))
                                c_ve.inc(v.tensor_mul(rowc[:], rowa[:], rowb[:]))
                                c_ve.inc(v.tensor_scalar_mul(rowc[:], rowc[:], -1.0))

                            @blk.tensor
                            def _(t):
                                c_ve.wait(t)
                                t.matmul(pa_bc[:, 0, :], ones_row_s[:], rowb[:], start=True, stop=True)
                                c_mm.inc(t.matmul(pa_bc[:, 1, :], ones_row_s[:], rowc[:], start=True, stop=True))

                            for ch in range(8):
                                cx = c_xt0 if ch % 2 == 0 else c_xt1

                                @blk.sync
                                def _(s, ch=ch, tsl=tsl, cx=cx):
                                    c_ve.wait(s)
                                    cx.inc(s.dma_start(out=xt[:, ch % 2, :], in_=xv[:, ch, tsl]), 16)

                                @blk.vector
                                def _(v, ch=ch, cx=cx):
                                    cx.wait(v)
                                    if ch == 0:
                                        c_mm.wait(v)
                                    c_ve.inc(v.tensor_mul(xt[:, ch % 2, :], xt[:, ch % 2, :], pa_bc[:, 0, :]))
                                    c_ve.inc(v.tensor_add(lnx[:, ch, :], xt[:, ch % 2, :], pa_bc[:, 1, :]))
                            lnx_done = c_ve.n
                            if DBG and tg == 1:
                                @blk.sync
                                def _(s):
                                    s.wait_ge(c_ve.sem, lnx_done)
                                    c_st.inc(s.dma_start(out=dbg_lnx, in_=lnx[:]), 16)

                            for mck in range(8):
                                @blk.tensor
                                def _(t, mck=mck, lnx_done=lnx_done):
                                    t.wait_ge(c_ve.sem, lnx_done)
                                    c_ve.wait(t)
                                    c_se.wait(t)
                                    if mck == 0:
                                        c_gld.wait(t)
                                    for dch in range(8):
                                        last = t.matmul(pa_kq[:], wkq_s[:, dch, mck * 128:(mck + 1) * 128],
                                                        lnx[:, dch, :], start=(dch == 0), stop=(dch == 7))
                                    c_mm.inc(last)

                                if mck < 4:
                                    @blk.vector
                                    def _(v, mck=mck, tsl=tsl):
                                        c_mm.wait(v)
                                        c_ve.inc(v.tensor_copy(khatA[0:64, 2 * mck, tsl], pa_kq[0:64, :]))
                                        c_ve.inc(v.tensor_copy(khatA[64:128, 2 * mck + 1, tsl], pa_kq[64:128, :]))
                                else:
                                    @blk.scalar
                                    def _(sc, mck=mck, tsl=tsl):
                                        c_mm.wait(sc)
                                        c_se.inc(sc.activation(qp_all[:, mck - 4, tsl], pa_kq[:], AF.Identity,
                                                               bias=qbias_s[:, mck - 4:mck - 3], scale=C_H))

                            for tt in range(4):
                                @blk.tensor
                                def _(t, tt=tt):
                                    c_ve.wait(t)
                                    c_se.wait(t)
                                    for dch in range(8):
                                        last = t.matmul(pa_v[:], lnx[:, dch, tt * 128:(tt + 1) * 128],
                                                        wv_s[:, dch, :], start=(dch == 0), stop=(dch == 7))
                                    c_mm.inc(last)

                                @blk.vector
                                def _(v, tg=tg, tt=tt):
                                    c_mm.wait(v)
                                    kt = tg * 4 + tt
                                    for h in range(NHC):
                                        c_ve.inc(v.tensor_copy(vhat[:, h, kt, 1:65],
                                                               pa_v[:, h * 64:(h + 1) * 64]))

                phb = ExitStack()
                with phb:
                    def sbb(name, shape, dt):
                        return phb.enter_context(nc.sbuf_tensor(name, shape, dt))

                    qhat = sbb("qhat", [128, NHC, 3, QC], BF16)
                    oacc = sbb("oacc", [65, 2, QC], F32)
                    at_s = sbb("at_s", [128, 2, 2, QC], BF16)
                    exp_s = sbb("exp_s", [65, 2, QC], BF16)
                    uws = sbb("uws", [128, QC], F32)
                    uw2 = sbb("uw2", [128, QC], F32)
                    den = sbb("den", [1, QC], F32)
                    sc_qc = sbb("sc_qc", [128, 2, 2, QC], BF16)

                    @blk.vector
                    def _(v):
                        c_mm.wait(v)
                        c_se.wait(v)
                        c_ve.inc(v.memset(qhat[:], 0.0))

                    if DBG:
                        @blk.sync
                        def _(s):
                            c_mm.wait(s)
                            c_ve.wait(s)
                            c_se.wait(s)
                            c_st.inc(s.dma_start(out=dbg_qp, in_=qp_all[:]), 16)
                            c_st.inc(s.dma_start(out=dbg_khA, in_=khatA[:]), 16)
                            c_st.inc(s.dma_start(out=dbg_vh, in_=vhat[:]), 16)

                    with (
                        nc.psum_tensor("pb_T", [128, 2, QC], F32) as pb_T,
                        nc.psum_tensor("pb_S", [128, 2, 2, QC], F32) as pb_S,
                        nc.psum_tensor("pb_O", [65, 2, QC], F32) as pb_O,
                    ):
                        it = 0
                        nmask = 0
                        for qc in range(NQC):
                            qsl = slice(qc * QC, (qc + 1) * QC)
                            s3_before = c_s3.n

                            @blk.sync
                            def _(s, qsl=qsl, first=(qc == 0)):
                                if first:
                                    c_mm.wait(s)
                                    c_ve.wait(s)
                                    c_se.wait(s)
                                c_rot.wait(s)
                                for e in (0, 1):
                                    c_sc.inc(s.dma_start(out=sc_qc[:, e, 0, :], in_=c_sin[e][:, qsl]), 16)
                                    c_sc.inc(s.dma_start(out=sc_qc[:, e, 1, :], in_=c_cos[e][:, qsl]), 16)

                            for h in range(NHC):
                                e = h & 1

                                @blk.tensor
                                def _(t, qc=qc, h=h, e=e, qsl=qsl):
                                    if h == 0:
                                        c_ve.wait(t)
                                    if qc == 0 and h == 0:
                                        c_se.wait(t)
                                    c_rot.wait(t)
                                    t.matmul(pb_T[:, 0, :], posb[64 * e:64 * e + 64, h, 0, :],
                                             qp_all[64 * e:64 * e + 64, h // 2, qsl],
                                             start=True, stop=True)
                                    c_tmm.inc(t.matmul(pb_T[:, 1, :], posb[64 * e:64 * e + 64, h, 1, :],
                                                       qp_all[64 * e:64 * e + 64, h // 2, qsl],
                                                       start=True, stop=True))

                                @blk.vector
                                def _(v, h=h, e=e, s3_before=s3_before):
                                    c_tmm.wait(v)
                                    c_sc.wait(v)
                                    if h == 0:
                                        v.wait_ge(c_s3.sem, s3_before)
                                    ko = 64 * e
                                    no = 64 - ko
                                    c_ve.inc(v.tensor_mul(uws[:], pb_T[:, 0, :], sc_qc[:, e, 0, :]))
                                    c_ve.inc(v.tensor_mul(uw2[:], pb_T[:, 1, :], sc_qc[:, e, 1, :]))
                                    c_ve.inc(v.tensor_add(uws[:], uws[:], uw2[:]))
                                    c_ve.inc(v.tensor_copy(qhat[ko:ko + 64, h, 1, :], uws[ko:ko + 64, :]))
                                    c_ve.inc(v.tensor_copy(qhat[no:no + 64, h, 0, :], uws[no:no + 64, :]))
                                    c_ve.inc(v.tensor_mul(uws[:], pb_T[:, 1, :], sc_qc[:, e, 0, :]))
                                    c_ve.inc(v.tensor_mul(uw2[:], pb_T[:, 0, :], sc_qc[:, e, 1, :]))
                                    c_ve.inc(v.tensor_sub(uws[:], uws[:], uw2[:]))
                                    c_ve.inc(v.tensor_copy(qhat[ko:ko + 64, h, 2, :], uws[ko:ko + 64, :]))
                                    c_rot.inc(v.tensor_copy(qhat[no:no + 64, h, 1, :], uws[no:no + 64, :]))

                            @blk.vector
                            def _(v, qsl=qsl):
                                c_ve.inc(v.tensor_copy(qhat[0:64, 0:NHC:2, 0, :], qp_all[0:64, :, qsl]))
                                c_rot.inc(v.tensor_copy(qhat[64:128, 1:NHC:2, 0, :], qp_all[64:128, :, qsl]))
                            rot_done = c_rot.n
                            if DBG and qc == 0:
                                @blk.sync
                                def _(s, rot_done=rot_done):
                                    s.wait_ge(c_rot.sem, rot_done)
                                    c_st.inc(s.dma_start(out=dbg_qhat, in_=qhat[:]), 16)

                            nkt = 4 * (qc + 1)
                            ktgs = [range(g0, min(g0 + 8, nkt)) for g0 in range(0, nkt, 8)]
                            for pr in range(4):
                                for gi, ktg in enumerate(ktgs):
                                    first = True
                                    pend_av = [None]

                                    def mk_av(pr, kt, buf, it, diag, nmask, first, last_kt):
                                        def emit_av(t):
                                            if diag:
                                                t.wait_ge(c_msk.sem, nmask)
                                            else:
                                                t.wait_ge(c_exp.sem, it + 1)
                                            if first:
                                                t.wait_ge(c_oev.sem, c_oev.n)
                                            for hp in range(2):
                                                h = 2 * pr + hp
                                                last = t.matmul(pb_O[:, hp, :], vhat[:, h, kt, :],
                                                                at_s[:, buf, hp, :],
                                                                start=first, stop=last_kt)
                                            c_av.inc(last)
                                        return emit_av

                                    for kt in ktg:
                                        buf = it % 2
                                        diag = (kt >= 4 * qc)
                                        if diag:
                                            nmask += 1

                                        @blk.tensor
                                        def _(t, pr=pr, kt=kt, buf=buf, it=it, rot_done=rot_done,
                                              pend=pend_av[0]):
                                            t.wait_ge(c_rot.sem, rot_done)
                                            if it >= 2:
                                                t.wait_ge(c_exp.sem, it - 1)
                                            ksl = slice(kt * 128, (kt + 1) * 128)
                                            for hp in range(2):
                                                h = 2 * pr + hp
                                                e = h & 1
                                                t.matmul(pb_S[:, buf, hp, :], khatA[:, h, ksl],
                                                         qhat[:, h, 0, :], start=True, stop=False)
                                                t.matmul(pb_S[:, buf, hp, :], khB[:, e, ksl],
                                                         qhat[:, h, 1, :], start=False, stop=False)
                                                last = t.matmul(pb_S[:, buf, hp, :],
                                                                khC[64 * e:64 * e + 64, ksl],
                                                                qhat[64 * e:64 * e + 64, h, 2, :],
                                                                start=False, stop=True)
                                            c_s3.inc(last)
                                            if pend is not None:
                                                pend(t)

                                        @blk.scalar
                                        def _(sc, buf=buf, it=it):
                                            sc.wait_ge(c_s3.sem, it + 1)
                                            if it >= 2:
                                                sc.wait_ge(c_av.sem, it - 1)
                                            c_exp.inc(sc.activation(
                                                at_s[:, buf, :, :].rearrange("p a q -> p (a q)"),
                                                pb_S[:, buf, :, :].rearrange("p a q -> p (a q)"),
                                                AF.Exp))

                                        if diag:
                                            j = kt - 4 * qc

                                            @blk.vector
                                            def _(v, buf=buf, it=it, j=j):
                                                v.wait_ge(c_exp.sem, it + 1)
                                                c_msk.inc(v.tensor_mul(
                                                    at_s[:, buf, :, :], at_s[:, buf, :, :],
                                                    masks_s[:, j:j + 1, :].broadcast_to([128, 2, QC])))

                                        if DBG and qc == 0 and pr == 0:
                                            @blk.sync
                                            def _(s, kt=kt, buf=buf, nmask=nmask):
                                                s.wait_ge(c_msk.sem, nmask)
                                                c_st.inc(s.dma_start(out=dbg_at[:, kt, :, :],
                                                                     in_=at_s[:, buf, :, :]), 16)

                                        pend_av[0] = mk_av(pr, kt, buf, it, diag, nmask,
                                                           first, kt == ktg[-1])
                                        first = False
                                        it += 1

                                    @blk.tensor
                                    def _(t, pend=pend_av[0]):
                                        pend(t)

                                    @blk.vector
                                    def _(v, it=it, first_group=(gi == 0)):
                                        v.wait_ge(c_av.sem, it)
                                        if first_group:
                                            c_oev.inc(v.tensor_copy(oacc[:], pb_O[:]))
                                        else:
                                            c_oev.inc(v.tensor_add(oacc[:], oacc[:], pb_O[:]))

                                if DBG and qc == 0 and pr == 0:
                                    @blk.sync
                                    def _(s):
                                        c_oev.wait(s)
                                        c_st.inc(s.dma_start(out=dbg_oacc, in_=oacc[:]), 16)

                                for hp in range(2):
                                    @blk.vector
                                    def _(v, hp=hp):
                                        c_oev.wait(v)
                                        c_mm.wait(v)
                                        c_exd.wait(v)
                                        v.wait_ge(c_ve.sem, c_ve.n)
                                        c_ve.inc(v.reciprocal(den[:], oacc[0:1, hp, :]))

                                    @blk.tensor
                                    def _(t, hp=hp):
                                        c_ve.wait(t)
                                        c_mm.inc(t.matmul(pb_T[:, hp, :], ones_row_s[:], den[:],
                                                          start=True, stop=True))

                                    @blk.vector
                                    def _(v, hp=hp):
                                        c_mm.wait(v)
                                        c_ve.inc(v.tensor_mul(exp_s[:, hp, :], oacc[:, hp, :],
                                                              pb_T[0:65, hp, :]))

                                @blk.sync
                                def _(s, qc=qc, pr=pr):
                                    c_ve.wait(s)
                                    c_exd.inc(s.dma_start(out=cin[qc, :, 2 * pr:2 * pr + 2, :],
                                                          in_=exp_s[:]), 16)

                        @blk.gpsimd
                        def _(g):
                            g.wait_ge(c_exd.sem, c_exd.n)
                            c_cc.inc(g.collective_compute(
                                "AllGather", ALU.bypass,
                                replica_groups=[[0, 1, 2, 3, 4, 5, 6, 7]],
                                ins=[cin], outs=[cout]))

            phd = ExitStack()
            with phd:
                def sbd(name, shape, dt):
                    return phd.enter_context(nc.sbuf_tensor(name, shape, dt))

                wout_s = sbd("wout_s", [128, 8, D], BF16)
                oall = sbd("oall", [128, 8, TOK2], BF16)
                x0t = sbd("x0t", [128, 4, 512], F32)
                x1t = sbd("x1t", [128, 4, 512], F32)

                @blk.gpsimd
                def _(g):
                    g.wait_ge(c_cc.sem, 1)
                    c_gld.inc(g.dma_start(out=wout_s[:], in_=w_out.rearrange("(c p) m -> p c m", p=128)), 16)
                    src_rows = cout.rearrange("r u f h q -> (r u f h) q")
                    for ch in range(8):
                        for b in range(4):
                            c_imp.inc(g.indirect_dma_start(
                                oall[:, ch, b * 512:(b + 1) * 512], None,
                                src_rows,
                                IndirectOffsetOnAxis(impi_s[:, ch, b:b + 1], 0)), 16)
                n_imp = c_imp.n
                if DBG:
                    @blk.sync
                    def _(s, n_imp=n_imp):
                        s.wait_ge(c_imp.sem, n_imp)
                        c_st.inc(s.dma_start(out=dbg_oall, in_=oall[:]), 16)

                with nc.psum_tensor("pd", [128, 4, 512], F32) as pd:
                    for mck in range(8):
                        @blk.sync
                        def _(s, mck=mck, first=(mck == 0)):
                            if first:
                                c_s3.wait(s)
                                c_av.wait(s)
                                c_exp.wait(s)
                                c_exd.wait(s)
                            c_ve.wait(s)
                            c_x0.inc(s.dma_start(
                                out=x0t[:],
                                in_=x_res_fm[mck * 128:(mck + 1) * 128, :]
                                    .rearrange("p (a t) -> p a t", t=512)), 16)

                        @blk.tensor
                        def _(t, mck=mck, n_imp=n_imp):
                            if mck == 0:
                                t.wait_ge(c_imp.sem, n_imp)
                                c_gld.wait(t)
                            else:
                                c_se.wait(t)
                            for tt in range(4):
                                for och in range(8):
                                    last = t.matmul(pd[:, tt, :],
                                                    wout_s[:, och, mck * 128:(mck + 1) * 128],
                                                    oall[:, och, tt * 512:(tt + 1) * 512],
                                                    start=(och == 0), stop=(och == 7))
                                c_mm.inc(last)

                        @blk.scalar
                        def _(sc, mck=mck):
                            c_mm.wait(sc)
                            c_se.inc(sc.activation(x1t[:].rearrange("p a t -> p (a t)"),
                                                   pd[:].rearrange("p a t -> p (a t)"), AF.Identity,
                                                   bias=obias_s[:, mck:mck + 1], scale=C_OUT))

                        @blk.vector
                        def _(v, mck=mck):
                            c_se.wait(v)
                            c_x0.wait(v)
                            c_ve.inc(v.tensor_add(x1t[:], x1t[:], x0t[:]))

                        @blk.sync
                        def _(s, mck=mck):
                            c_ve.wait(s)
                            c_st.inc(s.dma_start(
                                out=s_x1[mck * 128:(mck + 1) * 128, :]
                                    .rearrange("p (a t) -> p a t", t=512),
                                in_=x1t[:]), 16)


            def emit_ffn(s_in, s_out, wgT_d, wuT_d, wdT_d, wqu_scr, wqd_scr, tag):
                phe = ExitStack()
                with phe:
                    def sbe(name, shape, dt):
                        return phe.enter_context(nc.sbuf_tensor(name + tag, shape, dt))

                    wqg = sbe("wqg", [128, 8, INNER], BF16)
                    swcol = sbe("swcol", [128, 3], F32)
                    swinv = sbe("swinv", [128, 3], F32)
                    wq_done = []
                    with (
                        nc.sbuf_tensor("wtile" + tag, [128, 2, INNER], F32) as wtile,
                        nc.sbuf_tensor("wacc" + tag, [128, 1], F32) as wacc,
                        nc.sbuf_tensor("wrow" + tag, [128, 1], F32) as wrow,
                        nc.sbuf_tensor("wq_st" + tag, [128, INNER], BF16) as wq_st,
                        nc.psum_tensor("pw" + tag, [128, 512], F32) as pw,
                    ):
                        for mi, (wsrc, nch, fd) in enumerate(
                                [(wgT_d, 8, INNER), (wuT_d, 8, INNER), (wdT_d, 32, D)]):
                            wv2 = wsrc.rearrange("(c p) m -> p c m", p=128)
                            for i in range(nch):
                                cw = c_ws0 if i % 2 == 0 else c_ws1

                                @blk.sync
                                def _(s, i=i, cw=cw, fd=fd):
                                    c_ve.wait(s)
                                    cw.inc(s.dma_start(out=wtile[:, i % 2, 0:fd], in_=wv2[:, i, :]), 16)

                                @blk.vector
                                def _(v, i=i, cw=cw, fd=fd):
                                    cw.wait(v)
                                    c_ve.inc(v.reduce_sum(wrow[:], wtile[:, i % 2, 0:fd],
                                                          axis=AX.X, apply_absolute_value=True))
                                    v.wait_ge(c_ve.sem, c_ve.n)
                                    if i == 0:
                                        c_ve.inc(v.tensor_copy(wacc[:], wrow[:]))
                                    else:
                                        c_ve.inc(v.tensor_add(wacc[:], wacc[:], wrow[:]))

                            @blk.tensor
                            def _(t):
                                c_ve.wait(t)
                                c_mm.inc(t.matmul(pw[0:1, 0:1], ones_col_s[:], wacc[:],
                                                  start=True, stop=True))

                            @blk.vector
                            def _(v, mi=mi, nch=nch, fd=fd):
                                c_mm.wait(v)
                                c_ve.inc(v.tensor_scalar(wrow[0:1, :], pw[0:1, 0:1],
                                                         1.0 / (nch * fd * 128), 1e-5,
                                                         ALU.mult, ALU.max))
                                v.wait_ge(c_ve.sem, c_ve.n)
                                c_ve.inc(v.reciprocal(wacc[0:1, :], wrow[0:1, :]))

                            @blk.tensor
                            def _(t, mi=mi):
                                c_ve.wait(t)
                                t.matmul(pw[:, 1:2], ones_row_s[:], wacc[0:1, :], start=True, stop=True)
                                c_mm.inc(t.matmul(pw[:, 2:3], ones_row_s[:], wrow[0:1, :], start=True, stop=True))

                            @blk.vector
                            def _(v, mi=mi):
                                c_mm.wait(v)
                                c_ve.inc(v.tensor_copy(swcol[:, mi:mi + 1], pw[:, 1:2]))
                                c_ve.inc(v.tensor_copy(swinv[:, mi:mi + 1], pw[:, 2:3]))
                            sw_ready = c_ve.n

                            for i in range(nch):
                                cw = c_ws0 if i % 2 == 0 else c_ws1

                                @blk.sync
                                def _(s, i=i, cw=cw, fd=fd):
                                    c_ve.wait(s)
                                    c_sq.wait(s)
                                    cw.inc(s.dma_start(out=wtile[:, i % 2, 0:fd], in_=wv2[:, i, :]), 16)

                                @blk.vector
                                def _(v, mi=mi, i=i, cw=cw, fd=fd, sw_ready=sw_ready):
                                    cw.wait(v)
                                    v.wait_ge(c_ve.sem, sw_ready)
                                    c_sq.wait(v)
                                    if i == 0:
                                        v.wait_ge(c_ve.sem, c_ve.n)
                                    dst = wqg[:, i, :] if mi == 0 else wq_st[:, 0:fd]
                                    c_ve.inc(v.tensor_scalar(wtile[:, i % 2, 0:fd], wtile[:, i % 2, 0:fd],
                                                             swcol[:, mi:mi + 1], MAGIC, ALU.mult, ALU.add))
                                    c_ve.inc(v.tensor_scalar(wtile[:, i % 2, 0:fd], wtile[:, i % 2, 0:fd],
                                                             -MAGIC, 1.0, ALU.add, ALU.min))
                                    c_ve.inc(v.tensor_scalar(dst, wtile[:, i % 2, 0:fd],
                                                             -1.0, None, ALU.max))

                                if mi == 1:
                                    @blk.sync
                                    def _(s, i=i):
                                        c_ve.wait(s)
                                        c_sq.inc(s.dma_start(
                                            out=wqu_scr[:, :, i, :].rearrange("mc p c -> p mc c"),
                                            in_=wq_st[:].rearrange("p (mc c) -> p mc c", c=128)), 16)
                                elif mi == 2:
                                    @blk.sync
                                    def _(s, i=i):
                                        c_ve.wait(s)
                                        c_sq.inc(s.dma_start(
                                            out=wqd_scr.rearrange("(c p) m -> p c m", p=128)[:, i, :],
                                            in_=wq_st[:, 0:D]), 16)

                    with (
                        nc.sbuf_tensor("xln" + tag, [128, 8, 512], F32) as xln,
                        nc.sbuf_tensor("xqf" + tag, [128, 8, 512], BF16) as xqf,
                        nc.sbuf_tensor("guf" + tag, [128, 32, 512], BF16) as guf,
                        nc.sbuf_tensor("xst" + tag, [128, 2, 512], F32) as xst,
                        nc.sbuf_tensor("xsqb" + tag, [128, 512], F32) as xsqb,
                        nc.sbuf_tensor("gmax" + tag, [128, 512], F32) as gmax,
                        nc.sbuf_tensor("gmin" + tag, [128, 512], F32) as gmin,
                        nc.sbuf_tensor("foldw" + tag, [32, 12, 512], F32) as foldw,
                        nc.sbuf_tensor("rw1" + tag, [1, 512], F32) as rw1,
                        nc.sbuf_tensor("rw2" + tag, [1, 512], F32) as rw2,
                        nc.sbuf_tensor("rw3" + tag, [1, 512], F32) as rw3,
                        nc.sbuf_tensor("bct" + tag, [128, 4, 512], F32) as bct,
                        nc.sbuf_tensor("wut" + tag, [128, 2, 8, 128], BF16) as wut,
                        nc.sbuf_tensor("wdt" + tag, [128, 2, D], BF16) as wdt,
                        nc.sbuf_tensor("tmpa" + tag, [128, 512], F32) as tmpa,
                        nc.sbuf_tensor("tmpb" + tag, [128, 512], F32) as tmpb,
                        nc.sbuf_tensor("x1tl" + tag, [128, 512], F32) as x1tl,
                        nc.sbuf_tensor("x2tl" + tag, [128, 512], F32) as x2tl,
                        nc.psum_tensor("pe" + tag, [128, 8, 512], F32) as pe,
                    ):
                        xv2 = s_in.rearrange("(c p) t -> p c t", p=128)
                        ov2 = s_out.rearrange("(c p) t -> p c t", p=128)
                        st = {"dn_prev": None, "ln_free": None, "gu_mm_prev": None}
                        st_in = c_st.n

                        def emit_fold(gate_fn=None):
                            @blk.gpsimd
                            def _(g):
                                c_ve.wait(g)
                                c_fd.inc(g.dma_start(out=foldw[0:32, 0:4, :], in_=gmax[:]), 16)
                            @blk.vector
                            def _(v):
                                c_fd.wait(v)
                                c_ve.inc(v.tensor_tensor(foldw[0:32, 0:2, :], foldw[0:32, 0:2, :],
                                                         foldw[0:32, 2:4, :], ALU.max))
                                c_ve.inc(v.tensor_tensor(foldw[0:32, 0:1, :], foldw[0:32, 0:1, :],
                                                         foldw[0:32, 1:2, :], ALU.max))
                            @blk.gpsimd
                            def _(g):
                                c_ve.wait(g)
                                c_fd.inc(g.dma_start(out=foldw[0:4, 4:12, :], in_=foldw[0:32, 0, :]), 16)
                            @blk.vector
                            def _(v):
                                c_fd.wait(v)
                                c_ve.inc(v.tensor_tensor(foldw[0:4, 4:8, :], foldw[0:4, 4:8, :],
                                                         foldw[0:4, 8:12, :], ALU.max))
                                c_ve.inc(v.tensor_tensor(foldw[0:4, 4:6, :], foldw[0:4, 4:6, :],
                                                         foldw[0:4, 6:8, :], ALU.max))
                                c_ve.inc(v.tensor_tensor(foldw[0:4, 4:5, :], foldw[0:4, 4:5, :],
                                                         foldw[0:4, 5:6, :], ALU.max))
                            @blk.gpsimd
                            def _(g):
                                c_ve.wait(g)
                                c_fd.inc(g.dma_start(out=foldw[0:1, 0:4, :], in_=foldw[0:4, 4, :]), 16)
                            @blk.vector
                            def _(v):
                                c_fd.wait(v)
                                c_ve.inc(v.tensor_tensor(foldw[0:1, 0:2, :], foldw[0:1, 0:2, :],
                                                         foldw[0:1, 2:4, :], ALU.max))
                                c_ve.inc(v.tensor_tensor(rw1[:], foldw[0:1, 0, :],
                                                         foldw[0:1, 1, :], ALU.max))

                        def emit_minmax_fold():
                            @blk.vector
                            def _(v):
                                c_ve.inc(v.tensor_scalar_mul(gmin[:], gmin[:], -1.0))
                                c_ve.inc(v.tensor_max(gmax[:], gmax[:], gmin[:]))
                            emit_fold()

                        def _do_quarter(qt):
                            tq = slice(qt * 512, (qt + 1) * 512)
                            for ch in range(8):
                                cx = c_xs0 if ch % 2 == 0 else c_xs1

                                @blk.sync
                                def _(s, ch=ch, cx=cx, tq=tq, qt=qt):
                                    if qt == 0 and ch == 0:
                                        s.wait_ge(c_st.sem, st_in)
                                    c_mm.wait(s)
                                    c_ve.wait(s)
                                    cx.inc(s.dma_start(out=xst[:, ch % 2, :], in_=xv2[:, ch, tq]), 16)

                                @blk.vector
                                def _(v, ch=ch, cx=cx):
                                    cx.wait(v)
                                    if ch >= 1:
                                        c_mm.wait(v)
                                    c_ve.inc(v.tensor_mul(xsqb[:], xst[:, ch % 2, :], xst[:, ch % 2, :]))

                                @blk.tensor
                                def _(t, ch=ch):
                                    c_ve.wait(t)
                                    t.matmul(pe[0:1, 0, :], ones_col_s[:], xst[:, ch % 2, :],
                                             start=(ch == 0), stop=(ch == 7))
                                    c_mm.inc(t.matmul(pe[0:1, 1, :], ones_col_s[:], xsqb[:],
                                                      start=(ch == 0), stop=(ch == 7)))

                            @blk.vector
                            def _(v):
                                c_mm.wait(v)
                                c_ve.inc(v.tensor_scalar_mul(rw1[:], pe[0:1, 0, :], 1.0 / D))
                                c_ve.inc(v.tensor_mul(rw2[:], rw1[:], rw1[:]))
                                c_ve.inc(v.scalar_tensor_tensor(rw2[:], pe[0:1, 1, :], 1.0 / D,
                                                                rw2[:], ALU.mult, ALU.subtract))
                                c_ve.inc(v.tensor_scalar_add(rw2[:], rw2[:], LN_EPS))

                            @blk.scalar
                            def _(sc):
                                c_ve.wait(sc)
                                c_se.inc(sc.activation(rw2[:], rw2[:], AF.Sqrt))

                            @blk.vector
                            def _(v):
                                c_se.wait(v)
                                v.wait_ge(c_ve.sem, c_ve.n)
                                c_ve.inc(v.reciprocal(rw2[:], rw2[:]))
                                c_ve.inc(v.tensor_mul(rw3[:], rw1[:], rw2[:]))
                                c_ve.inc(v.tensor_scalar_mul(rw3[:], rw3[:], -1.0))

                            @blk.tensor
                            def _(t):
                                c_ve.wait(t)
                                t.matmul(pe[:, 2, :], ones_row_s[:], rw2[:], start=True, stop=True)
                                c_mm.inc(t.matmul(pe[:, 3, :], ones_row_s[:], rw3[:], start=True, stop=True))

                            for ch in range(8):
                                cx = c_xs0 if ch % 2 == 0 else c_xs1

                                @blk.sync
                                def _(s, ch=ch, cx=cx, tq=tq):
                                    c_ve.wait(s)
                                    cx.inc(s.dma_start(out=xst[:, ch % 2, :], in_=xv2[:, ch, tq]), 16)

                                @blk.vector
                                def _(v, ch=ch, cx=cx):
                                    cx.wait(v)
                                    if ch == 0:
                                        c_mm.wait(v)
                                    c_ve.inc(v.tensor_mul(xst[:, ch % 2, :], xst[:, ch % 2, :], pe[:, 2, :]))
                                    c_ve.inc(v.tensor_add(xln[:, ch, :], xst[:, ch % 2, :], pe[:, 3, :]))
                                    if ch == 0:
                                        c_ve.inc(v.tensor_copy(gmax[:], xln[:, ch, :]))
                                        c_ve.inc(v.tensor_copy(gmin[:], xln[:, ch, :]))
                                    else:
                                        c_ve.inc(v.tensor_max(gmax[:], gmax[:], xln[:, ch, :]))
                                        c_ve.inc(v.tensor_tensor(gmin[:], gmin[:], xln[:, ch, :], ALU.min))
                            st["ln_free"] = c_ve.n

                            emit_minmax_fold()

                            @blk.vector
                            def _(v):
                                c_ve.inc(v.tensor_scalar_max(rw1[:], rw1[:], 1e-5))
                                v.wait_ge(c_ve.sem, c_ve.n)
                                c_ve.inc(v.reciprocal(rw2[:], rw1[:]))
                                v.wait_ge(c_ve.sem, c_ve.n)
                                c_ve.inc(v.tensor_scalar_mul(rw2[:], rw2[:], 127.0))
                                c_ve.inc(v.tensor_scalar_mul(rw1[:], rw1[:], 1.0 / 127.0))

                            @blk.tensor
                            def _(t):
                                c_ve.wait(t)
                                t.matmul(pe[:, 2, :], ones_row_s[:], rw2[:], start=True, stop=True)
                                c_mm.inc(t.matmul(pe[:, 3, :], ones_row_s[:], rw1[:], start=True, stop=True))

                            @blk.vector
                            def _(v):
                                c_mm.wait(v)
                                c_ve.inc(v.tensor_copy(bct[:, 0, :], pe[:, 2, :]))
                                c_ve.inc(v.tensor_copy(bct[:, 1, :], pe[:, 3, :]))
                                for ch in range(8):
                                    c_ve.inc(v.tensor_mul(tmpa[:], xln[:, ch, :], bct[:, 0, :]))
                                    c_ve.inc(v.tensor_scalar(tmpa[:], tmpa[:], MAGIC, -MAGIC, ALU.add, ALU.add))
                                    c_ve.inc(v.tensor_copy(xqf[:, ch, :], tmpa[:]))
                            xq_ready = c_ve.n
                            if DBG and tag == "_f0" and qt == 0:
                                @blk.sync
                                def _(s):
                                    s.wait_ge(c_ve.sem, xq_ready)
                                    c_st.inc(s.dma_start(out=dbg_sw[:, 0:3], in_=swcol[:]), 16)
                                    c_st.inc(s.dma_start(out=dbg_sw[:, 3:6], in_=swinv[:]), 16)
                                    c_st.inc(s.dma_start(out=dbg_wqg, in_=wqg[:]), 16)
                                    c_st.inc(s.dma_start(out=dbg_xq, in_=xqf[:]), 16)
                                    c_st.inc(s.dma_start(out=dbg_rows[:, 0:1, :], in_=rw1[:]), 16)
                                    c_st.inc(s.dma_start(out=dbg_rows[:, 1:2, :], in_=rw2[:]), 16)

                            pe_mm = {}
                            pe_free = {}
                            for m in range(32):
                                cwu = c_wu0 if m % 2 == 0 else c_wu1
                                gb = 4 + 2 * (m % 2)
                                ub = gb + 1

                                @blk.sync
                                def _(s, m=m, cwu=cwu):
                                    if m < 2:
                                        if st["gu_mm_prev"] is None:
                                            c_sq.wait(s)
                                            c_mm.wait(s)
                                        else:
                                            s.wait_ge(c_mm.sem, st["gu_mm_prev"][m])
                                    else:
                                        s.wait_ge(c_mm.sem, pe_mm[m - 2])
                                    cwu.inc(s.dma_start(out=wut[:, m % 2, :, :], in_=wqu_scr[m]), 16)

                                if DBG and tag == "_f0" and qt == 0 and m == 5:
                                    @blk.sync
                                    def _(s, cwu=cwu):
                                        cwu.wait(s)
                                        c_st.inc(s.dma_start(out=dbg_wut, in_=wut[:, 1, :, :]), 16)

                                @blk.tensor
                                def _(t, m=m, cwu=cwu, xq_ready=xq_ready, gb=gb, ub=ub):
                                    t.wait_ge(c_ve.sem, xq_ready)
                                    cwu.wait(t)
                                    if m >= 2:
                                        t.wait_ge(c_ve.sem, pe_free[m - 2])
                                    for dc in range(8):
                                        t.matmul(pe[:, gb, :], wqg[:, dc, m * 128:(m + 1) * 128],
                                                 xqf[:, dc, :], start=(dc == 0), stop=(dc == 7))
                                    for dc in range(8):
                                        last = t.matmul(pe[:, ub, :], wut[:, m % 2, dc, :],
                                                        xqf[:, dc, :], start=(dc == 0), stop=(dc == 7))
                                    c_mm.inc(last)
                                pe_mm[m] = c_mm.n

                                @blk.vector
                                def _(v, m=m, gb=gb):
                                    c_mm.wait(v)
                                    c_ve.inc(v.tensor_mul(tmpa[:], pe[:, gb, :], bct[:, 1, :]))

                                @blk.scalar
                                def _(sc, m=m):
                                    c_ve.wait(sc)
                                    c_se.inc(sc.activation(tmpb[:], tmpa[:], AF.Silu,
                                                           scale=swinv[:, 0:1]))

                                @blk.vector
                                def _(v, m=m, ub=ub):
                                    c_se.wait(v)
                                    c_ve.inc(v.tensor_mul(tmpa[:], pe[:, ub, :], bct[:, 1, :]))
                                    c_ve.inc(v.scalar_tensor_tensor(tmpa[:], tmpa[:], swinv[:, 1:2],
                                                                    tmpb[:], ALU.mult, ALU.mult))
                                    c_ve.inc(v.tensor_copy(guf[:, m, :], tmpa[:]))
                                    if m == 0:
                                        c_ve.inc(v.tensor_copy(gmax[:], tmpa[:]))
                                        c_ve.inc(v.tensor_copy(gmin[:], tmpa[:]))
                                    else:
                                        c_ve.inc(v.tensor_max(gmax[:], gmax[:], tmpa[:]))
                                        c_ve.inc(v.tensor_tensor(gmin[:], gmin[:], tmpa[:], ALU.min))
                                pe_free[m] = c_ve.n
                            st["gu_mm_prev"] = (pe_mm[30], pe_mm[31])

                            emit_minmax_fold()

                            @blk.vector
                            def _(v):
                                c_ve.inc(v.tensor_scalar_max(rw1[:], rw1[:], 1e-5))
                                v.wait_ge(c_ve.sem, c_ve.n)
                                c_ve.inc(v.reciprocal(rw2[:], rw1[:]))
                                v.wait_ge(c_ve.sem, c_ve.n)
                                c_ve.inc(v.tensor_scalar_mul(rw2[:], rw2[:], 127.0))
                                c_ve.inc(v.tensor_scalar_mul(rw1[:], rw1[:], 1.0 / 127.0))

                            @blk.tensor
                            def _(t):
                                c_ve.wait(t)
                                t.matmul(pe[:, 2, :], ones_row_s[:], rw2[:], start=True, stop=True)
                                c_mm.inc(t.matmul(pe[:, 3, :], ones_row_s[:], rw1[:], start=True, stop=True))

                            @blk.vector
                            def _(v):
                                c_mm.wait(v)
                                c_ve.inc(v.tensor_copy(bct[:, 2, :], pe[:, 2, :]))
                                c_ve.inc(v.tensor_copy(bct[:, 3, :], pe[:, 3, :]))
                                for m in range(32):
                                    c_ve.inc(v.tensor_mul(tmpa[:], guf[:, m, :], bct[:, 2, :]))
                                    c_ve.inc(v.tensor_scalar(tmpa[:], tmpa[:], MAGIC, -MAGIC, ALU.add, ALU.add))
                                    c_ve.inc(v.tensor_copy(guf[:, m, :], tmpa[:]))
                            gu_ready = c_ve.n
                            if DBG and tag == "_f0" and qt == 0:
                                @blk.sync
                                def _(s):
                                    s.wait_ge(c_ve.sem, gu_ready)
                                    c_st.inc(s.dma_start(out=dbg_gu, in_=guf[:]), 16)

                            dn_mm = {}
                            for ic in range(32):
                                cwd = c_wd0 if ic % 2 == 0 else c_wd1

                                @blk.sync
                                def _(s, ic=ic, cwd=cwd):
                                    if ic < 2:
                                        if st["dn_prev"] is None:
                                            c_sq.wait(s)
                                            c_mm.wait(s)
                                        else:
                                            s.wait_ge(c_mm.sem, st["dn_prev"][30 + ic])
                                    else:
                                        s.wait_ge(c_mm.sem, dn_mm[ic - 2])
                                    cwd.inc(s.dma_start(
                                        out=wdt[:, ic % 2, :],
                                        in_=wqd_scr.rearrange("(c p) m -> p c m", p=128)[:, ic, :]), 16)

                                @blk.tensor
                                def _(t, ic=ic, cwd=cwd, gu_ready=gu_ready):
                                    t.wait_ge(c_ve.sem, gu_ready)
                                    cwd.wait(t)
                                    for dm in range(8):
                                        last = t.matmul(pe[:, dm, :], wdt[:, ic % 2, dm * 128:(dm + 1) * 128],
                                                        guf[:, ic, :], start=(ic == 0), stop=(ic == 31))
                                    c_mm.inc(last)
                                dn_mm[ic] = c_mm.n
                            st["dn_prev"] = dn_mm

                            for dm in range(8):
                                @blk.sync
                                def _(s, dm=dm, tq=tq):
                                    c_ve.wait(s)
                                    c_x1r.inc(s.dma_start(out=x1tl[:], in_=xv2[:, dm, tq]), 16)

                                @blk.scalar
                                def _(sc, dm=dm):
                                    c_mm.wait(sc)
                                    if dm > 0:
                                        c_ve.wait(sc)
                                    c_se.inc(sc.activation(tmpb[:], pe[:, dm, :], AF.Identity,
                                                           scale=swinv[:, 2:3]))

                                @blk.vector
                                def _(v, dm=dm):
                                    c_se.wait(v)
                                    c_x1r.wait(v)
                                    c_ve.inc(v.tensor_mul(tmpb[:], tmpb[:], bct[:, 3, :]))
                                    c_ve.inc(v.tensor_add(x2tl[:], tmpb[:], x1tl[:]))

                                @blk.sync
                                def _(s, dm=dm, tq=tq):
                                    c_ve.wait(s)
                                    c_st.inc(s.dma_start(out=ov2[:, dm, tq], in_=x2tl[:]), 16)

                        for qt in range(4):
                            _do_quarter(qt)

            emit_ffn(s_x1, s_x2, wgT_0, wuT_0, wdT_0, wqu_hbm_0, wqd_hbm, "_f0")


            phf = ExitStack()
            with phf:
                def sbf(name, shape, dt):
                    return phf.enter_context(nc.sbuf_tensor(name, shape, dt))

                hstat = sbf("hstat", [128, 2, 8, 128], BF16)
                qk4 = sbf("qk4", [128, 4, 3088], F32)
                loT = sbf("loT", [128, 8, TOK2], BF16)
                wqvT = sbf("wqvT", [128, 8, 3088], BF16)
                woT_s = sbf("woT_s", [128, 8, D], BF16)
                iden_s = sbf("iden_s", [128, 128], F32)
                sca = sbf("sca", [128, 1024], F32)
                scb = sbf("scb", [128, 1024], F32)
                u_s = sbf("u_s", [128, 4, 1024], F32)
                o_s = sbf("o_s", [128, 4, 1024], F32)
                gseg = sbf("gseg", [128, 16, 20], F32)
                xs2 = sbf("xs2", [128, 2, 512], F32)
                xq2 = sbf("xq2", [128, 512], F32)
                rf1 = sbf("rf1", [1, 512], F32)
                rf2 = sbf("rf2", [1, 512], F32)
                lngb_s = sbf("lngb_s", [128, 8, 2], F32)
                htile = sbf("htile", [128, 512], F32)
                ytile = sbf("ytile", [128, 512], F32)

                @blk.gpsimd
                def _(g):
                    c_st.wait(g)
                    c_gld.inc(g.dma_start(out=wqvT[:], in_=wqkvbT_d.rearrange("(c p) m -> p c m", p=128)), 16)
                    c_gld.inc(g.dma_start(out=woT_s[:], in_=woT_d.rearrange("(c p) m -> p c m", p=128)), 16)
                    c_gld.inc(g.dma_start(out=iden_s[:], in_=c_iden), 16)

                @blk.sync
                def _(s):
                    c_st.wait(s)
                    c_ld.inc(s.dma_start(out=lngb_s[:], in_=lngb_d), 16)

                @blk.vector
                def _(v):
                    c_gld.wait(v)
                    c_ve.inc(v.tensor_scalar_mul(woT_s[:], woT_s[:], float(HS ** -0.5)))

                with nc.psum_tensor("pf", [128, 8, 512], F32) as pf:
                    xv3 = s_x2.rearrange("(c p) t -> p c t", p=128)
                    hv = s_h.rearrange("(c p) t -> p c t", p=128)
                    st_f1 = c_st.n
                    for qt in range(4):
                        tq = slice(qt * 512, (qt + 1) * 512)
                        for ch in range(8):
                            cx = c_xs0 if ch % 2 == 0 else c_xs1

                            @blk.sync
                            def _(s, ch=ch, cx=cx, tq=tq, qt=qt):
                                if qt == 0 and ch == 0:
                                    s.wait_ge(c_st.sem, st_f1)
                                c_mm.wait(s)
                                c_ve.wait(s)
                                cx.inc(s.dma_start(out=xs2[:, ch % 2, :], in_=xv3[:, ch, tq]), 16)

                            @blk.vector
                            def _(v, ch=ch, cx=cx):
                                cx.wait(v)
                                if ch >= 1:
                                    c_mm.wait(v)
                                c_ve.inc(v.tensor_mul(xq2[:], xs2[:, ch % 2, :], xs2[:, ch % 2, :]))

                            @blk.tensor
                            def _(t, ch=ch):
                                c_ve.wait(t)
                                t.matmul(pf[0:1, 0, :], ones_col_s[:], xs2[:, ch % 2, :],
                                         start=(ch == 0), stop=(ch == 7))
                                c_mm.inc(t.matmul(pf[0:1, 1, :], ones_col_s[:], xq2[:],
                                                  start=(ch == 0), stop=(ch == 7)))

                        @blk.vector
                        def _(v):
                            c_mm.wait(v)
                            c_ve.inc(v.tensor_scalar_mul(rf1[:], pf[0:1, 0, :], 1.0 / D))
                            c_ve.inc(v.tensor_mul(rf2[:], rf1[:], rf1[:]))
                            c_ve.inc(v.scalar_tensor_tensor(rf2[:], pf[0:1, 1, :], 1.0 / D,
                                                            rf2[:], ALU.mult, ALU.subtract))
                            c_ve.inc(v.tensor_scalar_add(rf2[:], rf2[:], LN_EPS))

                        @blk.scalar
                        def _(sc):
                            c_ve.wait(sc)
                            c_se.inc(sc.activation(rf2[:], rf2[:], AF.Sqrt))

                        @blk.vector
                        def _(v):
                            c_se.wait(v)
                            v.wait_ge(c_ve.sem, c_ve.n)
                            c_ve.inc(v.reciprocal(rf2[:], rf2[:]))
                            c_ve.inc(v.tensor_mul(rf1[:], rf1[:], rf2[:]))
                            c_ve.inc(v.tensor_scalar_mul(rf1[:], rf1[:], -1.0))

                        @blk.tensor
                        def _(t):
                            c_ve.wait(t)
                            t.matmul(pf[:, 2, :], ones_row_s[:], rf2[:], start=True, stop=True)
                            c_mm.inc(t.matmul(pf[:, 3, :], ones_row_s[:], rf1[:], start=True, stop=True))

                        for ch in range(8):
                            cx = c_xs0 if ch % 2 == 0 else c_xs1

                            @blk.sync
                            def _(s, ch=ch, cx=cx, tq=tq):
                                c_ve.wait(s)
                                cx.inc(s.dma_start(out=xs2[:, ch % 2, :], in_=xv3[:, ch, tq]), 16)

                            @blk.vector
                            def _(v, ch=ch, cx=cx, qt=qt, tq=tq):
                                cx.wait(v)
                                if ch == 0:
                                    c_mm.wait(v)
                                c_ve.inc(v.tensor_mul(xs2[:, ch % 2, :], xs2[:, ch % 2, :], pf[:, 2, :]))
                                c_ve.inc(v.tensor_add(htile[:], xs2[:, ch % 2, :], pf[:, 3, :]))

                            @blk.sync
                            def _(s, ch=ch, tq=tq):
                                c_ve.wait(s)
                                c_st.inc(s.dma_start(out=hv[:, ch, tq], in_=htile[:]), 16)

                with (
                    nc.psum_tensor("pq", [128, 7, 512], F32) as pq,
                    nc.psum_tensor("pt2", [128, 512], F32) as pt2,
                ):
                    hv4 = s_h.rearrange("(c p) t -> p c t", p=128)
                    for sub in range(4):
                        for b in range(B):
                            tt = b * 4 + sub
                            chs = c_hs0 if tt % 2 == 0 else c_hs1

                            @blk.gpsimd
                            def _(g, tt=tt, chs=chs):
                                c_mm.wait(g)
                                if tt < 2:
                                    c_st.wait(g)
                                chs.inc(g.dma_start(out=hstat[:, tt % 2, :, :],
                                                    in_=hv4[:, :, tt * 128:(tt + 1) * 128]), 16)

                            @blk.tensor
                            def _(t, tt=tt, b=b, sub=sub, chs=chs):
                                if sub == 0 and b == 0:
                                    c_gld.wait(t)
                                    c_ve.wait(t)
                                chs.wait(t)
                                c_ve.wait(t)
                                for nch in range(7):
                                    nsl = slice(nch * 512, min((nch + 1) * 512, 3088))
                                    w = nsl.stop - nsl.start
                                    for dch in range(8):
                                        last = t.matmul(pq[:, nch, 0:w], hstat[:, tt % 2, dch, :],
                                                        wqvT[:, dch, nsl], start=(dch == 0), stop=(dch == 7))
                                c_mm.inc(last)

                            @blk.vector
                            def _(v, b=b):
                                c_mm.wait(v)
                                c_ve.inc(v.tensor_copy(qk4[:, b, 0:3072],
                                                       pq[:, 0:6, :].rearrange("p a t -> p (a t)")))
                                c_ve.inc(v.tensor_copy(qk4[:, b, 3072:3088], pq[:, 6, 0:16]))

                        QS, KS, VS, BS = 0, 1024, 2048, 3072
                        seg = lambda b, base: qk4[:, b, base:base + 1024]
                        seg3 = lambda b, base: qk4[:, b, base:base + 1024].rearrange("p (h d) -> p h d", d=64)
                        bseg = lambda b: qk4[:, b, BS:BS + 16]

                        @blk.scalar
                        def _(sc):
                            c_ve.wait(sc)
                            for b in range(B):
                                c_se.inc(sc.activation(bseg(b), bseg(b), AF.Sigmoid))

                        for b in range(B):
                            for base in (QS, KS):
                                @blk.vector
                                def _(v, b=b, base=base):
                                    c_se.wait(v)
                                    c_ve.inc(v.tensor_scalar_min(sca[:], seg(b, base), 0.0))

                                @blk.scalar
                                def _(sc):
                                    c_ve.wait(sc)
                                    c_se.inc(sc.activation(sca[:], sca[:], AF.Exp))

                                @blk.vector
                                def _(v, b=b, base=base):
                                    c_se.wait(v)
                                    c_ve.inc(v.tensor_scalar_max(scb[:], seg(b, base), 0.0))
                                    c_ve.inc(v.tensor_add(sca[:], sca[:], scb[:]))
                                    c_ve.inc(v.reduce_sum(gseg[:, :, 0:1].rearrange("p h o -> p (h o)"),
                                                          sca[:].rearrange("p (h d) -> p h d", d=64), axis=AX.X))
                                    v.wait_ge(c_ve.sem, c_ve.n)
                                    c_ve.inc(v.reciprocal(gseg[:, :, 0], gseg[:, :, 0]))
                                    v.wait_ge(c_ve.sem, c_ve.n)
                                    c_ve.inc(v.tensor_tensor(
                                        seg(b, base).rearrange("p (h d) -> p h d", d=64),
                                        sca[:].rearrange("p (h d) -> p h d", d=64),
                                        gseg[:, :, 0:1].broadcast_to([128, 16, 64]), ALU.mult))

                        @blk.vector
                        def _(v):
                            gi = 0
                            for i in range(B):
                                for j in range(i + 1, B):
                                    c_ve.inc(v.tensor_mul(sca[:], seg(i, KS), seg(j, KS)))
                                    c_ve.inc(v.reduce_sum(gseg[:, :, 1 + gi:2 + gi].rearrange("p h o -> p (h o)"),
                                                          sca[:].rearrange("p (h d) -> p h d", d=64), axis=AX.X))
                                    gi += 1
                            ai = 0
                            for ss in range(B):
                                for j in range(ss + 1):
                                    c_ve.inc(v.tensor_mul(sca[:], seg(j, KS), seg(ss, QS)))
                                    c_ve.inc(v.reduce_sum(gseg[:, :, 8 + ai:9 + ai].rearrange("p h o -> p (h o)"),
                                                          sca[:].rearrange("p (h d) -> p h d", d=64), axis=AX.X))
                                    ai += 1
                        @blk.vector
                        def _(v):
                            bc = lambda col: gseg[:, :, col:col + 1].broadcast_to([128, 16, 64])
                            G = {}
                            gi = 1
                            for i in range(B):
                                for j in range(i + 1, B):
                                    G[(i, j)] = gi
                                    gi += 1
                            AQ = {}
                            ai = 8
                            for ss in range(B):
                                for j in range(ss + 1):
                                    AQ[(ss, j)] = ai
                                    ai += 1
                            bcb = lambda b: qk4[:, b, BS:BS + 16].rearrange("p (h o) -> p h o", o=1).broadcast_to([128, 16, 64])
                            u3 = lambda i: u_s[:, i, :].rearrange("p (h d) -> p h d", d=64)
                            o3 = lambda i: o_s[:, i, :].rearrange("p (h d) -> p h d", d=64)
                            v3 = lambda b: seg3(b, VS)
                            s3a = sca[:].rearrange("p (h d) -> p h d", d=64)
                            v.wait_ge(c_ve.sem, c_ve.n)
                            c_ve.inc(v.tensor_tensor(u3(0), v3(0), bcb(0), ALU.mult))
                            for ss in range(1, B):
                                c_ve.inc(v.tensor_tensor(s3a, u3(0), bc(G[(0, ss)]), ALU.mult))
                                for i in range(1, ss):
                                    c_ve.inc(v.tensor_tensor(scb[:].rearrange("p (h d) -> p h d", d=64),
                                                             u3(i), bc(G[(i, ss)]), ALU.mult))
                                    c_ve.inc(v.tensor_add(sca[:], sca[:], scb[:]))
                                c_ve.inc(v.tensor_sub(sca[:], seg(ss, VS), sca[:]))
                                c_ve.inc(v.tensor_tensor(u3(ss), s3a, bcb(ss), ALU.mult))
                            for ss in range(B):
                                c_ve.inc(v.tensor_tensor(o3(ss), u3(0), bc(AQ[(ss, 0)]), ALU.mult))
                                for j in range(1, ss + 1):
                                    c_ve.inc(v.tensor_tensor(scb[:].rearrange("p (h d) -> p h d", d=64),
                                                             u3(j), bc(AQ[(ss, j)]), ALU.mult))
                                    c_ve.inc(v.tensor_add(o_s[:, ss, :], o_s[:, ss, :], scb[:]))

                        for b in range(B):
                            for hp2 in range(8):
                                @blk.tensor
                                def _(t, b=b, hp2=hp2, sub=sub):
                                    c_ve.wait(t)
                                    if hp2 == 0 and b == 0:
                                        pass
                                    c_se.wait(t)
                                    c_mm.inc(t.transpose(pt2[:, 0:128], o_s[:, b, hp2 * 128:(hp2 + 1) * 128],
                                                         iden_s[:]))

                                @blk.vector
                                def _(v, b=b, hp2=hp2, sub=sub):
                                    c_mm.wait(v)
                                    tok = slice(b * 512 + sub * 128, b * 512 + sub * 128 + 128)
                                    c_ve.inc(v.tensor_copy(loT[:, hp2, tok], pt2[:, 0:128]))

                with nc.psum_tensor("pw2", [128, 4, 512], F32) as pw2:
                    yv = s_y.rearrange("(c p) t -> p c t", p=128)
                    st_f3 = c_st.n
                    for m in range(8):
                        @blk.tensor
                        def _(t, m=m):
                            c_ve.wait(t)
                            for tt2 in range(4):
                                for ec in range(8):
                                    last = t.matmul(pw2[:, tt2, :], woT_s[:, ec, m * 128:(m + 1) * 128],
                                                    loT[:, ec, tt2 * 512:(tt2 + 1) * 512],
                                                    start=(ec == 0), stop=(ec == 7))
                                c_mm.inc(last)

                        for tt2 in range(4):
                            @blk.sync
                            def _(s, m=m, tt2=tt2):
                                if m == 0 and tt2 == 0:
                                    s.wait_ge(c_st.sem, st_f3)
                                c_ve.wait(s)
                                c_x1r.inc(s.dma_start(out=htile[:], in_=hv[:, m, tt2 * 512:(tt2 + 1) * 512]), 16)

                            @blk.vector
                            def _(v, m=m, tt2=tt2):
                                c_x1r.wait(v)
                                c_mm.wait(v)
                                c_ve.inc(v.tensor_add(ytile[:], pw2[:, tt2, :], htile[:]))

                            @blk.sync
                            def _(s, m=m, tt2=tt2):
                                c_ve.wait(s)
                                c_st.inc(s.dma_start(out=yv[:, m, tt2 * 512:(tt2 + 1) * 512], in_=ytile[:]), 16)

                with nc.psum_tensor("pf2", [128, 4, 512], F32) as pf2:
                    yv = s_y.rearrange("(c p) t -> p c t", p=128)
                    xv3 = s_x2.rearrange("(c p) t -> p c t", p=128)
                    ov3 = s_x3.rearrange("(c p) t -> p c t", p=128)
                    for qt in range(4):
                        tq = slice(qt * 512, (qt + 1) * 512)
                        for ch in range(8):
                            cx = c_xs0 if ch % 2 == 0 else c_xs1

                            @blk.sync
                            def _(s, ch=ch, cx=cx, tq=tq):
                                c_mm.wait(s)
                                c_ve.wait(s)
                                c_st.wait(s)
                                cx.inc(s.dma_start(out=xs2[:, ch % 2, :], in_=yv[:, ch, tq]), 16)

                            @blk.vector
                            def _(v, ch=ch, cx=cx):
                                cx.wait(v)
                                if ch >= 1:
                                    c_mm.wait(v)
                                c_ve.inc(v.tensor_mul(xq2[:], xs2[:, ch % 2, :], xs2[:, ch % 2, :]))

                            @blk.tensor
                            def _(t, ch=ch):
                                c_ve.wait(t)
                                t.matmul(pf2[0:1, 0, :], ones_col_s[:], xs2[:, ch % 2, :],
                                         start=(ch == 0), stop=(ch == 7))
                                c_mm.inc(t.matmul(pf2[0:1, 1, :], ones_col_s[:], xq2[:],
                                                  start=(ch == 0), stop=(ch == 7)))

                        @blk.vector
                        def _(v):
                            c_mm.wait(v)
                            c_ve.inc(v.tensor_scalar_mul(rf1[:], pf2[0:1, 0, :], 1.0 / D))
                            c_ve.inc(v.tensor_mul(rf2[:], rf1[:], rf1[:]))
                            c_ve.inc(v.scalar_tensor_tensor(rf2[:], pf2[0:1, 1, :], 1.0 / D,
                                                            rf2[:], ALU.mult, ALU.subtract))
                            c_ve.inc(v.tensor_scalar_add(rf2[:], rf2[:], LN_EPS))

                        @blk.scalar
                        def _(sc):
                            c_ve.wait(sc)
                            c_se.inc(sc.activation(rf2[:], rf2[:], AF.Sqrt))

                        @blk.vector
                        def _(v):
                            c_se.wait(v)
                            v.wait_ge(c_ve.sem, c_ve.n)
                            c_ve.inc(v.reciprocal(rf2[:], rf2[:]))
                            c_ve.inc(v.tensor_mul(rf1[:], rf1[:], rf2[:]))
                            c_ve.inc(v.tensor_scalar_mul(rf1[:], rf1[:], -1.0))

                        @blk.tensor
                        def _(t):
                            c_ve.wait(t)
                            t.matmul(pf2[:, 2, :], ones_row_s[:], rf2[:], start=True, stop=True)
                            c_mm.inc(t.matmul(pf2[:, 3, :], ones_row_s[:], rf1[:], start=True, stop=True))

                        for ch in range(8):
                            cx = c_xs0 if ch % 2 == 0 else c_xs1

                            @blk.sync
                            def _(s, ch=ch, cx=cx, tq=tq):
                                c_ve.wait(s)
                                cx.inc(s.dma_start(out=xs2[:, ch % 2, :], in_=yv[:, ch, tq]), 16)
                                c_x1r.inc(s.dma_start(out=htile[:], in_=xv3[:, ch, tq]), 16)

                            @blk.vector
                            def _(v, ch=ch, cx=cx):
                                cx.wait(v)
                                c_x1r.wait(v)
                                if ch == 0:
                                    c_mm.wait(v)
                                c_ve.inc(v.tensor_mul(xs2[:, ch % 2, :], xs2[:, ch % 2, :], pf2[:, 2, :]))
                                c_ve.inc(v.tensor_add(ytile[:], xs2[:, ch % 2, :], pf2[:, 3, :]))
                                c_ve.inc(v.tensor_scalar_add(htile[:], htile[:], lngb_s[:, ch, 1:2]))
                                c_ve.inc(v.scalar_tensor_tensor(ytile[:], ytile[:], lngb_s[:, ch, 0:1],
                                                                htile[:], ALU.mult, ALU.add))

                            @blk.sync
                            def _(s, ch=ch, tq=tq):
                                c_ve.wait(s)
                                c_st.inc(s.dma_start(out=ov3[:, ch, tq], in_=ytile[:]), 16)

            emit_ffn(s_x3, out_fin, wgT_1, wuT_1, wdT_1, wqu_hbm_0, wqd_hbm, "_f1")

            @blk.sync
            def _(s):
                s.wait_ge(c_st.sem, c_st.n)

    return nc


_NC_CACHE = {}


def _get_nc():
    if "nc" not in _NC_CACHE:
        _NC_CACHE["nc"] = build_nc()
    return _NC_CACHE["nc"]


def kernel(**inputs):
    maps = _stage_inputs(inputs)
    nc = _get_nc()
    res = run_bass_kernel_spmd(nc, maps, core_ids=list(range(8)))
    out = np.zeros((B, S, D), np.float32)
    for c in range(8):
        u = c
        y = res.results[c]["out_fin"]
        out[:, 512 * u:512 * (u + 1), :] = y.reshape(D, B, 512).transpose(1, 2, 0)
    return out


# version 09
# speedup vs baseline: 1.4853x; 1.1179x over previous
import math
import os
import numpy as np
import ml_dtypes
import concourse.bass as bass
import concourse.mybir as mybir
from concourse.bass_core_rust import IndirectOffsetOnAxis
from concourse.bass_utils import run_bass_kernel_spmd

F32 = mybir.dt.float32
BF16 = mybir.dt.bfloat16
AF = mybir.ActivationFunctionType
ALU = mybir.AluOpType
AX = mybir.AxisListType

D = 1024
S = 4096
B = 4
NH = 16
HS = 64
FR = 256
NHC = 8
QC = 512
NQC = S // QC
NKT = S // 128
TOK2 = 2048
INNER = 4096

C_QKV = float((3.0 * D * HS * NH) ** (-0.25))
C_SQ2F = float(math.sqrt(2.0) / 16.0)
C_H = float(HS ** (-0.5))
C_OUT = float((D * D) ** (-0.25))
MAGIC = float(1.5 * 2 ** 23)
LN_EPS = 1e-5

_OMEGA = (10000.0 ** (-(np.arange(FR) // 2 * 2) / FR)).astype(np.float64)[::2]


def _g_of_lane(eps):
    l = np.arange(128)
    return l if eps == 1 else (l + 64) % 128


def _host_tables():
    t = {}
    k = np.arange(S, dtype=np.float64)
    for eps in (0, 1):
        g = _g_of_lane(eps)
        ang = k[None, :] * _OMEGA[g][:, None]
        sin, cos = np.sin(ang), np.cos(ang)
        in_k = ((np.arange(128) // 64) == eps)[:, None]
        t[f"sinT{eps}"] = sin.astype(ml_dtypes.bfloat16)
        t[f"cosT{eps}"] = cos.astype(ml_dtypes.bfloat16)
        t[f"khatB{eps}"] = np.where(in_k, cos, sin).astype(ml_dtypes.bfloat16)
        t[f"ceA{eps}"] = np.where(~in_k, cos, 0.0).astype(ml_dtypes.bfloat16)
    l = np.arange(128)
    ang2 = k[None, :] * _OMEGA[(l % 64) + 64][:, None]
    t["khatC2"] = np.sin(ang2).astype(ml_dtypes.bfloat16)
    m = np.zeros((4, 128, QC), np.float32)
    kk = np.arange(128)[:, None]
    qq = np.arange(QC)[None, :]
    for j in range(4):
        m[j] = (128 * j + kk <= qq)
    t["masks"] = np.ascontiguousarray(m.transpose(1, 0, 2)).astype(ml_dtypes.bfloat16)
    return t


def _stage_inputs(inp):
    x = np.ascontiguousarray(np.asarray(inp["L0_attn_input"], dtype=np.float32))
    qkv = np.asarray(inp["L0_attn_qkv"], dtype=np.float32)
    qb = np.asarray(inp["L0_attn_q_bias"], dtype=np.float32)
    pos = np.asarray(inp["L0_attn_positional"], dtype=np.float32)
    wout = np.asarray(inp["L0_attn_out_kernel"], dtype=np.float32)
    ob = np.asarray(inp["L0_attn_out_bias"], dtype=np.float32)

    Wqkvb = np.asarray(inp["L1_attn_W_qkvb"], dtype=np.float32)
    eperm = np.zeros(3088, np.int64)
    for n in range(16):
        j = np.arange(193)
        src_rows = n * 193 + j
        dst = np.where(j < 64, n * 64 + j,
              np.where(j < 128, 1024 + n * 64 + (j - 64),
              np.where(j < 192, 2048 + n * 64 + (j - 128), 3072 + n)))
        eperm[dst] = src_rows
    wqkvbT = np.ascontiguousarray(Wqkvb[eperm, :].T)
    woT = np.ascontiguousarray(np.asarray(inp["L1_attn_W_o"], dtype=np.float32).T)
    lng = np.asarray(inp["L1_attn_ln_g"], dtype=np.float32)
    lnb = np.asarray(inp["L1_attn_ln_b"], dtype=np.float32)
    lngb = np.stack([lng.reshape(8, 128).T, lnb.reshape(8, 128).T], axis=2)
    wgT0 = np.ascontiguousarray(np.asarray(inp["L0_ffn_w_gate"], dtype=np.float32).T)
    wuT0 = np.ascontiguousarray(np.asarray(inp["L0_ffn_w_up"], dtype=np.float32).T)
    wdT0 = np.ascontiguousarray(np.asarray(inp["L0_ffn_w_down"], dtype=np.float32).T)
    wgT1 = np.ascontiguousarray(np.asarray(inp["L1_ffn_w_gate"], dtype=np.float32).T)
    wuT1 = np.ascontiguousarray(np.asarray(inp["L1_ffn_w_up"], dtype=np.float32).T)
    wdT1 = np.ascontiguousarray(np.asarray(inp["L1_ffn_w_down"], dtype=np.float32).T)
    maps = []
    for c in range(8):
        b, hf = c // 2, c % 2
        m = {}
        m["x_b_fm"] = np.ascontiguousarray(x[b].T)
        u = c
        m["x_res_fm"] = np.ascontiguousarray(
            x[:, 512 * u:512 * (u + 1), :].transpose(2, 0, 1).reshape(D, TOK2))
        cols = []
        for p in range(4):
            cols += [qkv[:, 1, hf * 8 + 2 * p, :], qkv[:, 1, hf * 8 + 2 * p + 1, :]]
        for p in range(4):
            cols += [qkv[:, 0, hf * 8 + 2 * p, :], qkv[:, 0, hf * 8 + 2 * p + 1, :]]
        m["w_kq"] = np.ascontiguousarray(np.concatenate(cols, axis=1))
        m["w_v"] = np.ascontiguousarray(qkv[:, 2, hf * 8:hf * 8 + 8, :].reshape(D, 512))
        qbc = np.zeros((128, 4), np.float32)
        for p in range(4):
            qbc[0:64, p] = qb[hf * 8 + 2 * p]
            qbc[64:128, p] = qb[hf * 8 + 2 * p + 1]
        m["q_bias_col"] = qbc
        pa = np.zeros((128, NHC, 2, 128), np.float32)
        for h in range(NHC):
            eps = h & 1
            g = _g_of_lane(eps)
            n = hf * 8 + h
            for e in range(2):
                f = 2 * g + e
                pa[64 * eps:64 * eps + 64, h, e, :] = pos[f, n, :].T
        m["pos_arr"] = pa
        perm = np.zeros(D, np.int64)
        for ch in range(8):
            hf2, fc = ch // 4, ch % 4
            p = np.arange(128)
            n = hf2 * 8 + fc * 2 + p // 64
            perm[ch * 128 + p] = n * 64 + (p % 64)
        m["w_out"] = np.ascontiguousarray(wout[perm, :])
        m["out_bias_col"] = np.ascontiguousarray(ob.reshape(8, 128).T)
        m["wqkvbT"] = wqkvbT
        m["woT"] = woT
        m["lngb"] = np.ascontiguousarray(lngb)
        for ly, (wg, wu, wd) in enumerate([(wgT0, wuT0, wdT0), (wgT1, wuT1, wdT1)]):
            m[f"wg_sl_{ly}"] = np.ascontiguousarray(wg[c * 128:(c + 1) * 128, :])
            m[f"wu_sl_{ly}"] = np.ascontiguousarray(
                wu.reshape(8, 128, INNER)[:, :, 512 * c:512 * (c + 1)]
                  .transpose(1, 0, 2).reshape(128, INNER))
            m[f"wd_sl_{ly}"] = np.ascontiguousarray(
                wd.reshape(32, 128, D)[4 * c:4 * c + 4]
                  .transpose(1, 0, 2).reshape(128, INNER))
        idx = np.zeros((128, 8, 4), np.int32)
        p = np.arange(128)
        for ch in range(8):
            hf2, fc = ch // 4, ch % 4
            for bb in range(4):
                r = 2 * bb + hf2
                idx[:, ch, bb] = (r * NQC * 65 * NHC + u * (65 * NHC)
                                  + (1 + p % 64) * NHC + (fc * 2 + p // 64))
        m["imp_idx"] = idx
        maps.append(m)
    return maps


class Ctr:
    def __init__(self, stack, nc, name):
        self.sem = stack.enter_context(nc.semaphore(name))
        self.n = 0

    def inc(self, instr, k=1):
        instr.then_inc(self.sem, k)
        self.n += k
        return instr

    def wait(self, eng, n=None):
        eng.wait_ge(self.sem, self.n if n is None else n)


def build_nc():
    nc = bass.Bass("TRN2", target_bir_lowering=False, detect_race_conditions=False)
    T = _host_tables()

    x_b_fm = nc.dram_tensor("x_b_fm", [D, S], F32, kind="ExternalInput").ap()
    x_res_fm = nc.dram_tensor("x_res_fm", [D, TOK2], F32, kind="ExternalInput").ap()
    w_kq = nc.dram_tensor("w_kq", [D, D], F32, kind="ExternalInput").ap()
    w_v = nc.dram_tensor("w_v", [D, 512], F32, kind="ExternalInput").ap()
    q_bias_col = nc.dram_tensor("q_bias_col", [128, 4], F32, kind="ExternalInput").ap()
    pos_arr = nc.dram_tensor("pos_arr", [128, NHC, 2, 128], F32, kind="ExternalInput").ap()
    w_out = nc.dram_tensor("w_out", [D, D], F32, kind="ExternalInput").ap()
    out_bias_col = nc.dram_tensor("out_bias_col", [128, 8], F32, kind="ExternalInput").ap()
    imp_idx = nc.dram_tensor("imp_idx", [128, 8, 4], mybir.dt.int32, kind="ExternalInput").ap()

    out_fin = nc.dram_tensor("out_fin", [D, TOK2], F32, kind="ExternalOutput").ap()
    _K2 = "ExternalOutput" if os.environ.get("KDBG2") == "1" else "Internal"
    s_x1 = nc.dram_tensor("s_x1", [D, TOK2], F32, kind=_K2).ap()
    wsl_in = {}
    Qd = {}
    for ly in (0, 1):
        for nm in ("wg_sl", "wu_sl", "wd_sl"):
            wsl_in[(ly, nm)] = nc.dram_tensor(f"{nm}_{ly}", [128, INNER], F32,
                                              kind="ExternalInput").ap()
        Qd[ly] = {
            "qsum_in": nc.dram_tensor(f"qsum_in_{ly}", [1, 8], F32, kind="Internal").ap(),
            "qsum_out": nc.dram_tensor(f"qsum_out_{ly}", [8, 1, 8], F32, kind="Internal").ap(),
            "qg_in": nc.dram_tensor(f"qg_in_{ly}", [128, INNER], BF16, kind="Internal").ap(),
            "qg_out": nc.dram_tensor(f"qg_out_{ly}", [8, 128, INNER], BF16, kind="Internal").ap(),
            "qu_in": nc.dram_tensor(f"qu_in_{ly}", [4, 128, 8, 128], BF16, kind="Internal").ap(),
            "qu_out": nc.dram_tensor(f"qu_out_{ly}", [8, 4, 128, 8, 128], BF16, kind="Internal").ap(),
            "qd_in": nc.dram_tensor(f"qd_in_{ly}", [4, 128, D], BF16, kind="Internal").ap(),
            "qd_out": nc.dram_tensor(f"qd_out_{ly}", [8, 4, 128, D], BF16, kind="Internal").ap(),
        }
    s_x2 = nc.dram_tensor("s_x2", [D, TOK2], F32, kind=_K2).ap()
    s_x3 = nc.dram_tensor("s_x3", [D, TOK2], F32, kind=_K2).ap()
    s_h = nc.dram_tensor("s_h", [D, TOK2], F32, kind="Internal").ap()
    s_y = nc.dram_tensor("s_y", [D, TOK2], F32, kind="Internal").ap()
    wqkvbT_d = nc.dram_tensor("wqkvbT", [D, 3088], F32, kind="ExternalInput").ap()
    woT_d = nc.dram_tensor("woT", [D, D], F32, kind="ExternalInput").ap()
    lngb_d = nc.dram_tensor("lngb", [128, 8, 2], F32, kind="ExternalInput").ap()
    c_iden = nc.inline_tensor(np.eye(128, dtype=np.float32), name="c_iden").ap()
    DBG = os.environ.get("KDBG") == "1"
    if DBG:
        dbg_lnx = nc.dram_tensor("dbg_lnx", [128, 8, 512], BF16, kind="ExternalOutput").ap()
        dbg_qp = nc.dram_tensor("dbg_qp", [128, 4, S], BF16, kind="ExternalOutput").ap()
        dbg_khA = nc.dram_tensor("dbg_khA", [128, NHC, S], BF16, kind="ExternalOutput").ap()
        dbg_vh = nc.dram_tensor("dbg_vh", [128, NHC, NKT, 65], BF16, kind="ExternalOutput").ap()
        dbg_qhat = nc.dram_tensor("dbg_qhat", [128, NHC, 3, QC], BF16, kind="ExternalOutput").ap()
        dbg_oacc = nc.dram_tensor("dbg_oacc", [65, 2, QC], F32, kind="ExternalOutput").ap()
        dbg_oall = nc.dram_tensor("dbg_oall", [128, 8, TOK2], BF16, kind="ExternalOutput").ap()
        dbg_at = nc.dram_tensor("dbg_at", [128, 4, 2, QC], BF16, kind="ExternalOutput").ap()
        dbg_xq = nc.dram_tensor("dbg_xq", [128, 8, 512], BF16, kind="ExternalOutput").ap()
        dbg_gu = nc.dram_tensor("dbg_gu", [128, 32, 512], BF16, kind="ExternalOutput").ap()
        dbg_rows = nc.dram_tensor("dbg_rows", [1, 4, 512], F32, kind="ExternalOutput").ap()
        dbg_wqg = nc.dram_tensor("dbg_wqg", [128, 8, INNER], BF16, kind="ExternalOutput").ap()
        dbg_wut = nc.dram_tensor("dbg_wut", [128, 8, 128], BF16, kind="ExternalOutput").ap()
        dbg_sw = nc.dram_tensor("dbg_sw", [128, 6], F32, kind="ExternalOutput").ap()

    cin = nc.dram_tensor("cin", [NQC, 65, NHC, QC], BF16, kind="Internal").ap()
    cout = nc.dram_tensor("cout", [8, NQC, 65, NHC, QC], BF16, kind="Internal").ap()

    c_sin = [nc.inline_tensor(np.ascontiguousarray(T[f"sinT{e}"]), name=f"c_sin{e}").ap() for e in (0, 1)]
    c_cos = [nc.inline_tensor(np.ascontiguousarray(T[f"cosT{e}"]), name=f"c_cos{e}").ap() for e in (0, 1)]
    c_khB = [nc.inline_tensor(np.ascontiguousarray(T[f"khatB{e}"]), name=f"c_khB{e}").ap() for e in (0, 1)]
    c_khC2 = nc.inline_tensor(np.ascontiguousarray(T["khatC2"]), name="c_khC2").ap()
    c_ceA = [nc.inline_tensor(np.ascontiguousarray(T[f"ceA{e}"]), name=f"c_ceA{e}").ap() for e in (0, 1)]
    c_masks = nc.inline_tensor(np.ascontiguousarray(T["masks"]), name="c_masks").ap()
    c_ones_row = nc.inline_tensor(np.ones((1, 128), np.float32), name="c_ones_row").ap()
    c_ones_col = nc.inline_tensor(np.ones((128, 1), np.float32), name="c_ones_col").ap()

    from contextlib import ExitStack
    stk = ExitStack()
    with stk:
        def sb(name, shape, dt, st=stk):
            return st.enter_context(nc.sbuf_tensor(name, shape, dt))

        ones_row_s = sb("ones_row_s", [1, 128], F32)
        ones_col_s = sb("ones_col_s", [128, 1], F32)
        qbias_s = sb("qbias_s", [128, 4], F32)
        obias_s = sb("obias_s", [128, 8], F32)
        impi_s = sb("impi_s", [128, 8, 4], mybir.dt.int32)

        c_ld = Ctr(stk, nc, "c_ld")
        c_xt0 = Ctr(stk, nc, "c_xt0")
        c_xt1 = Ctr(stk, nc, "c_xt1")
        c_sc = Ctr(stk, nc, "c_sc")
        c_x0 = Ctr(stk, nc, "c_x0")
        c_gld = Ctr(stk, nc, "c_gld")
        c_mm = Ctr(stk, nc, "c_mm")
        c_ve = Ctr(stk, nc, "c_ve")
        c_se = Ctr(stk, nc, "c_se")
        c_s3 = Ctr(stk, nc, "c_s3")
        c_exp = Ctr(stk, nc, "c_exp")
        c_msk = Ctr(stk, nc, "c_msk")
        c_av = Ctr(stk, nc, "c_av")
        c_oev = Ctr(stk, nc, "c_oev")
        c_rot = Ctr(stk, nc, "c_rot")
        c_tmm = Ctr(stk, nc, "c_tmm")
        c_exd = Ctr(stk, nc, "c_exd")
        c_cc = Ctr(stk, nc, "c_cc")
        c_imp = Ctr(stk, nc, "c_imp")
        c_st = Ctr(stk, nc, "c_st")
        c_ws0 = Ctr(stk, nc, "c_ws0")
        c_ws1 = Ctr(stk, nc, "c_ws1")
        c_wu0 = Ctr(stk, nc, "c_wu0")
        c_wu1 = Ctr(stk, nc, "c_wu1")
        c_wd0 = Ctr(stk, nc, "c_wd0")
        c_wd1 = Ctr(stk, nc, "c_wd1")
        c_xs0 = Ctr(stk, nc, "c_xs0")
        c_xs1 = Ctr(stk, nc, "c_xs1")
        c_x1r = Ctr(stk, nc, "c_x1r")
        c_fd = Ctr(stk, nc, "c_fd")
        c_hs0 = Ctr(stk, nc, "c_hs0")
        c_hs1 = Ctr(stk, nc, "c_hs1")
        c_sq = Ctr(stk, nc, "c_sq")

        with nc.Block() as blk:
            atn = ExitStack()
            with atn:
                def sba(name, shape, dt):
                    return atn.enter_context(nc.sbuf_tensor(name, shape, dt))

                khatA = sba("khatA", [128, NHC, S], BF16)
                khB = sba("khB", [128, 2, S], BF16)
                khC = sba("khC", [128, S], BF16)
                vhat = sba("vhat", [128, NHC, NKT, 65], BF16)
                qp_all = sba("qp_all", [128, 4, S], BF16)
                posb = sba("posb", [128, NHC, 2, 128], BF16)
                masks_s = sba("masks_s", [128, 4, QC], BF16)

                @blk.sync
                def _(s):
                    c_ld.inc(s.dma_start(out=masks_s[:], in_=c_masks), 16)
                    c_ld.inc(s.dma_start(out=ones_row_s[:], in_=c_ones_row), 16)
                    c_ld.inc(s.dma_start(out=ones_col_s[:], in_=c_ones_col), 16)
                    c_ld.inc(s.dma_start(out=qbias_s[:], in_=q_bias_col), 16)
                    c_ld.inc(s.dma_start(out=obias_s[:], in_=out_bias_col), 16)
                    c_ld.inc(s.dma_start(out=impi_s[:], in_=imp_idx), 16)
                    for e in (0, 1):
                        c_ld.inc(s.dma_start(out=khB[:, e, :], in_=c_khB[e]), 16)
                    c_ld.inc(s.dma_start(out=khC[:], in_=c_khC2), 16)
                    for h in range(NHC):
                        e = h & 1
                        lo = 64 - 64 * e
                        c_ld.inc(s.dma_start(out=khatA[lo:lo + 64, h, :],
                                             in_=c_ceA[e][lo:lo + 64, :]), 16)
                n_const = c_ld.n

                @blk.gpsimd
                def _(g):
                    c_gld.inc(g.dma_start(out=posb[:], in_=pos_arr), 16)

                @blk.vector
                def _(v):
                    c_ve.inc(v.memset(vhat[:, :, :, 0:1], 1.0))
                    v.wait_ge(c_ld.sem, n_const)
                    c_ve.inc(v.tensor_scalar_mul(qbias_s[:], qbias_s[:], C_H))

                pha = ExitStack()
                with pha:
                    def sbp(name, shape, dt):
                        return pha.enter_context(nc.sbuf_tensor(name, shape, dt))

                    xt = sbp("xt", [128, 2, 512], F32)
                    xsq = sbp("xsq", [128, 512], F32)
                    lnx = sbp("lnx", [128, 8, 512], BF16)
                    rowa = sbp("rowa", [1, 512], F32)
                    rowb = sbp("rowb", [1, 512], F32)
                    rowc = sbp("rowc", [1, 512], F32)
                    wkq_s = sbp("wkq_s", [128, 8, D], BF16)
                    wv_s = sbp("wv_s", [128, 8, 512], BF16)

                    @blk.gpsimd
                    def _(g):
                        c_gld.inc(g.dma_start(out=wkq_s[:], in_=w_kq.rearrange("(c p) m -> p c m", p=128)), 16)
                        c_gld.inc(g.dma_start(out=wv_s[:], in_=w_v.rearrange("(c p) m -> p c m", p=128)), 16)

                    @blk.vector
                    def _(v):
                        c_gld.wait(v)
                        c_ve.inc(v.tensor_scalar_mul(posb[:], posb[:], C_SQ2F))

                    with (
                        nc.psum_tensor("pa_sum", [1, 2, 512], F32) as pa_sum,
                        nc.psum_tensor("pa_bc", [128, 2, 512], F32) as pa_bc,
                        nc.psum_tensor("pa_kq", [128, 512], F32) as pa_kq,
                        nc.psum_tensor("pa_v", [128, 512], F32) as pa_v,
                    ):
                        xv = x_b_fm.rearrange("(c p) t -> p c t", p=128)
                        for tg in range(8):
                            tsl = slice(tg * 512, (tg + 1) * 512)
                            for ch in range(8):
                                cx = c_xt0 if ch % 2 == 0 else c_xt1

                                @blk.sync
                                def _(s, ch=ch, tsl=tsl, cx=cx):
                                    c_mm.wait(s)
                                    c_ve.wait(s)
                                    cx.inc(s.dma_start(out=xt[:, ch % 2, :], in_=xv[:, ch, tsl]), 16)

                                @blk.vector
                                def _(v, ch=ch, cx=cx):
                                    cx.wait(v)
                                    if ch >= 1:
                                        c_mm.wait(v)
                                    c_ve.inc(v.tensor_mul(xsq[:], xt[:, ch % 2, :], xt[:, ch % 2, :]))

                                @blk.tensor
                                def _(t, ch=ch):
                                    c_ve.wait(t)
                                    t.matmul(pa_sum[:, 0, :], ones_col_s[:], xt[:, ch % 2, :],
                                             start=(ch == 0), stop=(ch == 7))
                                    c_mm.inc(t.matmul(pa_sum[:, 1, :], ones_col_s[:], xsq[:],
                                                      start=(ch == 0), stop=(ch == 7)))

                            @blk.vector
                            def _(v):
                                c_mm.wait(v)
                                c_ve.inc(v.tensor_scalar_mul(rowa[:], pa_sum[0:1, 0, :], 1.0 / D))
                                c_ve.inc(v.tensor_mul(rowc[:], rowa[:], rowa[:]))
                                c_ve.inc(v.scalar_tensor_tensor(rowb[:], pa_sum[0:1, 1, :], 1.0 / D,
                                                                rowc[:], ALU.mult, ALU.subtract))
                                c_ve.inc(v.tensor_scalar_add(rowb[:], rowb[:], LN_EPS))

                            @blk.scalar
                            def _(sc):
                                c_ve.wait(sc)
                                c_se.inc(sc.activation(rowb[:], rowb[:], AF.Sqrt))

                            @blk.vector
                            def _(v):
                                c_se.wait(v)
                                v.wait_ge(c_ve.sem, c_ve.n)
                                c_ve.inc(v.reciprocal(rowb[:], rowb[:]))
                                c_ve.inc(v.tensor_scalar_mul(rowb[:], rowb[:], C_QKV))
                                c_ve.inc(v.tensor_mul(rowc[:], rowa[:], rowb[:]))
                                c_ve.inc(v.tensor_scalar_mul(rowc[:], rowc[:], -1.0))

                            @blk.tensor
                            def _(t):
                                c_ve.wait(t)
                                t.matmul(pa_bc[:, 0, :], ones_row_s[:], rowb[:], start=True, stop=True)
                                c_mm.inc(t.matmul(pa_bc[:, 1, :], ones_row_s[:], rowc[:], start=True, stop=True))

                            for ch in range(8):
                                cx = c_xt0 if ch % 2 == 0 else c_xt1

                                @blk.sync
                                def _(s, ch=ch, tsl=tsl, cx=cx):
                                    c_ve.wait(s)
                                    cx.inc(s.dma_start(out=xt[:, ch % 2, :], in_=xv[:, ch, tsl]), 16)

                                @blk.vector
                                def _(v, ch=ch, cx=cx):
                                    cx.wait(v)
                                    if ch == 0:
                                        c_mm.wait(v)
                                    c_ve.inc(v.tensor_mul(xt[:, ch % 2, :], xt[:, ch % 2, :], pa_bc[:, 0, :]))
                                    c_ve.inc(v.tensor_add(lnx[:, ch, :], xt[:, ch % 2, :], pa_bc[:, 1, :]))
                            lnx_done = c_ve.n
                            if DBG and tg == 1:
                                @blk.sync
                                def _(s):
                                    s.wait_ge(c_ve.sem, lnx_done)
                                    c_st.inc(s.dma_start(out=dbg_lnx, in_=lnx[:]), 16)

                            for mck in range(8):
                                @blk.tensor
                                def _(t, mck=mck, lnx_done=lnx_done):
                                    t.wait_ge(c_ve.sem, lnx_done)
                                    c_ve.wait(t)
                                    c_se.wait(t)
                                    if mck == 0:
                                        c_gld.wait(t)
                                    for dch in range(8):
                                        last = t.matmul(pa_kq[:], wkq_s[:, dch, mck * 128:(mck + 1) * 128],
                                                        lnx[:, dch, :], start=(dch == 0), stop=(dch == 7))
                                    c_mm.inc(last)

                                if mck < 4:
                                    @blk.vector
                                    def _(v, mck=mck, tsl=tsl):
                                        c_mm.wait(v)
                                        c_ve.inc(v.tensor_copy(khatA[0:64, 2 * mck, tsl], pa_kq[0:64, :]))
                                        c_ve.inc(v.tensor_copy(khatA[64:128, 2 * mck + 1, tsl], pa_kq[64:128, :]))
                                else:
                                    @blk.scalar
                                    def _(sc, mck=mck, tsl=tsl):
                                        c_mm.wait(sc)
                                        c_se.inc(sc.activation(qp_all[:, mck - 4, tsl], pa_kq[:], AF.Identity,
                                                               bias=qbias_s[:, mck - 4:mck - 3], scale=C_H))

                            for tt in range(4):
                                @blk.tensor
                                def _(t, tt=tt):
                                    c_ve.wait(t)
                                    c_se.wait(t)
                                    for dch in range(8):
                                        last = t.matmul(pa_v[:], lnx[:, dch, tt * 128:(tt + 1) * 128],
                                                        wv_s[:, dch, :], start=(dch == 0), stop=(dch == 7))
                                    c_mm.inc(last)

                                @blk.vector
                                def _(v, tg=tg, tt=tt):
                                    c_mm.wait(v)
                                    kt = tg * 4 + tt
                                    for h in range(NHC):
                                        c_ve.inc(v.tensor_copy(vhat[:, h, kt, 1:65],
                                                               pa_v[:, h * 64:(h + 1) * 64]))

                phb = ExitStack()
                with phb:
                    def sbb(name, shape, dt):
                        return phb.enter_context(nc.sbuf_tensor(name, shape, dt))

                    qhat = sbb("qhat", [128, NHC, 3, QC], BF16)
                    oacc = sbb("oacc", [65, 2, QC], F32)
                    at_s = sbb("at_s", [128, 2, 2, QC], BF16)
                    exp_s = sbb("exp_s", [65, 2, QC], BF16)
                    uws = sbb("uws", [128, QC], F32)
                    uw2 = sbb("uw2", [128, QC], F32)
                    den = sbb("den", [1, QC], F32)
                    sc_qc = sbb("sc_qc", [128, 2, 2, QC], BF16)

                    @blk.vector
                    def _(v):
                        c_mm.wait(v)
                        c_se.wait(v)
                        c_ve.inc(v.memset(qhat[:], 0.0))

                    if DBG:
                        @blk.sync
                        def _(s):
                            c_mm.wait(s)
                            c_ve.wait(s)
                            c_se.wait(s)
                            c_st.inc(s.dma_start(out=dbg_qp, in_=qp_all[:]), 16)
                            c_st.inc(s.dma_start(out=dbg_khA, in_=khatA[:]), 16)
                            c_st.inc(s.dma_start(out=dbg_vh, in_=vhat[:]), 16)

                    with (
                        nc.psum_tensor("pb_T", [128, 2, QC], F32) as pb_T,
                        nc.psum_tensor("pb_S", [128, 2, 2, QC], F32) as pb_S,
                        nc.psum_tensor("pb_O", [65, 2, QC], F32) as pb_O,
                    ):
                        it = 0
                        nmask = 0
                        for qc in range(NQC):
                            qsl = slice(qc * QC, (qc + 1) * QC)
                            s3_before = c_s3.n

                            @blk.sync
                            def _(s, qsl=qsl, first=(qc == 0)):
                                if first:
                                    c_mm.wait(s)
                                    c_ve.wait(s)
                                    c_se.wait(s)
                                c_rot.wait(s)
                                for e in (0, 1):
                                    c_sc.inc(s.dma_start(out=sc_qc[:, e, 0, :], in_=c_sin[e][:, qsl]), 16)
                                    c_sc.inc(s.dma_start(out=sc_qc[:, e, 1, :], in_=c_cos[e][:, qsl]), 16)

                            for h in range(NHC):
                                e = h & 1

                                @blk.tensor
                                def _(t, qc=qc, h=h, e=e, qsl=qsl):
                                    if h == 0:
                                        c_ve.wait(t)
                                    if qc == 0 and h == 0:
                                        c_se.wait(t)
                                    c_rot.wait(t)
                                    t.matmul(pb_T[:, 0, :], posb[64 * e:64 * e + 64, h, 0, :],
                                             qp_all[64 * e:64 * e + 64, h // 2, qsl],
                                             start=True, stop=True)
                                    c_tmm.inc(t.matmul(pb_T[:, 1, :], posb[64 * e:64 * e + 64, h, 1, :],
                                                       qp_all[64 * e:64 * e + 64, h // 2, qsl],
                                                       start=True, stop=True))

                                @blk.vector
                                def _(v, h=h, e=e, s3_before=s3_before):
                                    c_tmm.wait(v)
                                    c_sc.wait(v)
                                    if h == 0:
                                        v.wait_ge(c_s3.sem, s3_before)
                                    ko = 64 * e
                                    no = 64 - ko
                                    c_ve.inc(v.tensor_mul(uws[:], pb_T[:, 0, :], sc_qc[:, e, 0, :]))
                                    c_ve.inc(v.tensor_mul(uw2[:], pb_T[:, 1, :], sc_qc[:, e, 1, :]))
                                    c_ve.inc(v.tensor_add(uws[:], uws[:], uw2[:]))
                                    c_ve.inc(v.tensor_copy(qhat[ko:ko + 64, h, 1, :], uws[ko:ko + 64, :]))
                                    c_ve.inc(v.tensor_copy(qhat[no:no + 64, h, 0, :], uws[no:no + 64, :]))
                                    c_ve.inc(v.tensor_mul(uws[:], pb_T[:, 1, :], sc_qc[:, e, 0, :]))
                                    c_ve.inc(v.tensor_mul(uw2[:], pb_T[:, 0, :], sc_qc[:, e, 1, :]))
                                    c_ve.inc(v.tensor_sub(uws[:], uws[:], uw2[:]))
                                    c_ve.inc(v.tensor_copy(qhat[ko:ko + 64, h, 2, :], uws[ko:ko + 64, :]))
                                    c_rot.inc(v.tensor_copy(qhat[no:no + 64, h, 1, :], uws[no:no + 64, :]))

                            @blk.vector
                            def _(v, qsl=qsl):
                                c_ve.inc(v.tensor_copy(qhat[0:64, 0:NHC:2, 0, :], qp_all[0:64, :, qsl]))
                                c_rot.inc(v.tensor_copy(qhat[64:128, 1:NHC:2, 0, :], qp_all[64:128, :, qsl]))
                            rot_done = c_rot.n
                            if DBG and qc == 0:
                                @blk.sync
                                def _(s, rot_done=rot_done):
                                    s.wait_ge(c_rot.sem, rot_done)
                                    c_st.inc(s.dma_start(out=dbg_qhat, in_=qhat[:]), 16)

                            nkt = 4 * (qc + 1)
                            ktgs = [range(g0, min(g0 + 8, nkt)) for g0 in range(0, nkt, 8)]
                            for pr in range(4):
                                for gi, ktg in enumerate(ktgs):
                                    first = True
                                    pend_av = [None]

                                    def mk_av(pr, kt, buf, it, diag, nmask, first, last_kt):
                                        def emit_av(t):
                                            if diag:
                                                t.wait_ge(c_msk.sem, nmask)
                                            else:
                                                t.wait_ge(c_exp.sem, it + 1)
                                            if first:
                                                t.wait_ge(c_oev.sem, c_oev.n)
                                            for hp in range(2):
                                                h = 2 * pr + hp
                                                last = t.matmul(pb_O[:, hp, :], vhat[:, h, kt, :],
                                                                at_s[:, buf, hp, :],
                                                                start=first, stop=last_kt)
                                            c_av.inc(last)
                                        return emit_av

                                    for kt in ktg:
                                        buf = it % 2
                                        diag = (kt >= 4 * qc)
                                        if diag:
                                            nmask += 1

                                        @blk.tensor
                                        def _(t, pr=pr, kt=kt, buf=buf, it=it, rot_done=rot_done,
                                              pend=pend_av[0]):
                                            t.wait_ge(c_rot.sem, rot_done)
                                            if it >= 2:
                                                t.wait_ge(c_exp.sem, it - 1)
                                            ksl = slice(kt * 128, (kt + 1) * 128)
                                            for hp in range(2):
                                                h = 2 * pr + hp
                                                e = h & 1
                                                t.matmul(pb_S[:, buf, hp, :], khatA[:, h, ksl],
                                                         qhat[:, h, 0, :], start=True, stop=False)
                                                t.matmul(pb_S[:, buf, hp, :], khB[:, e, ksl],
                                                         qhat[:, h, 1, :], start=False, stop=False)
                                                last = t.matmul(pb_S[:, buf, hp, :],
                                                                khC[64 * e:64 * e + 64, ksl],
                                                                qhat[64 * e:64 * e + 64, h, 2, :],
                                                                start=False, stop=True)
                                            c_s3.inc(last)
                                            if pend is not None:
                                                pend(t)

                                        @blk.scalar
                                        def _(sc, buf=buf, it=it):
                                            sc.wait_ge(c_s3.sem, it + 1)
                                            if it >= 2:
                                                sc.wait_ge(c_av.sem, it - 1)
                                            c_exp.inc(sc.activation(
                                                at_s[:, buf, :, :].rearrange("p a q -> p (a q)"),
                                                pb_S[:, buf, :, :].rearrange("p a q -> p (a q)"),
                                                AF.Exp))

                                        if diag:
                                            j = kt - 4 * qc

                                            @blk.vector
                                            def _(v, buf=buf, it=it, j=j):
                                                v.wait_ge(c_exp.sem, it + 1)
                                                c_msk.inc(v.tensor_mul(
                                                    at_s[:, buf, :, :], at_s[:, buf, :, :],
                                                    masks_s[:, j:j + 1, :].broadcast_to([128, 2, QC])))

                                        if DBG and qc == 0 and pr == 0:
                                            @blk.sync
                                            def _(s, kt=kt, buf=buf, nmask=nmask):
                                                s.wait_ge(c_msk.sem, nmask)
                                                c_st.inc(s.dma_start(out=dbg_at[:, kt, :, :],
                                                                     in_=at_s[:, buf, :, :]), 16)

                                        pend_av[0] = mk_av(pr, kt, buf, it, diag, nmask,
                                                           first, kt == ktg[-1])
                                        first = False
                                        it += 1

                                    @blk.tensor
                                    def _(t, pend=pend_av[0]):
                                        pend(t)

                                    @blk.vector
                                    def _(v, it=it, first_group=(gi == 0)):
                                        v.wait_ge(c_av.sem, it)
                                        if first_group:
                                            c_oev.inc(v.tensor_copy(oacc[:], pb_O[:]))
                                        else:
                                            c_oev.inc(v.tensor_add(oacc[:], oacc[:], pb_O[:]))

                                if DBG and qc == 0 and pr == 0:
                                    @blk.sync
                                    def _(s):
                                        c_oev.wait(s)
                                        c_st.inc(s.dma_start(out=dbg_oacc, in_=oacc[:]), 16)

                                for hp in range(2):
                                    @blk.vector
                                    def _(v, hp=hp):
                                        c_oev.wait(v)
                                        c_mm.wait(v)
                                        c_exd.wait(v)
                                        v.wait_ge(c_ve.sem, c_ve.n)
                                        c_ve.inc(v.reciprocal(den[:], oacc[0:1, hp, :]))

                                    @blk.tensor
                                    def _(t, hp=hp):
                                        c_ve.wait(t)
                                        c_mm.inc(t.matmul(pb_T[:, hp, :], ones_row_s[:], den[:],
                                                          start=True, stop=True))

                                    @blk.vector
                                    def _(v, hp=hp):
                                        c_mm.wait(v)
                                        c_ve.inc(v.tensor_mul(exp_s[:, hp, :], oacc[:, hp, :],
                                                              pb_T[0:65, hp, :]))

                                @blk.sync
                                def _(s, qc=qc, pr=pr):
                                    c_ve.wait(s)
                                    c_exd.inc(s.dma_start(out=cin[qc, :, 2 * pr:2 * pr + 2, :],
                                                          in_=exp_s[:]), 16)

                        @blk.gpsimd
                        def _(g):
                            g.wait_ge(c_exd.sem, c_exd.n)
                            c_cc.inc(g.collective_compute(
                                "AllGather", ALU.bypass,
                                replica_groups=[[0, 1, 2, 3, 4, 5, 6, 7]],
                                ins=[cin], outs=[cout]))

            phd = ExitStack()
            with phd:
                def sbd(name, shape, dt):
                    return phd.enter_context(nc.sbuf_tensor(name, shape, dt))

                wout_s = sbd("wout_s", [128, 8, D], BF16)
                oall = sbd("oall", [128, 8, TOK2], BF16)
                x0t = sbd("x0t", [128, 4, 512], F32)
                x1t = sbd("x1t", [128, 4, 512], F32)

                @blk.gpsimd
                def _(g):
                    g.wait_ge(c_cc.sem, 1)
                    c_gld.inc(g.dma_start(out=wout_s[:], in_=w_out.rearrange("(c p) m -> p c m", p=128)), 16)
                    src_rows = cout.rearrange("r u f h q -> (r u f h) q")
                    for ch in range(8):
                        for b in range(4):
                            c_imp.inc(g.indirect_dma_start(
                                oall[:, ch, b * 512:(b + 1) * 512], None,
                                src_rows,
                                IndirectOffsetOnAxis(impi_s[:, ch, b:b + 1], 0)), 16)
                n_imp = c_imp.n
                if DBG:
                    @blk.sync
                    def _(s, n_imp=n_imp):
                        s.wait_ge(c_imp.sem, n_imp)
                        c_st.inc(s.dma_start(out=dbg_oall, in_=oall[:]), 16)

                with nc.psum_tensor("pd", [128, 4, 512], F32) as pd:
                    for mck in range(8):
                        @blk.sync
                        def _(s, mck=mck, first=(mck == 0)):
                            if first:
                                c_s3.wait(s)
                                c_av.wait(s)
                                c_exp.wait(s)
                                c_exd.wait(s)
                            c_ve.wait(s)
                            c_x0.inc(s.dma_start(
                                out=x0t[:],
                                in_=x_res_fm[mck * 128:(mck + 1) * 128, :]
                                    .rearrange("p (a t) -> p a t", t=512)), 16)

                        @blk.tensor
                        def _(t, mck=mck, n_imp=n_imp):
                            if mck == 0:
                                t.wait_ge(c_imp.sem, n_imp)
                                c_gld.wait(t)
                            else:
                                c_se.wait(t)
                            for tt in range(4):
                                for och in range(8):
                                    last = t.matmul(pd[:, tt, :],
                                                    wout_s[:, och, mck * 128:(mck + 1) * 128],
                                                    oall[:, och, tt * 512:(tt + 1) * 512],
                                                    start=(och == 0), stop=(och == 7))
                                c_mm.inc(last)

                        @blk.scalar
                        def _(sc, mck=mck):
                            c_mm.wait(sc)
                            c_se.inc(sc.activation(x1t[:].rearrange("p a t -> p (a t)"),
                                                   pd[:].rearrange("p a t -> p (a t)"), AF.Identity,
                                                   bias=obias_s[:, mck:mck + 1], scale=C_OUT))

                        @blk.vector
                        def _(v, mck=mck):
                            c_se.wait(v)
                            c_x0.wait(v)
                            c_ve.inc(v.tensor_add(x1t[:], x1t[:], x0t[:]))

                        @blk.sync
                        def _(s, mck=mck):
                            c_ve.wait(s)
                            c_st.inc(s.dma_start(
                                out=s_x1[mck * 128:(mck + 1) * 128, :]
                                    .rearrange("p (a t) -> p a t", t=512),
                                in_=x1t[:]), 16)


            def emit_ffn(s_in, s_out, wg_sl, wu_sl, wd_sl, Q, tag):
                phe = ExitStack()
                with phe:
                    def sbe(name, shape, dt):
                        return phe.enter_context(nc.sbuf_tensor(name + tag, shape, dt))

                    wqg = sbe("wqg", [128, 8, INNER], BF16)
                    swcol = sbe("swcol", [128, 3], F32)
                    swinv = sbe("swinv", [128, 3], F32)
                    with (
                        nc.sbuf_tensor("wsl" + tag, [128, 3, INNER], F32) as wsl,
                        nc.sbuf_tensor("wacc" + tag, [128, 3], F32) as wacc,
                        nc.sbuf_tensor("wq_st" + tag, [128, INNER], BF16) as wq_st,
                        nc.sbuf_tensor("srow" + tag, [1, 8], F32) as srow,
                        nc.sbuf_tensor("sred" + tag, [1, 8, 8], F32) as sred,
                        nc.psum_tensor("pw" + tag, [128, 8], F32) as pw,
                    ):
                        for mi, wsrc in enumerate([wg_sl, wu_sl, wd_sl]):
                            cw = c_ws0 if mi % 2 == 0 else c_ws1

                            @blk.sync
                            def _(s, mi=mi, wsrc=wsrc, cw=cw):
                                if mi == 0:
                                    c_ve.wait(s)
                                    c_st.wait(s)
                                cw.inc(s.dma_start(out=wsl[:, mi, :], in_=wsrc), 16)

                            @blk.vector
                            def _(v, mi=mi, cw=cw):
                                cw.wait(v)
                                c_ve.inc(v.reduce_sum(wacc[:, mi:mi + 1], wsl[:, mi, :],
                                                      axis=AX.X, apply_absolute_value=True))

                        @blk.tensor
                        def _(t):
                            c_ve.wait(t)
                            c_mm.inc(t.matmul(pw[0:1, 0:3], ones_col_s[:], wacc[:, 0:3],
                                              start=True, stop=True))

                        @blk.vector
                        def _(v):
                            c_mm.wait(v)
                            c_ve.inc(v.tensor_copy(srow[:, 0:3], pw[0:1, 0:3]))

                        @blk.sync
                        def _(s):
                            c_ve.wait(s)
                            c_sq.inc(s.dma_start(out=Q["qsum_in"], in_=srow[:, 0:8]), 16)

                        @blk.gpsimd
                        def _(g):
                            g.wait_ge(c_sq.sem, c_sq.n)
                            c_cc.inc(g.collective_compute(
                                "AllGather", ALU.bypass,
                                replica_groups=[[0, 1, 2, 3, 4, 5, 6, 7]],
                                ins=[Q["qsum_in"]], outs=[Q["qsum_out"]]))
                        cc_sums = c_cc.n

                        @blk.sync
                        def _(s):
                            s.wait_ge(c_cc.sem, cc_sums)
                            c_ld.inc(s.dma_start(out=sred[:],
                                                 in_=Q["qsum_out"].rearrange("r o s -> o s r"),
                                                 allow_slow_non_contiguous=True), 16)

                        @blk.vector
                        def _(v):
                            c_ld.wait(v)
                            c_ve.inc(v.reduce_sum(srow[:, 0:8], sred[:], axis=AX.X))
                            v.wait_ge(c_ve.sem, c_ve.n)
                            c_ve.inc(v.tensor_scalar(srow[:, 0:3], srow[:, 0:3],
                                                     1.0 / (D * INNER), 1e-5,
                                                     ALU.mult, ALU.max))
                            v.wait_ge(c_ve.sem, c_ve.n)
                            c_ve.inc(v.reciprocal(srow[:, 4:7], srow[:, 0:3]))

                        @blk.tensor
                        def _(t):
                            c_ve.wait(t)
                            t.matmul(pw[:, 0:3], ones_row_s[:], srow[:, 4:7], start=True, stop=True)
                            c_mm.inc(t.matmul(pw[:, 3:6], ones_row_s[:], srow[:, 0:3],
                                              start=True, stop=True))

                        @blk.vector
                        def _(v):
                            c_mm.wait(v)
                            c_ve.inc(v.tensor_copy(swcol[:], pw[:, 0:3]))
                            c_ve.inc(v.tensor_copy(swinv[:], pw[:, 3:6]))
                        sw_ready = c_ve.n

                        for mi in range(3):
                            @blk.vector
                            def _(v, mi=mi, sw_ready=sw_ready):
                                if mi == 0:
                                    v.wait_ge(c_ve.sem, sw_ready)
                                c_sq.wait(v)
                                c_ve.inc(v.tensor_scalar(wsl[:, mi, :], wsl[:, mi, :],
                                                         swcol[:, mi:mi + 1], MAGIC, ALU.mult, ALU.add))
                                c_ve.inc(v.tensor_scalar(wsl[:, mi, :], wsl[:, mi, :],
                                                         -MAGIC, 1.0, ALU.add, ALU.min))
                                c_ve.inc(v.tensor_scalar(wq_st[:], wsl[:, mi, :],
                                                         -1.0, None, ALU.max))

                            @blk.sync
                            def _(s, mi=mi):
                                c_ve.wait(s)
                                if mi == 0:
                                    c_sq.inc(s.dma_start(out=Q["qg_in"], in_=wq_st[:]), 16)
                                elif mi == 1:
                                    c_sq.inc(s.dma_start(
                                        out=Q["qu_in"].rearrange("l p d c -> p d l c"),
                                        in_=wq_st[:].rearrange("p (d l c) -> p d l c", l=4, c=128)), 16)
                                else:
                                    c_sq.inc(s.dma_start(
                                        out=Q["qd_in"].rearrange("j p m -> p j m"),
                                        in_=wq_st[:].rearrange("p (j m) -> p j m", m=D)), 16)

                        @blk.gpsimd
                        def _(g):
                            g.wait_ge(c_sq.sem, c_sq.n)
                            for a, b in (("qg_in", "qg_out"), ("qu_in", "qu_out"), ("qd_in", "qd_out")):
                                c_cc.inc(g.collective_compute(
                                    "AllGather", ALU.bypass,
                                    replica_groups=[[0, 1, 2, 3, 4, 5, 6, 7]],
                                    ins=[Q[a]], outs=[Q[b]]))
                        cc_w = c_cc.n

                        @blk.gpsimd
                        def _(g):
                            g.wait_ge(c_cc.sem, cc_sums + 1)
                            c_gld.inc(g.dma_start(out=wqg[:],
                                                  in_=Q["qg_out"].rearrange("d p m -> p d m")), 16)
                        gld_w = c_gld.n
                    wqu_view = Q["qu_out"].rearrange("r l p d c -> (r l) p d c")
                    wqd_view = Q["qd_out"].rearrange("r j p m -> p (r j) m")

                    with (
                        nc.sbuf_tensor("xln" + tag, [128, 8, 512], F32) as xln,
                        nc.sbuf_tensor("xqf" + tag, [128, 8, 512], BF16) as xqf,
                        nc.sbuf_tensor("guf" + tag, [128, 32, 512], BF16) as guf,
                        nc.sbuf_tensor("xst" + tag, [128, 2, 512], F32) as xst,
                        nc.sbuf_tensor("xsqb" + tag, [128, 512], F32) as xsqb,
                        nc.sbuf_tensor("gmax" + tag, [128, 512], F32) as gmax,
                        nc.sbuf_tensor("gmin" + tag, [128, 512], F32) as gmin,
                        nc.sbuf_tensor("foldw" + tag, [32, 12, 512], F32) as foldw,
                        nc.sbuf_tensor("rw1" + tag, [1, 512], F32) as rw1,
                        nc.sbuf_tensor("rw2" + tag, [1, 512], F32) as rw2,
                        nc.sbuf_tensor("rw3" + tag, [1, 512], F32) as rw3,
                        nc.sbuf_tensor("bct" + tag, [128, 4, 512], F32) as bct,
                        nc.sbuf_tensor("wut" + tag, [128, 2, 8, 128], BF16) as wut,
                        nc.sbuf_tensor("wdt" + tag, [128, 2, D], BF16) as wdt,
                        nc.sbuf_tensor("tmpa" + tag, [128, 512], F32) as tmpa,
                        nc.sbuf_tensor("tmpb" + tag, [128, 512], F32) as tmpb,
                        nc.sbuf_tensor("x1tl" + tag, [128, 512], F32) as x1tl,
                        nc.sbuf_tensor("x2tl" + tag, [128, 512], F32) as x2tl,
                        nc.psum_tensor("pe" + tag, [128, 8, 512], F32) as pe,
                    ):
                        xv2 = s_in.rearrange("(c p) t -> p c t", p=128)
                        ov2 = s_out.rearrange("(c p) t -> p c t", p=128)
                        st = {"dn_prev": None, "ln_free": None, "gu_mm_prev": None}
                        st_in = c_st.n

                        def emit_fold(gate_fn=None):
                            @blk.gpsimd
                            def _(g):
                                c_ve.wait(g)
                                c_fd.inc(g.dma_start(out=foldw[0:32, 0:4, :], in_=gmax[:]), 16)
                            @blk.vector
                            def _(v):
                                c_fd.wait(v)
                                c_ve.inc(v.tensor_tensor(foldw[0:32, 0:2, :], foldw[0:32, 0:2, :],
                                                         foldw[0:32, 2:4, :], ALU.max))
                                c_ve.inc(v.tensor_tensor(foldw[0:32, 0:1, :], foldw[0:32, 0:1, :],
                                                         foldw[0:32, 1:2, :], ALU.max))
                            @blk.gpsimd
                            def _(g):
                                c_ve.wait(g)
                                c_fd.inc(g.dma_start(out=foldw[0:4, 4:12, :], in_=foldw[0:32, 0, :]), 16)
                            @blk.vector
                            def _(v):
                                c_fd.wait(v)
                                c_ve.inc(v.tensor_tensor(foldw[0:4, 4:8, :], foldw[0:4, 4:8, :],
                                                         foldw[0:4, 8:12, :], ALU.max))
                                c_ve.inc(v.tensor_tensor(foldw[0:4, 4:6, :], foldw[0:4, 4:6, :],
                                                         foldw[0:4, 6:8, :], ALU.max))
                                c_ve.inc(v.tensor_tensor(foldw[0:4, 4:5, :], foldw[0:4, 4:5, :],
                                                         foldw[0:4, 5:6, :], ALU.max))
                            @blk.gpsimd
                            def _(g):
                                c_ve.wait(g)
                                c_fd.inc(g.dma_start(out=foldw[0:1, 0:4, :], in_=foldw[0:4, 4, :]), 16)
                            @blk.vector
                            def _(v):
                                c_fd.wait(v)
                                c_ve.inc(v.tensor_tensor(foldw[0:1, 0:2, :], foldw[0:1, 0:2, :],
                                                         foldw[0:1, 2:4, :], ALU.max))
                                c_ve.inc(v.tensor_tensor(rw1[:], foldw[0:1, 0, :],
                                                         foldw[0:1, 1, :], ALU.max))

                        def emit_minmax_fold():
                            @blk.vector
                            def _(v):
                                c_ve.inc(v.tensor_scalar_mul(gmin[:], gmin[:], -1.0))
                                c_ve.inc(v.tensor_max(gmax[:], gmax[:], gmin[:]))
                            emit_fold()

                        def _do_quarter(qt):
                            tq = slice(qt * 512, (qt + 1) * 512)
                            for ch in range(8):
                                cx = c_xs0 if ch % 2 == 0 else c_xs1

                                @blk.sync
                                def _(s, ch=ch, cx=cx, tq=tq, qt=qt):
                                    if qt == 0 and ch == 0:
                                        s.wait_ge(c_st.sem, st_in)
                                    c_mm.wait(s)
                                    c_ve.wait(s)
                                    cx.inc(s.dma_start(out=xst[:, ch % 2, :], in_=xv2[:, ch, tq]), 16)

                                @blk.vector
                                def _(v, ch=ch, cx=cx):
                                    cx.wait(v)
                                    if ch >= 1:
                                        c_mm.wait(v)
                                    c_ve.inc(v.tensor_mul(xsqb[:], xst[:, ch % 2, :], xst[:, ch % 2, :]))

                                @blk.tensor
                                def _(t, ch=ch):
                                    c_ve.wait(t)
                                    t.matmul(pe[0:1, 0, :], ones_col_s[:], xst[:, ch % 2, :],
                                             start=(ch == 0), stop=(ch == 7))
                                    c_mm.inc(t.matmul(pe[0:1, 1, :], ones_col_s[:], xsqb[:],
                                                      start=(ch == 0), stop=(ch == 7)))

                            @blk.vector
                            def _(v):
                                c_mm.wait(v)
                                c_ve.inc(v.tensor_scalar_mul(rw1[:], pe[0:1, 0, :], 1.0 / D))
                                c_ve.inc(v.tensor_mul(rw2[:], rw1[:], rw1[:]))
                                c_ve.inc(v.scalar_tensor_tensor(rw2[:], pe[0:1, 1, :], 1.0 / D,
                                                                rw2[:], ALU.mult, ALU.subtract))
                                c_ve.inc(v.tensor_scalar_add(rw2[:], rw2[:], LN_EPS))

                            @blk.scalar
                            def _(sc):
                                c_ve.wait(sc)
                                c_se.inc(sc.activation(rw2[:], rw2[:], AF.Sqrt))

                            @blk.vector
                            def _(v):
                                c_se.wait(v)
                                v.wait_ge(c_ve.sem, c_ve.n)
                                c_ve.inc(v.reciprocal(rw2[:], rw2[:]))
                                c_ve.inc(v.tensor_mul(rw3[:], rw1[:], rw2[:]))
                                c_ve.inc(v.tensor_scalar_mul(rw3[:], rw3[:], -1.0))

                            @blk.tensor
                            def _(t):
                                c_ve.wait(t)
                                t.matmul(pe[:, 2, :], ones_row_s[:], rw2[:], start=True, stop=True)
                                c_mm.inc(t.matmul(pe[:, 3, :], ones_row_s[:], rw3[:], start=True, stop=True))

                            for ch in range(8):
                                cx = c_xs0 if ch % 2 == 0 else c_xs1

                                @blk.sync
                                def _(s, ch=ch, cx=cx, tq=tq):
                                    c_ve.wait(s)
                                    cx.inc(s.dma_start(out=xst[:, ch % 2, :], in_=xv2[:, ch, tq]), 16)

                                @blk.vector
                                def _(v, ch=ch, cx=cx):
                                    cx.wait(v)
                                    if ch == 0:
                                        c_mm.wait(v)
                                    c_ve.inc(v.tensor_mul(xst[:, ch % 2, :], xst[:, ch % 2, :], pe[:, 2, :]))
                                    c_ve.inc(v.tensor_add(xln[:, ch, :], xst[:, ch % 2, :], pe[:, 3, :]))
                                    if ch == 0:
                                        c_ve.inc(v.tensor_copy(gmax[:], xln[:, ch, :]))
                                        c_ve.inc(v.tensor_copy(gmin[:], xln[:, ch, :]))
                                    else:
                                        c_ve.inc(v.tensor_max(gmax[:], gmax[:], xln[:, ch, :]))
                                        c_ve.inc(v.tensor_tensor(gmin[:], gmin[:], xln[:, ch, :], ALU.min))
                            st["ln_free"] = c_ve.n

                            emit_minmax_fold()

                            @blk.vector
                            def _(v):
                                c_ve.inc(v.tensor_scalar_max(rw1[:], rw1[:], 1e-5))
                                v.wait_ge(c_ve.sem, c_ve.n)
                                c_ve.inc(v.reciprocal(rw2[:], rw1[:]))
                                v.wait_ge(c_ve.sem, c_ve.n)
                                c_ve.inc(v.tensor_scalar_mul(rw2[:], rw2[:], 127.0))
                                c_ve.inc(v.tensor_scalar_mul(rw1[:], rw1[:], 1.0 / 127.0))

                            @blk.tensor
                            def _(t):
                                c_ve.wait(t)
                                t.matmul(pe[:, 2, :], ones_row_s[:], rw2[:], start=True, stop=True)
                                c_mm.inc(t.matmul(pe[:, 3, :], ones_row_s[:], rw1[:], start=True, stop=True))

                            @blk.vector
                            def _(v):
                                c_mm.wait(v)
                                c_ve.inc(v.tensor_copy(bct[:, 0, :], pe[:, 2, :]))
                                c_ve.inc(v.tensor_copy(bct[:, 1, :], pe[:, 3, :]))
                                for ch in range(8):
                                    c_ve.inc(v.tensor_mul(tmpa[:], xln[:, ch, :], bct[:, 0, :]))
                                    c_ve.inc(v.tensor_scalar(tmpa[:], tmpa[:], MAGIC, -MAGIC, ALU.add, ALU.add))
                                    c_ve.inc(v.tensor_copy(xqf[:, ch, :], tmpa[:]))
                            xq_ready = c_ve.n
                            if DBG and tag == "_f0" and qt == 0:
                                @blk.sync
                                def _(s):
                                    s.wait_ge(c_ve.sem, xq_ready)
                                    c_st.inc(s.dma_start(out=dbg_sw[:, 0:3], in_=swcol[:]), 16)
                                    c_st.inc(s.dma_start(out=dbg_sw[:, 3:6], in_=swinv[:]), 16)
                                    c_st.inc(s.dma_start(out=dbg_wqg, in_=wqg[:]), 16)
                                    c_st.inc(s.dma_start(out=dbg_xq, in_=xqf[:]), 16)
                                    c_st.inc(s.dma_start(out=dbg_rows[:, 0:1, :], in_=rw1[:]), 16)
                                    c_st.inc(s.dma_start(out=dbg_rows[:, 1:2, :], in_=rw2[:]), 16)

                            pe_mm = {}
                            pe_free = {}
                            for m in range(32):
                                cwu = c_wu0 if m % 2 == 0 else c_wu1
                                gb = 4 + 2 * (m % 2)
                                ub = gb + 1

                                @blk.sync
                                def _(s, m=m, cwu=cwu):
                                    if m < 2:
                                        if st["gu_mm_prev"] is None:
                                            s.wait_ge(c_cc.sem, cc_w)
                                            c_mm.wait(s)
                                        else:
                                            s.wait_ge(c_mm.sem, st["gu_mm_prev"][m])
                                    else:
                                        s.wait_ge(c_mm.sem, pe_mm[m - 2])
                                    cwu.inc(s.dma_start(out=wut[:, m % 2, :, :], in_=wqu_view[m]), 16)

                                if DBG and tag == "_f0" and qt == 0 and m == 5:
                                    @blk.sync
                                    def _(s, cwu=cwu):
                                        cwu.wait(s)
                                        c_st.inc(s.dma_start(out=dbg_wut, in_=wut[:, 1, :, :]), 16)

                                @blk.tensor
                                def _(t, m=m, cwu=cwu, xq_ready=xq_ready, gb=gb, ub=ub, qt=qt):
                                    t.wait_ge(c_ve.sem, xq_ready)
                                    cwu.wait(t)
                                    if m == 0 and qt == 0:
                                        t.wait_ge(c_gld.sem, gld_w)
                                    if m >= 2:
                                        t.wait_ge(c_ve.sem, pe_free[m - 2])
                                    for dc in range(8):
                                        t.matmul(pe[:, gb, :], wqg[:, dc, m * 128:(m + 1) * 128],
                                                 xqf[:, dc, :], start=(dc == 0), stop=(dc == 7))
                                    for dc in range(8):
                                        last = t.matmul(pe[:, ub, :], wut[:, m % 2, dc, :],
                                                        xqf[:, dc, :], start=(dc == 0), stop=(dc == 7))
                                    c_mm.inc(last)
                                pe_mm[m] = c_mm.n

                                @blk.vector
                                def _(v, m=m, gb=gb):
                                    c_mm.wait(v)
                                    c_ve.inc(v.tensor_mul(tmpa[:], pe[:, gb, :], bct[:, 1, :]))

                                @blk.scalar
                                def _(sc, m=m):
                                    c_ve.wait(sc)
                                    c_se.inc(sc.activation(tmpb[:], tmpa[:], AF.Silu,
                                                           scale=swinv[:, 0:1]))

                                @blk.vector
                                def _(v, m=m, ub=ub):
                                    c_se.wait(v)
                                    c_ve.inc(v.tensor_mul(tmpa[:], pe[:, ub, :], bct[:, 1, :]))
                                    c_ve.inc(v.scalar_tensor_tensor(tmpa[:], tmpa[:], swinv[:, 1:2],
                                                                    tmpb[:], ALU.mult, ALU.mult))
                                    c_ve.inc(v.tensor_copy(guf[:, m, :], tmpa[:]))
                                    if m == 0:
                                        c_ve.inc(v.tensor_copy(gmax[:], tmpa[:]))
                                        c_ve.inc(v.tensor_copy(gmin[:], tmpa[:]))
                                    else:
                                        c_ve.inc(v.tensor_max(gmax[:], gmax[:], tmpa[:]))
                                        c_ve.inc(v.tensor_tensor(gmin[:], gmin[:], tmpa[:], ALU.min))
                                pe_free[m] = c_ve.n
                            st["gu_mm_prev"] = (pe_mm[30], pe_mm[31])

                            emit_minmax_fold()

                            @blk.vector
                            def _(v):
                                c_ve.inc(v.tensor_scalar_max(rw1[:], rw1[:], 1e-5))
                                v.wait_ge(c_ve.sem, c_ve.n)
                                c_ve.inc(v.reciprocal(rw2[:], rw1[:]))
                                v.wait_ge(c_ve.sem, c_ve.n)
                                c_ve.inc(v.tensor_scalar_mul(rw2[:], rw2[:], 127.0))
                                c_ve.inc(v.tensor_scalar_mul(rw1[:], rw1[:], 1.0 / 127.0))

                            @blk.tensor
                            def _(t):
                                c_ve.wait(t)
                                t.matmul(pe[:, 2, :], ones_row_s[:], rw2[:], start=True, stop=True)
                                c_mm.inc(t.matmul(pe[:, 3, :], ones_row_s[:], rw1[:], start=True, stop=True))

                            @blk.vector
                            def _(v):
                                c_mm.wait(v)
                                c_ve.inc(v.tensor_copy(bct[:, 2, :], pe[:, 2, :]))
                                c_ve.inc(v.tensor_copy(bct[:, 3, :], pe[:, 3, :]))
                                for m in range(32):
                                    c_ve.inc(v.tensor_mul(tmpa[:], guf[:, m, :], bct[:, 2, :]))
                                    c_ve.inc(v.tensor_scalar(tmpa[:], tmpa[:], MAGIC, -MAGIC, ALU.add, ALU.add))
                                    c_ve.inc(v.tensor_copy(guf[:, m, :], tmpa[:]))
                            gu_ready = c_ve.n
                            if DBG and tag == "_f0" and qt == 0:
                                @blk.sync
                                def _(s):
                                    s.wait_ge(c_ve.sem, gu_ready)
                                    c_st.inc(s.dma_start(out=dbg_gu, in_=guf[:]), 16)

                            dn_mm = {}
                            for ic in range(32):
                                cwd = c_wd0 if ic % 2 == 0 else c_wd1

                                @blk.sync
                                def _(s, ic=ic, cwd=cwd):
                                    if ic < 2:
                                        if st["dn_prev"] is None:
                                            s.wait_ge(c_cc.sem, cc_w)
                                            c_mm.wait(s)
                                        else:
                                            s.wait_ge(c_mm.sem, st["dn_prev"][30 + ic])
                                    else:
                                        s.wait_ge(c_mm.sem, dn_mm[ic - 2])
                                    cwd.inc(s.dma_start(
                                        out=wdt[:, ic % 2, :],
                                        in_=wqd_view[:, ic, :]), 16)

                                @blk.tensor
                                def _(t, ic=ic, cwd=cwd, gu_ready=gu_ready):
                                    t.wait_ge(c_ve.sem, gu_ready)
                                    cwd.wait(t)
                                    for dm in range(8):
                                        last = t.matmul(pe[:, dm, :], wdt[:, ic % 2, dm * 128:(dm + 1) * 128],
                                                        guf[:, ic, :], start=(ic == 0), stop=(ic == 31))
                                    c_mm.inc(last)
                                dn_mm[ic] = c_mm.n
                            st["dn_prev"] = dn_mm

                            for dm in range(8):
                                @blk.sync
                                def _(s, dm=dm, tq=tq):
                                    c_ve.wait(s)
                                    c_x1r.inc(s.dma_start(out=x1tl[:], in_=xv2[:, dm, tq]), 16)

                                @blk.scalar
                                def _(sc, dm=dm):
                                    c_mm.wait(sc)
                                    if dm > 0:
                                        c_ve.wait(sc)
                                    c_se.inc(sc.activation(tmpb[:], pe[:, dm, :], AF.Identity,
                                                           scale=swinv[:, 2:3]))

                                @blk.vector
                                def _(v, dm=dm):
                                    c_se.wait(v)
                                    c_x1r.wait(v)
                                    c_ve.inc(v.tensor_mul(tmpb[:], tmpb[:], bct[:, 3, :]))
                                    c_ve.inc(v.tensor_add(x2tl[:], tmpb[:], x1tl[:]))

                                @blk.sync
                                def _(s, dm=dm, tq=tq):
                                    c_ve.wait(s)
                                    c_st.inc(s.dma_start(out=ov2[:, dm, tq], in_=x2tl[:]), 16)

                        for qt in range(4):
                            _do_quarter(qt)

            emit_ffn(s_x1, s_x2, wsl_in[(0, "wg_sl")], wsl_in[(0, "wu_sl")],
                     wsl_in[(0, "wd_sl")], Qd[0], "_f0")


            phf = ExitStack()
            with phf:
                def sbf(name, shape, dt):
                    return phf.enter_context(nc.sbuf_tensor(name, shape, dt))

                hstat = sbf("hstat", [128, 2, 8, 128], BF16)
                qk4 = sbf("qk4", [128, 4, 3088], F32)
                loT = sbf("loT", [128, 8, TOK2], BF16)
                wqvT = sbf("wqvT", [128, 8, 3088], BF16)
                woT_s = sbf("woT_s", [128, 8, D], BF16)
                iden_s = sbf("iden_s", [128, 128], F32)
                sca = sbf("sca", [128, 1024], F32)
                scb = sbf("scb", [128, 1024], F32)
                u_s = sbf("u_s", [128, 4, 1024], F32)
                o_s = sbf("o_s", [128, 4, 1024], F32)
                gseg = sbf("gseg", [128, 16, 20], F32)
                xs2 = sbf("xs2", [128, 2, 512], F32)
                xq2 = sbf("xq2", [128, 512], F32)
                rf1 = sbf("rf1", [1, 512], F32)
                rf2 = sbf("rf2", [1, 512], F32)
                lngb_s = sbf("lngb_s", [128, 8, 2], F32)
                htile = sbf("htile", [128, 512], F32)
                ytile = sbf("ytile", [128, 512], F32)

                @blk.gpsimd
                def _(g):
                    c_st.wait(g)
                    c_gld.inc(g.dma_start(out=wqvT[:], in_=wqkvbT_d.rearrange("(c p) m -> p c m", p=128)), 16)
                    c_gld.inc(g.dma_start(out=woT_s[:], in_=woT_d.rearrange("(c p) m -> p c m", p=128)), 16)
                    c_gld.inc(g.dma_start(out=iden_s[:], in_=c_iden), 16)

                @blk.sync
                def _(s):
                    c_st.wait(s)
                    c_ld.inc(s.dma_start(out=lngb_s[:], in_=lngb_d), 16)

                @blk.vector
                def _(v):
                    c_gld.wait(v)
                    c_ve.inc(v.tensor_scalar_mul(woT_s[:], woT_s[:], float(HS ** -0.5)))

                with nc.psum_tensor("pf", [128, 8, 512], F32) as pf:
                    xv3 = s_x2.rearrange("(c p) t -> p c t", p=128)
                    hv = s_h.rearrange("(c p) t -> p c t", p=128)
                    st_f1 = c_st.n
                    for qt in range(4):
                        tq = slice(qt * 512, (qt + 1) * 512)
                        for ch in range(8):
                            cx = c_xs0 if ch % 2 == 0 else c_xs1

                            @blk.sync
                            def _(s, ch=ch, cx=cx, tq=tq, qt=qt):
                                if qt == 0 and ch == 0:
                                    s.wait_ge(c_st.sem, st_f1)
                                c_mm.wait(s)
                                c_ve.wait(s)
                                cx.inc(s.dma_start(out=xs2[:, ch % 2, :], in_=xv3[:, ch, tq]), 16)

                            @blk.vector
                            def _(v, ch=ch, cx=cx):
                                cx.wait(v)
                                if ch >= 1:
                                    c_mm.wait(v)
                                c_ve.inc(v.tensor_mul(xq2[:], xs2[:, ch % 2, :], xs2[:, ch % 2, :]))

                            @blk.tensor
                            def _(t, ch=ch):
                                c_ve.wait(t)
                                t.matmul(pf[0:1, 0, :], ones_col_s[:], xs2[:, ch % 2, :],
                                         start=(ch == 0), stop=(ch == 7))
                                c_mm.inc(t.matmul(pf[0:1, 1, :], ones_col_s[:], xq2[:],
                                                  start=(ch == 0), stop=(ch == 7)))

                        @blk.vector
                        def _(v):
                            c_mm.wait(v)
                            c_ve.inc(v.tensor_scalar_mul(rf1[:], pf[0:1, 0, :], 1.0 / D))
                            c_ve.inc(v.tensor_mul(rf2[:], rf1[:], rf1[:]))
                            c_ve.inc(v.scalar_tensor_tensor(rf2[:], pf[0:1, 1, :], 1.0 / D,
                                                            rf2[:], ALU.mult, ALU.subtract))
                            c_ve.inc(v.tensor_scalar_add(rf2[:], rf2[:], LN_EPS))

                        @blk.scalar
                        def _(sc):
                            c_ve.wait(sc)
                            c_se.inc(sc.activation(rf2[:], rf2[:], AF.Sqrt))

                        @blk.vector
                        def _(v):
                            c_se.wait(v)
                            v.wait_ge(c_ve.sem, c_ve.n)
                            c_ve.inc(v.reciprocal(rf2[:], rf2[:]))
                            c_ve.inc(v.tensor_mul(rf1[:], rf1[:], rf2[:]))
                            c_ve.inc(v.tensor_scalar_mul(rf1[:], rf1[:], -1.0))

                        @blk.tensor
                        def _(t):
                            c_ve.wait(t)
                            t.matmul(pf[:, 2, :], ones_row_s[:], rf2[:], start=True, stop=True)
                            c_mm.inc(t.matmul(pf[:, 3, :], ones_row_s[:], rf1[:], start=True, stop=True))

                        for ch in range(8):
                            cx = c_xs0 if ch % 2 == 0 else c_xs1

                            @blk.sync
                            def _(s, ch=ch, cx=cx, tq=tq):
                                c_ve.wait(s)
                                cx.inc(s.dma_start(out=xs2[:, ch % 2, :], in_=xv3[:, ch, tq]), 16)

                            @blk.vector
                            def _(v, ch=ch, cx=cx, qt=qt, tq=tq):
                                cx.wait(v)
                                if ch == 0:
                                    c_mm.wait(v)
                                c_ve.inc(v.tensor_mul(xs2[:, ch % 2, :], xs2[:, ch % 2, :], pf[:, 2, :]))
                                c_ve.inc(v.tensor_add(htile[:], xs2[:, ch % 2, :], pf[:, 3, :]))

                            @blk.sync
                            def _(s, ch=ch, tq=tq):
                                c_ve.wait(s)
                                c_st.inc(s.dma_start(out=hv[:, ch, tq], in_=htile[:]), 16)

                with (
                    nc.psum_tensor("pq", [128, 7, 512], F32) as pq,
                    nc.psum_tensor("pt2", [128, 512], F32) as pt2,
                ):
                    hv4 = s_h.rearrange("(c p) t -> p c t", p=128)
                    for sub in range(4):
                        for b in range(B):
                            tt = b * 4 + sub
                            chs = c_hs0 if tt % 2 == 0 else c_hs1

                            @blk.gpsimd
                            def _(g, tt=tt, chs=chs):
                                c_mm.wait(g)
                                if tt < 2:
                                    c_st.wait(g)
                                chs.inc(g.dma_start(out=hstat[:, tt % 2, :, :],
                                                    in_=hv4[:, :, tt * 128:(tt + 1) * 128]), 16)

                            @blk.tensor
                            def _(t, tt=tt, b=b, sub=sub, chs=chs):
                                if sub == 0 and b == 0:
                                    c_gld.wait(t)
                                    c_ve.wait(t)
                                chs.wait(t)
                                c_ve.wait(t)
                                for nch in range(7):
                                    nsl = slice(nch * 512, min((nch + 1) * 512, 3088))
                                    w = nsl.stop - nsl.start
                                    for dch in range(8):
                                        last = t.matmul(pq[:, nch, 0:w], hstat[:, tt % 2, dch, :],
                                                        wqvT[:, dch, nsl], start=(dch == 0), stop=(dch == 7))
                                c_mm.inc(last)

                            @blk.vector
                            def _(v, b=b):
                                c_mm.wait(v)
                                c_ve.inc(v.tensor_copy(qk4[:, b, 0:3072],
                                                       pq[:, 0:6, :].rearrange("p a t -> p (a t)")))
                                c_ve.inc(v.tensor_copy(qk4[:, b, 3072:3088], pq[:, 6, 0:16]))

                        QS, KS, VS, BS = 0, 1024, 2048, 3072
                        seg = lambda b, base: qk4[:, b, base:base + 1024]
                        seg3 = lambda b, base: qk4[:, b, base:base + 1024].rearrange("p (h d) -> p h d", d=64)
                        bseg = lambda b: qk4[:, b, BS:BS + 16]

                        @blk.scalar
                        def _(sc):
                            c_ve.wait(sc)
                            for b in range(B):
                                c_se.inc(sc.activation(bseg(b), bseg(b), AF.Sigmoid))

                        for b in range(B):
                            for base in (QS, KS):
                                @blk.vector
                                def _(v, b=b, base=base):
                                    c_se.wait(v)
                                    c_ve.inc(v.tensor_scalar_min(sca[:], seg(b, base), 0.0))

                                @blk.scalar
                                def _(sc):
                                    c_ve.wait(sc)
                                    c_se.inc(sc.activation(sca[:], sca[:], AF.Exp))

                                @blk.vector
                                def _(v, b=b, base=base):
                                    c_se.wait(v)
                                    c_ve.inc(v.tensor_scalar_max(scb[:], seg(b, base), 0.0))
                                    c_ve.inc(v.tensor_add(sca[:], sca[:], scb[:]))
                                    c_ve.inc(v.reduce_sum(gseg[:, :, 0:1].rearrange("p h o -> p (h o)"),
                                                          sca[:].rearrange("p (h d) -> p h d", d=64), axis=AX.X))
                                    v.wait_ge(c_ve.sem, c_ve.n)
                                    c_ve.inc(v.reciprocal(gseg[:, :, 0], gseg[:, :, 0]))
                                    v.wait_ge(c_ve.sem, c_ve.n)
                                    c_ve.inc(v.tensor_tensor(
                                        seg(b, base).rearrange("p (h d) -> p h d", d=64),
                                        sca[:].rearrange("p (h d) -> p h d", d=64),
                                        gseg[:, :, 0:1].broadcast_to([128, 16, 64]), ALU.mult))

                        @blk.vector
                        def _(v):
                            gi = 0
                            for i in range(B):
                                for j in range(i + 1, B):
                                    c_ve.inc(v.tensor_mul(sca[:], seg(i, KS), seg(j, KS)))
                                    c_ve.inc(v.reduce_sum(gseg[:, :, 1 + gi:2 + gi].rearrange("p h o -> p (h o)"),
                                                          sca[:].rearrange("p (h d) -> p h d", d=64), axis=AX.X))
                                    gi += 1
                            ai = 0
                            for ss in range(B):
                                for j in range(ss + 1):
                                    c_ve.inc(v.tensor_mul(sca[:], seg(j, KS), seg(ss, QS)))
                                    c_ve.inc(v.reduce_sum(gseg[:, :, 8 + ai:9 + ai].rearrange("p h o -> p (h o)"),
                                                          sca[:].rearrange("p (h d) -> p h d", d=64), axis=AX.X))
                                    ai += 1
                        @blk.vector
                        def _(v):
                            bc = lambda col: gseg[:, :, col:col + 1].broadcast_to([128, 16, 64])
                            G = {}
                            gi = 1
                            for i in range(B):
                                for j in range(i + 1, B):
                                    G[(i, j)] = gi
                                    gi += 1
                            AQ = {}
                            ai = 8
                            for ss in range(B):
                                for j in range(ss + 1):
                                    AQ[(ss, j)] = ai
                                    ai += 1
                            bcb = lambda b: qk4[:, b, BS:BS + 16].rearrange("p (h o) -> p h o", o=1).broadcast_to([128, 16, 64])
                            u3 = lambda i: u_s[:, i, :].rearrange("p (h d) -> p h d", d=64)
                            o3 = lambda i: o_s[:, i, :].rearrange("p (h d) -> p h d", d=64)
                            v3 = lambda b: seg3(b, VS)
                            s3a = sca[:].rearrange("p (h d) -> p h d", d=64)
                            v.wait_ge(c_ve.sem, c_ve.n)
                            c_ve.inc(v.tensor_tensor(u3(0), v3(0), bcb(0), ALU.mult))
                            for ss in range(1, B):
                                c_ve.inc(v.tensor_tensor(s3a, u3(0), bc(G[(0, ss)]), ALU.mult))
                                for i in range(1, ss):
                                    c_ve.inc(v.tensor_tensor(scb[:].rearrange("p (h d) -> p h d", d=64),
                                                             u3(i), bc(G[(i, ss)]), ALU.mult))
                                    c_ve.inc(v.tensor_add(sca[:], sca[:], scb[:]))
                                c_ve.inc(v.tensor_sub(sca[:], seg(ss, VS), sca[:]))
                                c_ve.inc(v.tensor_tensor(u3(ss), s3a, bcb(ss), ALU.mult))
                            for ss in range(B):
                                c_ve.inc(v.tensor_tensor(o3(ss), u3(0), bc(AQ[(ss, 0)]), ALU.mult))
                                for j in range(1, ss + 1):
                                    c_ve.inc(v.tensor_tensor(scb[:].rearrange("p (h d) -> p h d", d=64),
                                                             u3(j), bc(AQ[(ss, j)]), ALU.mult))
                                    c_ve.inc(v.tensor_add(o_s[:, ss, :], o_s[:, ss, :], scb[:]))

                        for b in range(B):
                            for hp2 in range(8):
                                @blk.tensor
                                def _(t, b=b, hp2=hp2, sub=sub):
                                    c_ve.wait(t)
                                    if hp2 == 0 and b == 0:
                                        pass
                                    c_se.wait(t)
                                    c_mm.inc(t.transpose(pt2[:, 0:128], o_s[:, b, hp2 * 128:(hp2 + 1) * 128],
                                                         iden_s[:]))

                                @blk.vector
                                def _(v, b=b, hp2=hp2, sub=sub):
                                    c_mm.wait(v)
                                    tok = slice(b * 512 + sub * 128, b * 512 + sub * 128 + 128)
                                    c_ve.inc(v.tensor_copy(loT[:, hp2, tok], pt2[:, 0:128]))

                with nc.psum_tensor("pw2", [128, 4, 512], F32) as pw2:
                    yv = s_y.rearrange("(c p) t -> p c t", p=128)
                    st_f3 = c_st.n
                    for m in range(8):
                        @blk.tensor
                        def _(t, m=m):
                            c_ve.wait(t)
                            for tt2 in range(4):
                                for ec in range(8):
                                    last = t.matmul(pw2[:, tt2, :], woT_s[:, ec, m * 128:(m + 1) * 128],
                                                    loT[:, ec, tt2 * 512:(tt2 + 1) * 512],
                                                    start=(ec == 0), stop=(ec == 7))
                                c_mm.inc(last)

                        for tt2 in range(4):
                            @blk.sync
                            def _(s, m=m, tt2=tt2):
                                if m == 0 and tt2 == 0:
                                    s.wait_ge(c_st.sem, st_f3)
                                c_ve.wait(s)
                                c_x1r.inc(s.dma_start(out=htile[:], in_=hv[:, m, tt2 * 512:(tt2 + 1) * 512]), 16)

                            @blk.vector
                            def _(v, m=m, tt2=tt2):
                                c_x1r.wait(v)
                                c_mm.wait(v)
                                c_ve.inc(v.tensor_add(ytile[:], pw2[:, tt2, :], htile[:]))

                            @blk.sync
                            def _(s, m=m, tt2=tt2):
                                c_ve.wait(s)
                                c_st.inc(s.dma_start(out=yv[:, m, tt2 * 512:(tt2 + 1) * 512], in_=ytile[:]), 16)

                with nc.psum_tensor("pf2", [128, 4, 512], F32) as pf2:
                    yv = s_y.rearrange("(c p) t -> p c t", p=128)
                    xv3 = s_x2.rearrange("(c p) t -> p c t", p=128)
                    ov3 = s_x3.rearrange("(c p) t -> p c t", p=128)
                    for qt in range(4):
                        tq = slice(qt * 512, (qt + 1) * 512)
                        for ch in range(8):
                            cx = c_xs0 if ch % 2 == 0 else c_xs1

                            @blk.sync
                            def _(s, ch=ch, cx=cx, tq=tq):
                                c_mm.wait(s)
                                c_ve.wait(s)
                                c_st.wait(s)
                                cx.inc(s.dma_start(out=xs2[:, ch % 2, :], in_=yv[:, ch, tq]), 16)

                            @blk.vector
                            def _(v, ch=ch, cx=cx):
                                cx.wait(v)
                                if ch >= 1:
                                    c_mm.wait(v)
                                c_ve.inc(v.tensor_mul(xq2[:], xs2[:, ch % 2, :], xs2[:, ch % 2, :]))

                            @blk.tensor
                            def _(t, ch=ch):
                                c_ve.wait(t)
                                t.matmul(pf2[0:1, 0, :], ones_col_s[:], xs2[:, ch % 2, :],
                                         start=(ch == 0), stop=(ch == 7))
                                c_mm.inc(t.matmul(pf2[0:1, 1, :], ones_col_s[:], xq2[:],
                                                  start=(ch == 0), stop=(ch == 7)))

                        @blk.vector
                        def _(v):
                            c_mm.wait(v)
                            c_ve.inc(v.tensor_scalar_mul(rf1[:], pf2[0:1, 0, :], 1.0 / D))
                            c_ve.inc(v.tensor_mul(rf2[:], rf1[:], rf1[:]))
                            c_ve.inc(v.scalar_tensor_tensor(rf2[:], pf2[0:1, 1, :], 1.0 / D,
                                                            rf2[:], ALU.mult, ALU.subtract))
                            c_ve.inc(v.tensor_scalar_add(rf2[:], rf2[:], LN_EPS))

                        @blk.scalar
                        def _(sc):
                            c_ve.wait(sc)
                            c_se.inc(sc.activation(rf2[:], rf2[:], AF.Sqrt))

                        @blk.vector
                        def _(v):
                            c_se.wait(v)
                            v.wait_ge(c_ve.sem, c_ve.n)
                            c_ve.inc(v.reciprocal(rf2[:], rf2[:]))
                            c_ve.inc(v.tensor_mul(rf1[:], rf1[:], rf2[:]))
                            c_ve.inc(v.tensor_scalar_mul(rf1[:], rf1[:], -1.0))

                        @blk.tensor
                        def _(t):
                            c_ve.wait(t)
                            t.matmul(pf2[:, 2, :], ones_row_s[:], rf2[:], start=True, stop=True)
                            c_mm.inc(t.matmul(pf2[:, 3, :], ones_row_s[:], rf1[:], start=True, stop=True))

                        for ch in range(8):
                            cx = c_xs0 if ch % 2 == 0 else c_xs1

                            @blk.sync
                            def _(s, ch=ch, cx=cx, tq=tq):
                                c_ve.wait(s)
                                cx.inc(s.dma_start(out=xs2[:, ch % 2, :], in_=yv[:, ch, tq]), 16)
                                c_x1r.inc(s.dma_start(out=htile[:], in_=xv3[:, ch, tq]), 16)

                            @blk.vector
                            def _(v, ch=ch, cx=cx):
                                cx.wait(v)
                                c_x1r.wait(v)
                                if ch == 0:
                                    c_mm.wait(v)
                                c_ve.inc(v.tensor_mul(xs2[:, ch % 2, :], xs2[:, ch % 2, :], pf2[:, 2, :]))
                                c_ve.inc(v.tensor_add(ytile[:], xs2[:, ch % 2, :], pf2[:, 3, :]))
                                c_ve.inc(v.tensor_scalar_add(htile[:], htile[:], lngb_s[:, ch, 1:2]))
                                c_ve.inc(v.scalar_tensor_tensor(ytile[:], ytile[:], lngb_s[:, ch, 0:1],
                                                                htile[:], ALU.mult, ALU.add))

                            @blk.sync
                            def _(s, ch=ch, tq=tq):
                                c_ve.wait(s)
                                c_st.inc(s.dma_start(out=ov3[:, ch, tq], in_=ytile[:]), 16)

            emit_ffn(s_x3, out_fin, wsl_in[(1, "wg_sl")], wsl_in[(1, "wu_sl")],
                     wsl_in[(1, "wd_sl")], Qd[1], "_f1")

            @blk.sync
            def _(s):
                s.wait_ge(c_st.sem, c_st.n)

    return nc


_NC_CACHE = {}


def _get_nc():
    if "nc" not in _NC_CACHE:
        _NC_CACHE["nc"] = build_nc()
    return _NC_CACHE["nc"]


def kernel(**inputs):
    maps = _stage_inputs(inputs)
    nc = _get_nc()
    res = run_bass_kernel_spmd(nc, maps, core_ids=list(range(8)))
    out = np.zeros((B, S, D), np.float32)
    for c in range(8):
        u = c
        y = res.results[c]["out_fin"]
        out[:, 512 * u:512 * (u + 1), :] = y.reshape(D, B, 512).transpose(1, 2, 0)
    return out
